# version 07
# baseline (speedup 1.0000x reference)
import numpy as np
from contextlib import ExitStack, contextmanager

import concourse.bass as bass
import concourse.mybir as mybir
import bass_rust
from concourse.bass_utils import run_bass_kernel_spmd

F32 = mybir.dt.float32
BF16 = mybir.dt.bfloat16
F8E4 = mybir.dt.float8e4
AF = mybir.ActivationFunctionType
ALU = mybir.AluOpType
AX = mybir.AxisListType
RO = bass_rust.ReduceOp

D = 1024
NCORES = 8

CONSTS = [1.0, 0.5, 1e-6, 1e-5, 2.0, -1.0, 0.0, 1.0 / D, 6.0, 448.0, 2.25, 4.5,
          127.0, -128.0, 32767.0, -32768.0, 65535.0, 1.0 / 65536.0, 2.0 ** -40]
CIDX = {v: i for i, v in enumerate(CONSTS)}


class Ctx:

    NDMASEM = 8

    def __init__(self, nc, block):
        self.nc = nc
        self.block = block
        self.stack = ExitStack()
        self.scopes = [self.stack]
        self.engs = {}
        self.regions = {}
        self.region_of = {}
        self._uid = 0
        for name, sec in (("T", block.tensor), ("V", block.vector), ("S", block.scalar),
                          ("P", block.gpsimd)):
            sem = self.stack.enter_context(nc.semaphore("sem_" + name))
            self.engs[name] = dict(sec=sec, sem=sem, ops=[], wm={}, pend=[])
        self.dmaq = {}
        for name, sec in (("qS", block.sync), ("qA", block.scalar), ("qP", block.gpsimd)):
            sems = [self.stack.enter_context(nc.semaphore(f"sem_{name}_{i}"))
                    for i in range(self.NDMASEM)]
            self.dmaq[name] = dict(sec=sec, sems=sems, tot=[0] * self.NDMASEM, n=0,
                                   pend=[], wm={})
        self.dmaq["qA"]["pend"] = self.engs["S"]["pend"]
        self.dmaq["qP"]["pend"] = self.engs["P"]["pend"]

    def sbuf(self, name, shape, dtype):
        self._uid += 1
        t = self.scopes[-1].enter_context(
            self.nc.sbuf_tensor(f"{name}_{self._uid}", shape, dtype))
        self.region_of[id(t)] = f"{name}_{self._uid}"
        return t

    def psum(self, name, shape, dtype):
        self._uid += 1
        t = self.scopes[-1].enter_context(
            self.nc.psum_tensor(f"{name}_{self._uid}", shape, dtype))
        self.region_of[id(t)] = f"{name}_{self._uid}"
        return t

    @contextmanager
    def scope(self):
        es = ExitStack()
        self.scopes.append(es)
        try:
            yield
        finally:
            self.barrier()
            self.scopes.pop()
            es.close()

    def _wait_value(self, pe, idx):
        return idx + 1

    def _flush(self):
        for name, e in list(self.engs.items()) + [("qS", self.dmaq["qS"])]:
            pend = e.get("pend")
            if not pend:
                continue
            batch = list(pend)
            pend.clear()

            def run(q, batch=batch):
                for item in batch:
                    item(q)
            e["sec"](run)

    def _collect_waits(self, ename, reads, writes):
        waits = {}
        for r in reads:
            reg = self.regions.get(r)
            if reg:
                for pe, idx in reg["w"].items():
                    if not (pe == ename == "T") and waits.get(pe, -1) < idx:
                        waits[pe] = idx
        for w in writes:
            reg = self.regions.get(w)
            if reg:
                for dd in (reg["w"], reg["r"]):
                    for pe, idx in dd.items():
                        if not (pe == ename == "T") and waits.get(pe, -1) < idx:
                            waits[pe] = idx
        return waits

    def _do_waits(self, ent, waits, self_name=None):
        wm = ent["wm"]
        pend = ent["pend"]
        self_idx = waits.pop(self_name, None) if self_name else None
        for pe, idx in sorted(waits.items()):
            if "#" in pe:
                qn, slot = pe.split("#")
                slot = int(slot)
                sem = self.dmaq[qn]["sems"][slot]
                tgt = idx
            else:
                sem = self.engs[pe]["sem"]
                tgt = self._wait_value(pe, idx)
            if wm.get(pe, -1) < tgt:
                pend.append(lambda q, s=sem, t=tgt: q.wait_ge(s, t))
                wm[pe] = tgt
        if self_idx is not None:
            tgt = self._wait_value(self_name, self_idx)
            e = self.engs[self_name]
            if wm.get(self_name, -1) < tgt:
                pend.append(lambda q, s=e["sem"], t=tgt: q.wait_ge(s, t))
                wm[self_name] = tgt

    def emit(self, ename, fn, reads=(), writes=()):
        if ename.startswith("q"):
            return self._dma(ename, fn, reads, writes)
        e = self.engs[ename]
        waits = self._collect_waits(ename, reads, writes)
        self._do_waits(e, waits, self_name=ename)
        opidx = len(e["ops"])
        e["ops"].append(None)

        def runner(q, fn=fn, e=e, opidx=opidx):
            inst = fn(q)
            inst.then_inc(e["sem"], 1)
            e["ops"][opidx] = inst
        e["pend"].append(runner)
        for w in writes:
            self.regions[w] = {"w": {ename: opidx}, "r": {}}
        for r in reads:
            reg = self.regions.setdefault(r, {"w": {}, "r": {}})
            if reg["r"].get(ename, -1) < opidx:
                reg["r"][ename] = opidx
        return None

    def _dma(self, qname, fn, reads=(), writes=()):
        q = self.dmaq[qname]
        slot = q["n"] % self.NDMASEM
        waits = self._collect_waits(qname, reads, writes)
        if q["tot"][slot] > 0:
            key = f"{qname}#{slot}"
            if waits.get(key, -1) < q["tot"][slot]:
                waits[key] = q["tot"][slot]
        self._do_waits(q, waits)

        def runner(qq, fn=fn, sem=q["sems"][slot]):
            fn(qq).then_inc(sem, 16)
        q["pend"].append(runner)
        q["tot"][slot] += 16
        q["n"] += 1
        token = f"{qname}#{slot}"
        tot = q["tot"][slot]
        for w in writes:
            self.regions[w] = {"w": {token: tot}, "r": {}}
        for r in reads:
            reg = self.regions.setdefault(r, {"w": {}, "r": {}})
            if reg["r"].get(token, -1) < tot:
                reg["r"][token] = tot
        return None

    def barrier(self):
        targets = {}
        for pe, e in self.engs.items():
            if e["ops"]:
                targets[pe] = len(e["ops"]) - 1
        for qn, q in self.dmaq.items():
            for slot in range(self.NDMASEM):
                if q["tot"][slot] > 0:
                    targets[f"{qn}#{slot}"] = q["tot"][slot]
        for ename, e in self.engs.items():
            self._do_waits(e, dict(targets), self_name=ename)
        for qn, q in self.dmaq.items():
            self._do_waits(q, dict(targets))
        self.regions = {}
        self._flush()


def build_nc(T=4096, layers=(0, 1, 2, 3), taps=(), n_cores=NCORES):
    import os
    nc = bass.Bass("TRN2", target_bir_lowering=False, debug=bool(os.environ.get("KDBG")))
    NT = T // 128
    taps = set(taps)

    def dram(name, shape, dtype=F32):
        kind = "ExternalOutput" if name in taps else "Internal"
        return nc.dram_tensor(name, shape, dtype, kind=kind)

    din = {}

    def inp(name, shape, dtype=F32):
        din[name] = nc.dram_tensor(name, shape, dtype, kind="ExternalInput")
        return din[name]

    x_in = inp("x", [T, D])
    inp("ident", [128, 128])
    inp("consts", [128, len(CONSTS)])
    inp("ones_row", [1, 128])
    if 0 in layers:
        inp("l0_WkT", [D, D]); inp("l0_WvT", [D, D]); inp("l0_WrT", [D, D]); inp("l0_WoT", [D, D])
        inp("l0_td", [128, 8]); inp("l0_tf", [128, 8])
        inp("l0_gwT", [D, 16])
        inp("l0_We1T", [16, D, 511]); inp("l0_We3T", [16, D, 511]); inp("l0_We2T", [16, 512, 1023])
        inp("l0_be1", [16, 511]); inp("l0_be3", [16, 511]); inp("l0_be2", [16, 1023])
        inp("l0_Ws1T", [D, 1023]); inp("l0_Ws3T", [D, 1023]); inp("l0_Ws2T", [1024, 1023])
        inp("l0_bs1", [1, 1023]); inp("l0_bs3", [1, 1023]); inp("l0_bs2", [1, 1023])

    if 1 in layers:
        inp("l1_WqT", [D, 2048]); inp("l1_WkvdT", [D, 64])
        inp("l1_WkuT", [64, 2048]); inp("l1_WvuT", [64, 2048]); inp("l1_WoT", [2048, D])
        inp("l1_gw", [4096, D]); inp("l1_uw", [4096, D]); inp("l1_dw", [D, 4096])
        inp("l1_s_in", [1, 1]); inp("l1_s_in_down", [1, 1])

    if 2 in layers:
        inp("l2_Wq", [D, 1024]); inp("l2_Wqr", [D, 512]); inp("l2_Wkr", [D, 32])
        inp("l2_Wkvd", [D, 256]); inp("l2_Wku", [256, 1024]); inp("l2_Wvu", [256, 1024])
        inp("l2_Wo", [D, 1024]); inp("l2_lobo", [1, 16])
        inp("ropeC", [T, 16]); inp("ropeS", [T, 16]); inp("causmaskB", [128, 128])
        inp("l2_gw", [3072, D]); inp("l2_uw", [3072, D]); inp("l2_dw", [D, 3072])
    if 3 in layers:
        inp("l3_WqT", [D, 64]); inp("l3_WkT", [D, 64]); inp("l3_WvT", [D, 64])
        inp("l3_bq", [64, 1]); inp("l3_bk", [64, 1]); inp("l3_bv", [1, 64])
        inp("l3_clip_x", [1, 1]); inp("l3_clip_down", [1, 1])
        inp("l3_gw", [2816, D]); inp("l3_uw", [2816, D]); inp("l3_dw", [D, 2816])

    out = nc.dram_tensor("out", [T, D], F32, kind="ExternalOutput")
    xres = dram("xres", [T, D])
    xn_f = dram("xn_f", [T, D])
    s_scr = dram("s_scr", [T, D])

    with nc.Block() as block:
        ctx = Ctx(nc, block)
        E = ctx.emit

        cc = ctx.sbuf("cc", [128, len(CONSTS)], F32)
        ident = ctx.sbuf("ident", [128, 128], BF16)
        onesb = ctx.sbuf("onesb", [1, 128], BF16)

        def C(v):
            return cc[:, CIDX[v]:CIDX[v] + 1]

        def R(t):
            return ctx.region_of[id(t)]

        E("qS", lambda q: q.dma_start(out=cc[:], in_=din["consts"].ap()), writes=[R(cc)])
        E("qP", lambda q: q.dma_start(out=ident[:], in_=din["ident"].ap()), writes=[R(ident)])
        E("qP", lambda q: q.dma_start(out=onesb[:], in_=din["ones_row"].ap()), writes=[R(onesb)])

        def mm_tile(ps, ps_nm, lhs, rhs, rdeps, first=True, last=True):
            n = len(lhs)
            for i, (la, ra) in enumerate(zip(lhs, rhs)):
                st, sp = first and i == 0, last and i == n - 1
                E("T", lambda q, la=la, ra=ra, st=st, sp=sp: q.matmul(ps, la, ra, start=st, stop=sp, skip_group_check=True),
                  reads=list(rdeps), writes=[ps_nm])

        def layernorm_phase(xnT, src_dram, src_nm, write_xn_f, absmax_col=None,
                            absmax_nm=None):
            with ctx.scope():
                xt = [ctx.sbuf("ln_xt", [128, D], F32) for _ in range(2)]
                xnb = [ctx.sbuf("ln_xnb", [128, D], BF16) for _ in range(2)]
                xnf = ctx.sbuf("ln_xnf", [128, D], F32)
                sumx = ctx.sbuf("ln_sumx", [128, 1], F32)
                sq = ctx.sbuf("ln_sq", [128, D], F32)
                ssq = ctx.sbuf("ln_ssq", [128, 1], F32)
                m = ctx.sbuf("ln_m", [128, 1], F32)
                var = ctx.sbuf("ln_var", [128, 1], F32)
                rstd = ctx.sbuf("ln_rstd", [128, 1], F32)
                nmr = ctx.sbuf("ln_nmr", [128, 1], F32)
                pt = [ctx.psum("ln_pt", [128, 1024], BF16) for _ in range(2)]
                for jt in range(NT):
                    b = jt % 2
                    E("qS", lambda q, jt=jt, b=b: q.dma_start(
                        out=xt[b][:], in_=src_dram.ap()[jt * 128:(jt + 1) * 128, :]),
                      reads=[src_nm], writes=[R(xt[b])])
                    E("V", lambda q, b=b: q.tensor_reduce(out=sumx[:], in_=xt[b][:],
                                                          axis=AX.X, op=ALU.add),
                      reads=[R(xt[b])], writes=[R(sumx)])
                    E("S", lambda q, b=b: q.activation(out=sq[:], in_=xt[b][:], func=AF.Square,
                                                       accum_out=ssq[:, 0:1]),
                      reads=[R(xt[b])], writes=[R(sq), R(ssq)])
                    E("V", lambda q: q.tensor_scalar(out=m[:], in0=sumx[:], scalar1=1.0 / D,
                                                     scalar2=None, op0=ALU.mult),
                      reads=[R(sumx)], writes=[R(m)])
                    E("V", lambda q: q.scalar_tensor_tensor(out=var[:], in0=m[:], scalar=-1.0,
                                                            in1=m[:], op0=ALU.mult, op1=ALU.mult),
                      reads=[R(m)], writes=[R(var)])
                    E("V", lambda q: q.scalar_tensor_tensor(out=var[:], in0=ssq[:], scalar=1.0 / D,
                                                            in1=var[:], op0=ALU.mult, op1=ALU.add),
                      reads=[R(ssq), R(var)], writes=[R(var)])
                    E("S", lambda q: q.activation(out=rstd[:], in_=var[:], func=AF.Sqrt,
                                                  bias=C(1e-5), scale=1.0),
                      reads=[R(var), R(cc)], writes=[R(rstd)])
                    E("V", lambda q: q.reciprocal(out=rstd[:], in_=rstd[:]),
                      reads=[R(rstd)], writes=[R(rstd)])
                    E("V", lambda q: q.scalar_tensor_tensor(out=nmr[:], in0=m[:], scalar=-1.0,
                                                            in1=rstd[:], op0=ALU.mult, op1=ALU.mult),
                      reads=[R(m), R(rstd)], writes=[R(nmr)])
                    E("S", lambda q, b=b: q.activation(out=xnb[b][:], in_=xt[b][:], func=AF.Identity,
                                                       bias=nmr[:, 0:1], scale=rstd[:, 0:1]),
                      reads=[R(xt[b]), R(nmr), R(rstd)], writes=[R(xnb[b])])
                    if write_xn_f:
                        E("S", lambda q, b=b: q.activation(out=xnf[:], in_=xt[b][:], func=AF.Identity,
                                                           bias=nmr[:, 0:1], scale=rstd[:, 0:1]),
                          reads=[R(xt[b]), R(nmr), R(rstd)], writes=[R(xnf)])
                        E("qA", lambda q, jt=jt: q.dma_start(
                            out=xn_f.ap()[jt * 128:(jt + 1) * 128, :], in_=xnf[:]),
                          reads=[R(xnf)], writes=["xn_f"])
                        if absmax_col is not None:
                            E("V", lambda q: q.tensor_reduce(out=m[:], in_=xnf[:], axis=AX.X,
                                                             op=ALU.max,
                                                             apply_absolute_value=True),
                              reads=[R(xnf)], writes=[R(m)])
                            E("V", lambda q: q.tensor_tensor(out=absmax_col, in0=absmax_col,
                                                             in1=m[:], op=ALU.max),
                              reads=[R(m), absmax_nm], writes=[absmax_nm])
                    for c in range(8):
                        E("T", lambda q, b=b, c=c: q.transpose(
                            pt[b][:, c * 128:(c + 1) * 128], xnb[b][:, c * 128:(c + 1) * 128],
                            ident[:]),
                          reads=[R(xnb[b]), R(ident)], writes=[R(pt[b])])
                    dst = xnT[:].rearrange("p (c t) -> p c t", c=8)[:, :, jt * 128:(jt + 1) * 128]
                    E("V", lambda q, b=b, dst=dst: q.tensor_copy(
                        out=dst, in_=pt[b][:].rearrange("p (c t) -> p c t", c=8)),
                      reads=[R(pt[b])], writes=[R(xnT)])

        def residual_epilogue(jt, g_src, g_nm, dst_dram, dst_nm, stile, xtile, xsrc, xsrc_nm):
            E("qS", lambda q, jt=jt: q.dma_start(
                out=stile[:], in_=s_scr.ap()[jt * 128:(jt + 1) * 128, :]),
              reads=["s_scr"], writes=[R(stile)])
            E("qS", lambda q, jt=jt: q.dma_start(
                out=xtile[:], in_=xsrc.ap()[jt * 128:(jt + 1) * 128, :]),
              reads=[xsrc_nm], writes=[R(xtile)])
            E("V", lambda q: q.tensor_mul(out=g_src[:], in0=g_src[:], in1=stile[:]),
              reads=[g_nm, R(stile)], writes=[g_nm])
            E("V", lambda q: q.tensor_add(out=g_src[:], in0=g_src[:], in1=xtile[:]),
              reads=[g_nm, R(xtile)], writes=[g_nm])
            E("qA", lambda q, jt=jt: q.dma_start(
                out=dst_dram.ap()[jt * 128:(jt + 1) * 128, :], in_=g_src[:]),
              reads=[g_nm], writes=[dst_nm])

        def layer0(xsrc, xsrc_nm, dst_dram, dst_nm):
            xnT = ctx.sbuf("xnT0", [128, 8 * T], BF16)
            layernorm_phase(xnT, xsrc, xsrc_nm, write_xn_f=False)

            with ctx.scope():
                sc = ctx.sbuf("l0_sc", [128, T], F32)
                outT = ctx.sbuf("l0_outT", [128, 8 * T], BF16)
                with ctx.scope():
                    td = ctx.sbuf("l0_td", [128, 8], F32)
                    eu = ctx.sbuf("l0_eu", [128, 8], F32)
                    dec = ctx.sbuf("l0_dec", [128, 8], F32)
                    E("qS", lambda q: q.dma_start(out=td[:], in_=din["l0_td"].ap()),
                      writes=[R(td)])
                    E("qS", lambda q: q.dma_start(out=eu[:], in_=din["l0_tf"].ap()),
                      writes=[R(eu)])
                    E("S", lambda q: q.activation(out=dec[:], in_=td[:], func=AF.Exp),
                      reads=[R(td)], writes=[R(dec)])
                    E("S", lambda q: q.activation(out=dec[:], in_=dec[:], func=AF.Exp, scale=-1.0),
                      reads=[R(dec)], writes=[R(dec)])
                    E("S", lambda q: q.activation(out=eu[:], in_=eu[:], func=AF.Exp),
                      reads=[R(eu)], writes=[R(eu)])

                    wk = ctx.sbuf("l0_wk", [128, 8 * D], BF16)
                    wv = ctx.sbuf("l0_wv", [128, 8 * D], BF16)
                    wr = ctx.sbuf("l0_wr", [128, 8 * D], BF16)
                    for nm, sb in (("l0_WkT", wk), ("l0_WvT", wv), ("l0_WrT", wr)):
                        E("qP", lambda q, nm=nm, sb=sb: q.dma_start(
                            out=sb[:].rearrange("p (c d) -> p c d", c=8),
                            in_=din[nm].ap().rearrange("(c p) d -> p c d", p=128)),
                          writes=[R(sb)])

                    H = T if T <= 128 else T // 2
                    kb = ctx.sbuf("l0_kb", [128, H], F32)
                    rb5 = ctx.sbuf("l0_rb5", [128, 512], F32)
                    wk5 = ctx.sbuf("l0_wk5", [128, 512], F32)
                    E("V", lambda q: q.memset(sc[:], 0.0), writes=[R(sc)])
                    ps = [ctx.psum("l0_ps", [128, 512], F32) for _ in range(4)]
                    for c in range(8):
                        for h in range(T // H):
                            base = h * H
                            chunks = [(s, min(s + 512, H)) for s in range(0, H, 512)]
                            for nt, (lo, hi) in enumerate(chunks):
                                w_ = hi - lo
                                pbk, pbv = (2 * nt) % 4, (2 * nt + 1) % 4
                                lhs = [wk[:, kc * D + c * 128: kc * D + (c + 1) * 128]
                                       for kc in range(8)]
                                rhs = [xnT[:, kc * T + base + lo: kc * T + base + hi]
                                       for kc in range(8)]
                                mm_tile(ps[pbk][:, 0:w_], R(ps[pbk]), lhs, rhs, [R(wk), R(xnT)])
                                E("S", lambda q, pbk=pbk, lo=lo, hi=hi, w_=w_: q.activation(
                                    out=kb[:, lo:hi], in_=ps[pbk][:, 0:w_], func=AF.Copy),
                                  reads=[R(ps[pbk])], writes=[R(kb)])
                                lhsv = [wv[:, kc * D + c * 128: kc * D + (c + 1) * 128]
                                        for kc in range(8)]
                                mm_tile(ps[pbv][:, 0:w_], R(ps[pbv]), lhsv, rhs, [R(wv), R(xnT)])
                                E("V", lambda q, pbv=pbv, lo=lo, hi=hi, w_=w_: q.tensor_mul(
                                    out=kb[:, lo:hi], in0=kb[:, lo:hi], in1=ps[pbv][:, 0:w_]),
                                  reads=[R(ps[pbv]), R(kb)], writes=[R(kb)])
                            if h == 0:
                                E("V", lambda q, c=c: q.tensor_tensor_scan(
                                    out=sc[:, 0:H], data0=dec[:, c:c + 1].broadcast_to([128, H]),
                                    data1=kb[:], initial=0.0, op0=ALU.mult, op1=ALU.add),
                                  reads=[R(kb), R(dec)], writes=[R(sc)])
                            else:
                                E("V", lambda q, c=c, base=base: q.tensor_tensor_scan(
                                    out=sc[:, base:base + H],
                                    data0=dec[:, c:c + 1].broadcast_to([128, H]),
                                    data1=kb[:], initial=sc[:, base - 1:base],
                                    op0=ALU.mult, op1=ALU.add),
                                  reads=[R(kb), R(dec), R(sc)], writes=[R(sc)])
                            for nt, (lo, hi) in enumerate(chunks):
                                w_ = hi - lo
                                gl = base + lo
                                pbr = nt % 4
                                lhsr = [wr[:, kc * D + c * 128: kc * D + (c + 1) * 128]
                                        for kc in range(8)]
                                rhs = [xnT[:, kc * T + gl: kc * T + gl + w_] for kc in range(8)]
                                mm_tile(ps[pbr][:, 0:w_], R(ps[pbr]), lhsr, rhs, [R(wr), R(xnT)])
                                E("S", lambda q, pbr=pbr, w_=w_: q.activation(
                                    out=rb5[:, 0:w_], in_=ps[pbr][:, 0:w_], func=AF.Sigmoid),
                                  reads=[R(ps[pbr])], writes=[R(rb5)])
                                if gl == 0:
                                    E("V", lambda q, c=c, lo=lo, hi=hi, gl=gl: q.scalar_tensor_tensor(
                                        out=wk5[:, 1:hi - lo], in0=kb[:, lo + 1:hi],
                                        scalar=eu[:, c:c + 1], in1=sc[:, gl:gl + hi - lo - 1],
                                        op0=ALU.mult, op1=ALU.add),
                                      reads=[R(kb), R(sc), R(eu)], writes=[R(wk5)])
                                    E("V", lambda q, c=c: q.tensor_scalar(
                                        out=wk5[:, 0:1], in0=kb[:, 0:1], scalar1=eu[:, c:c + 1],
                                        scalar2=None, op0=ALU.mult),
                                      reads=[R(kb), R(eu)], writes=[R(wk5)])
                                else:
                                    E("V", lambda q, c=c, lo=lo, w_=w_, gl=gl: q.scalar_tensor_tensor(
                                        out=wk5[:, 0:w_], in0=kb[:, lo:lo + w_],
                                        scalar=eu[:, c:c + 1], in1=sc[:, gl - 1:gl + w_ - 1],
                                        op0=ALU.mult, op1=ALU.add),
                                      reads=[R(kb), R(sc), R(eu)], writes=[R(wk5)])
                                E("V", lambda q, c=c, gl=gl, w_=w_: q.tensor_tensor(
                                    out=outT[:, c * T + gl: c * T + gl + w_], in0=rb5[:, 0:w_],
                                    in1=wk5[:, 0:w_], op=ALU.mult),
                                  reads=[R(rb5), R(wk5)], writes=[R(outT)])
                with ctx.scope():
                    wo = ctx.sbuf("l0_wo", [128, 8 * D], BF16)
                    E("qP", lambda q: q.dma_start(
                        out=wo[:].rearrange("p (c d) -> p c d", c=8),
                        in_=din["l0_WoT"].ap().rearrange("(c p) d -> p c d", p=128)),
                      writes=[R(wo)])
                    sbuf_s = ctx.sbuf("l0_sout", [128, D], F32)
                    po = [ctx.psum("l0_po", [128, 512], F32) for _ in range(2)]
                    for jt in range(NT):
                        for nh in range(2):
                            lhs = [outT[:, c * T + jt * 128: c * T + (jt + 1) * 128]
                                   for c in range(8)]
                            rhs = [wo[:, c * D + nh * 512: c * D + (nh + 1) * 512]
                                   for c in range(8)]
                            mm_tile(po[nh][:], R(po[nh]), lhs, rhs, [R(outT), R(wo)])
                            E("S", lambda q, nh=nh: q.activation(
                                out=sbuf_s[:, nh * 512:(nh + 1) * 512], in_=po[nh][:],
                                func=AF.Copy),
                              reads=[R(po[nh])], writes=[R(sbuf_s)])
                        E("qA", lambda q, jt=jt: q.dma_start(
                            out=s_scr.ap()[jt * 128:(jt + 1) * 128, :], in_=sbuf_s[:]),
                          reads=[R(sbuf_s)], writes=["s_scr"])

            wte = ctx.sbuf("l0_wte", [128, NT * 16], F32)
            hte = ctx.sbuf("l0_hte", [128, NT * 16], F32)
            with ctx.scope():
                gw = ctx.sbuf("l0_gw", [128, 8 * 16], BF16)
                E("qP", lambda q: q.dma_start(
                    out=gw[:].rearrange("p (c n) -> p c n", c=8),
                    in_=din["l0_gwT"].ap().rearrange("(c p) n -> p c n", p=128)),
                  writes=[R(gw)])
                pg = ctx.psum("l0_pg", [128, 512], F32)
                lg = ctx.sbuf("l0_lg", [128, 16], F32)
                mx = ctx.sbuf("l0_mx", [128, 1], F32)
                sm = ctx.sbuf("l0_sm", [128, 1], F32)
                eq1 = ctx.sbuf("l0_eq1", [128, 16], F32)
                msk = ctx.sbuf("l0_msk", [128, 16], F32)
                for jt in range(NT):
                    lhs = [xnT[:, c * T + jt * 128: c * T + (jt + 1) * 128] for c in range(8)]
                    rhs = [gw[:, c * 16:(c + 1) * 16] for c in range(8)]
                    mm_tile(pg[:, 0:16], R(pg), lhs, rhs, [R(xnT), R(gw)])
                    E("V", lambda q: q.tensor_reduce(out=mx[:], in_=pg[:, 0:16], axis=AX.X,
                                                     op=ALU.max),
                      reads=[R(pg)], writes=[R(mx)])
                    E("V", lambda q: q.tensor_scalar(out=mx[:], in0=mx[:], scalar1=-1.0,
                                                     scalar2=None, op0=ALU.mult),
                      reads=[R(mx)], writes=[R(mx)])
                    E("S", lambda q: q.activation(out=lg[:], in_=pg[:, 0:16], func=AF.Exp,
                                                  bias=mx[:, 0:1], scale=1.0,
                                                  accum_out=sm[:, 0:1]),
                      reads=[R(pg), R(mx)], writes=[R(lg), R(sm)])
                    E("V", lambda q: q.reciprocal(out=sm[:], in_=sm[:]),
                      reads=[R(sm)], writes=[R(sm)])
                    E("V", lambda q: q.tensor_scalar(out=lg[:], in0=lg[:], scalar1=sm[:, 0:1],
                                                     scalar2=None, op0=ALU.mult),
                      reads=[R(lg), R(sm)], writes=[R(lg)])
                    E("V", lambda q: q.tensor_reduce(out=mx[:], in_=lg[:], axis=AX.X, op=ALU.max),
                      reads=[R(lg)], writes=[R(mx)])
                    E("V", lambda q: q.tensor_scalar(out=eq1[:], in0=lg[:], scalar1=mx[:, 0:1],
                                                     scalar2=None, op0=ALU.is_ge),
                      reads=[R(lg), R(mx)], writes=[R(eq1)])
                    E("V", lambda q: q.scalar_tensor_tensor(out=msk[:], in0=eq1[:], scalar=-2.0,
                                                            in1=lg[:], op0=ALU.mult, op1=ALU.add),
                      reads=[R(eq1), R(lg)], writes=[R(msk)])
                    E("V", lambda q: q.tensor_reduce(out=mx[:], in_=msk[:], axis=AX.X, op=ALU.max),
                      reads=[R(msk)], writes=[R(mx)])
                    E("V", lambda q, jt=jt: q.tensor_scalar(
                        out=hte[:, jt * 16:(jt + 1) * 16], in0=msk[:], scalar1=mx[:, 0:1],
                        scalar2=None, op0=ALU.is_ge),
                      reads=[R(msk), R(mx)], writes=[R(hte)])
                    E("V", lambda q, jt=jt: q.tensor_add(
                        out=hte[:, jt * 16:(jt + 1) * 16], in0=hte[:, jt * 16:(jt + 1) * 16],
                        in1=eq1[:]),
                      reads=[R(hte), R(eq1)], writes=[R(hte)])
                    E("V", lambda q, jt=jt: q.tensor_mul(
                        out=wte[:, jt * 16:(jt + 1) * 16], in0=lg[:],
                        in1=hte[:, jt * 16:(jt + 1) * 16]),
                      reads=[R(lg), R(hte)], writes=[R(wte)])

            with ctx.scope():
                TBLK = min(1024, T)
                NBLK = T // TBLK
                NJB = TBLK // 128
                yb = ctx.sbuf("l0_yb", [128, TBLK * 8], F32)
                abuf = ctx.sbuf("l0_abuf", [128, 1023], F32)
                hbuf = ctx.sbuf("l0_hbuf", [128, 1024], F32)
                hbf = ctx.sbuf("l0_hbf", [128, 1024], BF16)
                hT = ctx.sbuf("l0_hT", [128, 8 * 128], BF16)
                oeb = ctx.sbuf("l0_oeb", [128, 1024], F32)
                zb = ctx.sbuf("l0_zb", [128, 1024], F32)
                sqs = ctx.sbuf("l0_sqs", [128, 1024], F32)
                ssc = ctx.sbuf("l0_ssc", [128, 1], F32)
                tcol = ctx.sbuf("l0_tcol", [128, 1], F32)
                p1 = ctx.psum("l0_p1", [128, 1024], F32)
                p2 = ctx.psum("l0_p2", [128, 1024], F32)
                p3 = ctx.psum("l0_p3", [128, 1024], F32)
                pt4 = ctx.psum("l0_pt4", [128, 1024], BF16)
                gbuf = ctx.sbuf("l0_gbuf", [128, 1024], F32)
                stile = ctx.sbuf("l0_stile", [128, 1024], F32)
                xtile = ctx.sbuf("l0_xtile", [128, 1024], F32)

                def lmlp(jt, n1, w1, b1, w3, b3, w2, b2):
                    nh = n1 + 1
                    nkh = nh // 128
                    ncol = [(0, min(512, n1))] + ([(512, n1)] if n1 > 512 else [])
                    lhs = [xnT[:, c * T + jt * 128: c * T + (jt + 1) * 128] for c in range(8)]
                    for (lo, hi) in ncol:
                        rhs = [w1[:, c * n1 + lo: c * n1 + hi] for c in range(8)]
                        mm_tile(p1[:, lo:hi], R(p1), lhs, rhs, [R(xnT), R(w1)], last=False)
                        E("T", lambda q, lo=lo, hi=hi: q.matmul(
                            p1[:, lo:hi], onesb[:], b1[:, lo:hi], start=False, stop=True, skip_group_check=True),
                          reads=[R(onesb), R(b1)], writes=[R(p1)])
                        E("S", lambda q, lo=lo, hi=hi: q.activation(
                            out=abuf[:, lo:hi], in_=p1[:, lo:hi], func=AF.Silu),
                          reads=[R(p1)], writes=[R(abuf)])
                        rhs = [w3[:, c * n1 + lo: c * n1 + hi] for c in range(8)]
                        mm_tile(p2[:, lo:hi], R(p2), lhs, rhs, [R(xnT), R(w3)], last=False)
                        E("T", lambda q, lo=lo, hi=hi: q.matmul(
                            p2[:, lo:hi], onesb[:], b3[:, lo:hi], start=False, stop=True, skip_group_check=True),
                          reads=[R(onesb), R(b3)], writes=[R(p2)])
                        E("V", lambda q, lo=lo, hi=hi: q.tensor_mul(
                            out=hbuf[:, 1 + lo:1 + hi], in0=abuf[:, lo:hi], in1=p2[:, lo:hi]),
                          reads=[R(abuf), R(p2)], writes=[R(hbuf)])
                    E("S", lambda q, n1=n1: q.activation(out=sqs[:, 0:n1], in_=hbuf[:, 1:1 + n1],
                                                         func=AF.Square, accum_out=ssc[:, 0:1]),
                      reads=[R(hbuf)], writes=[R(sqs), R(ssc)])
                    E("S", lambda q: q.activation(out=hbuf[:, 0:1], in_=ssc[:], func=AF.Sqrt,
                                                  bias=C(1.0), scale=1.0),
                      reads=[R(ssc), R(cc)], writes=[R(hbuf)])
                    E("V", lambda q, nh=nh: q.tensor_copy(out=hbf[:, 0:nh], in_=hbuf[:, 0:nh]),
                      reads=[R(hbuf)], writes=[R(hbf)])
                    for c in range(nkh):
                        E("T", lambda q, c=c: q.transpose(
                            pt4[:, c * 128:(c + 1) * 128], hbf[:, c * 128:(c + 1) * 128], ident[:]),
                          reads=[R(hbf), R(ident)], writes=[R(pt4)])
                    E("V", lambda q, nkh=nkh: q.tensor_copy(out=hT[:, 0:nkh * 128],
                                                            in_=pt4[:, 0:nkh * 128]),
                      reads=[R(pt4)], writes=[R(hT)])
                    lhs2 = [hT[:, c * 128:(c + 1) * 128] for c in range(nkh)]
                    for (lo, hi) in ((0, 512), (512, 1023)):
                        rhs = [w2[:, c * 1023 + lo: c * 1023 + hi] for c in range(nkh)]
                        mm_tile(p3[:, lo:hi], R(p3), lhs2, rhs, [R(hT), R(w2)], last=False)
                        E("T", lambda q, lo=lo, hi=hi: q.matmul(
                            p3[:, lo:hi], onesb[:], b2[:, lo:hi], start=False, stop=True, skip_group_check=True),
                          reads=[R(onesb), R(b2)], writes=[R(p3)])
                    E("S", lambda q: q.activation(out=oeb[:, 1:1024], in_=p3[:, 0:1023],
                                                  func=AF.Copy),
                      reads=[R(p3)], writes=[R(oeb)])
                    E("S", lambda q: q.activation(out=sqs[:, 0:1023], in_=p3[:, 0:1023],
                                                  func=AF.Square, accum_out=ssc[:, 0:1]),
                      reads=[R(p3)], writes=[R(sqs), R(ssc)])
                    E("S", lambda q: q.activation(out=oeb[:, 0:1], in_=ssc[:], func=AF.Sqrt,
                                                  bias=C(1.0), scale=1.0),
                      reads=[R(ssc), R(cc)], writes=[R(oeb)])

                def lresnet_update(y_ap, w_ap, wdeps, hit_ap, hitdeps):
                    E("V", lambda q: q.scalar_tensor_tensor(out=zb[:], in0=oeb[:], scalar=w_ap,
                                                            in1=y_ap, op0=ALU.mult, op1=ALU.add),
                      reads=[R(oeb), R(yb)] + wdeps, writes=[R(zb)])
                    E("S", lambda q: q.activation(out=sqs[:], in_=zb[:], func=AF.Square,
                                                  accum_out=ssc[:, 0:1]),
                      reads=[R(zb)], writes=[R(sqs), R(ssc)])
                    E("V", lambda q: q.scalar_tensor_tensor(out=tcol[:], in0=sqs[:, 0:1],
                                                            scalar=2.0, in1=ssc[:],
                                                            op0=ALU.mult, op1=ALU.subtract),
                      reads=[R(sqs), R(ssc)], writes=[R(tcol)])
                    E("V", lambda q: q.tensor_scalar(out=tcol[:], in0=tcol[:], scalar1=1e-6,
                                                     scalar2=None, op0=ALU.max),
                      reads=[R(tcol)], writes=[R(tcol)])
                    E("S", lambda q: q.activation(out=tcol[:], in_=tcol[:], func=AF.Sqrt),
                      reads=[R(tcol)], writes=[R(tcol)])
                    E("V", lambda q: q.reciprocal(out=tcol[:], in_=tcol[:]),
                      reads=[R(tcol)], writes=[R(tcol)])
                    E("V", lambda q: q.tensor_scalar(out=zb[:], in0=zb[:], scalar1=tcol[:, 0:1],
                                                     scalar2=None, op0=ALU.mult),
                      reads=[R(zb), R(tcol)], writes=[R(zb)])
                    if hit_ap is None:
                        return
                    E("V", lambda q: q.tensor_sub(out=zb[:], in0=zb[:], in1=y_ap),
                      reads=[R(zb), R(yb)], writes=[R(zb)])
                    E("V", lambda q: q.scalar_tensor_tensor(out=y_ap, in0=zb[:], scalar=hit_ap,
                                                            in1=y_ap, op0=ALU.mult, op1=ALU.add),
                      reads=[R(zb), R(yb)] + hitdeps, writes=[R(yb)])

                for blk in range(NBLK):
                    E("V", lambda q: q.memset(yb[:], 0.0), writes=[R(yb)])
                    E("V", lambda q: q.memset(
                        yb[:].rearrange("p (j d) -> p j d", j=NJB)[:, :, 0:1], 1.0),
                      reads=[R(yb)], writes=[R(yb)])
                    with ctx.scope():
                        we1 = [ctx.sbuf("l0_we1", [128, 8 * 511], BF16) for _ in range(2)]
                        we3 = [ctx.sbuf("l0_we3", [128, 8 * 511], BF16) for _ in range(2)]
                        we2 = [ctx.sbuf("l0_we2", [128, 4 * 1023], BF16) for _ in range(2)]
                        be1 = [ctx.sbuf("l0_be1", [1, 511], BF16) for _ in range(2)]
                        be3 = [ctx.sbuf("l0_be3", [1, 511], BF16) for _ in range(2)]
                        be2 = [ctx.sbuf("l0_be2", [1, 1023], BF16) for _ in range(2)]

                        def load_expert(e, b):
                            for nm, sb, c_ in (("l0_We1T", we1[b], 8), ("l0_We3T", we3[b], 8),
                                               ("l0_We2T", we2[b], 4)):
                                E("qP", lambda q, nm=nm, sb=sb, c_=c_, e=e: q.dma_start(
                                    out=sb[:].rearrange("p (c n) -> p c n", c=c_),
                                    in_=din[nm].ap()[e].rearrange("(c p) n -> p c n", p=128)),
                                  writes=[R(sb)])
                            for nm, sb in (("l0_be1", be1[b]), ("l0_be3", be3[b]),
                                           ("l0_be2", be2[b])):
                                E("qP", lambda q, nm=nm, sb=sb, e=e: q.dma_start(
                                    out=sb[:], in_=din[nm].ap()[e:e + 1, :]),
                                  writes=[R(sb)])

                        load_expert(0, 0)
                        for e in range(16):
                            b = e % 2
                            if e < 15:
                                load_expert(e + 1, 1 - b)
                            for jl in range(NJB):
                                jt = blk * NJB + jl
                                lmlp(jt, 511, we1[b], be1[b], we3[b], be3[b], we2[b], be2[b])
                                y_ap = yb[:, jl * 1024:(jl + 1) * 1024]
                                lresnet_update(y_ap, wte[:, jt * 16 + e: jt * 16 + e + 1],
                                               [R(wte)], hte[:, jt * 16 + e: jt * 16 + e + 1],
                                               [R(hte)])
                    with ctx.scope():
                        ws1 = ctx.sbuf("l0_ws1", [128, 8 * 1023], BF16)
                        ws3 = ctx.sbuf("l0_ws3", [128, 8 * 1023], BF16)
                        ws2 = ctx.sbuf("l0_ws2", [128, 8 * 1023], BF16)
                        bs1 = ctx.sbuf("l0_bs1", [1, 1023], BF16)
                        bs3 = ctx.sbuf("l0_bs3", [1, 1023], BF16)
                        bs2 = ctx.sbuf("l0_bs2", [1, 1023], BF16)
                        for nm, sb in (("l0_Ws1T", ws1), ("l0_Ws3T", ws3), ("l0_Ws2T", ws2)):
                            E("qP", lambda q, nm=nm, sb=sb: q.dma_start(
                                out=sb[:].rearrange("p (c n) -> p c n", c=8),
                                in_=din[nm].ap().rearrange("(c p) n -> p c n", p=128)),
                              writes=[R(sb)])
                        for nm, sb in (("l0_bs1", bs1), ("l0_bs3", bs3), ("l0_bs2", bs2)):
                            E("qP", lambda q, nm=nm, sb=sb: q.dma_start(
                                out=sb[:], in_=din[nm].ap()),
                              writes=[R(sb)])
                        for jl in range(NJB):
                            jt = blk * NJB + jl
                            lmlp(jt, 1023, ws1, bs1, ws3, bs3, ws2, bs2)
                            y_ap = yb[:, jl * 1024:(jl + 1) * 1024]
                            lresnet_update(y_ap, 1.0, [], None, None)
                            E("S", lambda q: q.activation(out=gbuf[:], in_=zb[:], func=AF.Silu),
                              reads=[R(zb)], writes=[R(gbuf)])
                            residual_epilogue(jt, gbuf, R(gbuf), dst_dram, dst_nm, stile,
                                              xtile, xsrc, xsrc_nm)


        onesf = ctx.sbuf("onesf", [1, 128], F32)
        E("qS", lambda q: q.dma_start(out=onesf[:], in_=din["ones_row"].ap()),
          writes=[R(onesf)])
        pbc_ps = ctx.psum("pbc_ps", [128, 512], F32)
        pm_row = ctx.sbuf("pm_row", [1, 128], F32)
        pm_val = ctx.sbuf("pm_val", [1, 1], F32)
        _pm_n = [0]

        def pbcast(dst_ap, dstnm, src_ap, srcnms, n=1):
            E("T", lambda q: q.matmul(pbc_ps[:, 0:n], onesf[:], src_ap, start=True,
                                      stop=True, skip_group_check=True),
              reads=list(srcnms) + [R(onesf)], writes=[R(pbc_ps)])
            E("V", lambda q: q.tensor_copy(out=dst_ap, in_=pbc_ps[:, 0:n]),
              reads=[R(pbc_ps)], writes=[dstnm])

        def pmax_col(colap, colnm):
            _pm_n[0] += 1
            scr = dram(f"pm_scr{_pm_n[0]}", [128])
            snm = f"pm_scr{_pm_n[0]}"
            E("qP", lambda q: q.dma_start(
                out=scr.ap().rearrange("(p o) -> p o", o=1), in_=colap),
              reads=[colnm], writes=[snm])
            E("qS", lambda q: q.dma_start(
                out=pm_row[:], in_=scr.ap().rearrange("(o p) -> o p", o=1)),
              reads=[snm], writes=[R(pm_row)])
            E("V", lambda q: q.tensor_reduce(out=pm_val[:], in_=pm_row[:], axis=AX.X,
                                             op=ALU.max),
              reads=[R(pm_row)], writes=[R(pm_val)])
            pbcast(colap, colnm, pm_val[0:1, 0:1], [R(pm_val)], 1)

        def armax_global(col, colnm, bname):
            pmax_col(col[:, 0:1], colnm)
            bi = dram(bname + "_i", [1, 1])
            bo = dram(bname + "_o", [1, 1])
            E("qP", lambda q: q.dma_start(out=bi.ap(), in_=col[0:1, 0:1]),
              reads=[colnm], writes=[bname + "_i"])
            E("P", lambda q: q.collective_compute(
                "AllReduce", ALU.max, replica_groups=[list(range(n_cores))],
                ins=[bi.ap()], outs=[bo.ap()]),
              reads=[bname + "_i"], writes=[bname + "_o"])
            E("qS", lambda q: q.dma_start(out=pm_val[:], in_=bo.ap()),
              reads=[bname + "_o"], writes=[R(pm_val)])
            pbcast(col[:, 0:1], colnm, pm_val[0:1, 0:1], [R(pm_val)], 1)

        def round_ste(dst, dstnm, src_ap, srcnms, tmp, w):
            E("V", lambda q: q.tensor_scalar(out=dst[:, 0:w], in0=src_ap, scalar1=2.0 ** 23,
                                             scalar2=2.0 ** 23, op0=ALU.add,
                                             op1=ALU.subtract),
              reads=list(srcnms), writes=[dstnm])

        def qdq(buf, bufnm, tmp, rs_col, s_col, scnms, qmin, qmax, w):
            E("V", lambda q: q.tensor_scalar(out=buf[:, 0:w], in0=buf[:, 0:w],
                                             scalar1=rs_col, scalar2=None, op0=ALU.mult),
              reads=[bufnm] + list(scnms), writes=[bufnm])
            round_ste(buf, bufnm, buf[:, 0:w], [bufnm], tmp, w)
            E("V", lambda q: q.tensor_scalar(out=buf[:, 0:w], in0=buf[:, 0:w], scalar1=qmin,
                                             scalar2=qmax, op0=ALU.max, op1=ALU.min),
              reads=[bufnm], writes=[bufnm])
            E("V", lambda q: q.tensor_scalar(out=buf[:, 0:w], in0=buf[:, 0:w],
                                             scalar1=s_col, scalar2=None, op0=ALU.mult),
              reads=[bufnm] + list(scnms), writes=[bufnm])

        def e4m3fn(buf, bufnm, t1, tf8a, tf8b, w):
            E("V", lambda q: q.tensor_scalar(out=t1[:, 0:w], in0=buf[:, 0:w], scalar1=0.5,
                                             scalar2=None, op0=ALU.mult),
              reads=[bufnm], writes=[R(t1)])
            E("P", lambda q: q.tensor_copy(out=tf8a[:, 0:w], in_=t1[:, 0:w]),
              reads=[R(t1)], writes=[R(tf8a)])
            E("V", lambda q: q.tensor_scalar(out=t1[:, 0:w], in0=buf[:, 0:w], scalar1=1.0,
                                             scalar2=None, op0=ALU.min),
              reads=[bufnm], writes=[R(t1)])
            E("P", lambda q: q.tensor_copy(out=tf8b[:, 0:w], in_=t1[:, 0:w]),
              reads=[R(t1)], writes=[R(tf8b)])
            E("V", lambda q: q.tensor_scalar(out=t1[:, 0:w], in0=buf[:, 0:w], scalar1=1.0,
                                             scalar2=None, op0=ALU.is_ge),
              reads=[bufnm], writes=[R(t1)])
            E("P", lambda q: q.tensor_copy(out=buf[:, 0:w], in_=tf8a[:, 0:w]),
              reads=[R(tf8a)], writes=[bufnm])
            E("V", lambda q: q.tensor_scalar(out=buf[:, 0:w], in0=buf[:, 0:w], scalar1=2.0,
                                             scalar2=None, op0=ALU.mult),
              reads=[bufnm], writes=[bufnm])
            E("V", lambda q: q.tensor_sub(out=buf[:, 0:w], in0=buf[:, 0:w], in1=tf8b[:, 0:w]),
              reads=[bufnm, R(tf8b)], writes=[bufnm])
            E("V", lambda q: q.tensor_mul(out=buf[:, 0:w], in0=buf[:, 0:w], in1=t1[:, 0:w]),
              reads=[bufnm, R(t1)], writes=[bufnm])
            E("V", lambda q: q.tensor_add(out=buf[:, 0:w], in0=buf[:, 0:w], in1=tf8b[:, 0:w]),
              reads=[bufnm, R(tf8b)], writes=[bufnm])

        def digitize_fp4(a, anm, acc1, acc2, acc3, w):
            E("V", lambda q: q.tensor_scalar(out=acc1[:, 0:w], in0=a[:, 0:w], scalar1=0.25,
                                             scalar2=None, op0=ALU.is_ge),
              reads=[anm], writes=[R(acc1)])
            for mid in (0.75, 1.25, 1.75):
                E("V", lambda q, mid=mid: q.scalar_tensor_tensor(
                    out=acc1[:, 0:w], in0=a[:, 0:w], scalar=mid, in1=acc1[:, 0:w],
                    op0=ALU.is_ge, op1=ALU.add),
                  reads=[anm, R(acc1)], writes=[R(acc1)])
            E("V", lambda q: q.tensor_scalar(out=acc2[:, 0:w], in0=a[:, 0:w], scalar1=2.5,
                                             scalar2=None, op0=ALU.is_ge),
              reads=[anm], writes=[R(acc2)])
            E("V", lambda q: q.scalar_tensor_tensor(out=acc2[:, 0:w], in0=a[:, 0:w],
                                                    scalar=3.5, in1=acc2[:, 0:w],
                                                    op0=ALU.is_ge, op1=ALU.add),
              reads=[anm, R(acc2)], writes=[R(acc2)])
            E("V", lambda q: q.tensor_scalar(out=acc3[:, 0:w], in0=a[:, 0:w], scalar1=5.0,
                                             scalar2=2.0, op0=ALU.is_ge, op1=ALU.mult),
              reads=[anm], writes=[R(acc3)])
            E("V", lambda q: q.scalar_tensor_tensor(out=a[:, 0:w], in0=acc1[:, 0:w],
                                                    scalar=0.5, in1=acc2[:, 0:w],
                                                    op0=ALU.mult, op1=ALU.add),
              reads=[R(acc1), R(acc2)], writes=[anm])
            E("V", lambda q: q.tensor_add(out=a[:, 0:w], in0=a[:, 0:w], in1=acc3[:, 0:w]),
              reads=[anm, R(acc3)], writes=[anm])

        def layer1(xsrc, xsrc_nm, dst_dram, dst_nm):
            oT_scr = dram("l1_oT", [T, 2048], BF16)

            with ctx.scope():
                xnT = ctx.sbuf("xnT1", [128, 8 * T], BF16)
                layernorm_phase(xnT, xsrc, xsrc_nm, write_xn_f=True)
                onesc = ctx.sbuf("onesc", [128, 1], BF16)
                E("V", lambda q: q.memset(onesc[:], 1.0), writes=[R(onesc)])
                wqt = ctx.sbuf("l1_wqt", [128, 8 * 2048], BF16)
                E("qP", lambda q: q.dma_start(
                    out=wqt[:].rearrange("p (c n) -> p c n", c=8),
                    in_=din["l1_WqT"].ap().rearrange("(c p) n -> p c n", p=128)),
                  writes=[R(wqt)])
                wkvd = ctx.sbuf("l1_wkvd", [128, 8 * 64], BF16)
                E("qP", lambda q: q.dma_start(
                    out=wkvd[:].rearrange("p (c n) -> p c n", c=8),
                    in_=din["l1_WkvdT"].ap().rearrange("(c p) n -> p c n", p=128)),
                  writes=[R(wkvd)])
                wku = ctx.sbuf("l1_wku", [64, 2048], BF16)
                wvu = ctx.sbuf("l1_wvu", [64, 2048], BF16)
                E("qP", lambda q: q.dma_start(out=wku[:], in_=din["l1_WkuT"].ap()),
                  writes=[R(wku)])
                E("qP", lambda q: q.dma_start(out=wvu[:], in_=din["l1_WvuT"].ap()),
                  writes=[R(wvu)])

                latT = ctx.sbuf("l1_latT", [64, T], BF16)
                psA = [ctx.psum("l1_psA", [128, 512], F32) for _ in range(2)]
                nch = [(s, min(s + 512, T)) for s in range(0, T, 512)]
                for nt, (lo, hi) in enumerate(nch):
                    pb = nt % 2
                    w_ = hi - lo
                    lhs = [wkvd[:, kc * 64:(kc + 1) * 64] for kc in range(8)]
                    rhs = [xnT[:, kc * T + lo: kc * T + hi] for kc in range(8)]
                    mm_tile(psA[pb][0:64, 0:w_], R(psA[pb]), lhs, rhs, [R(wkvd), R(xnT)])
                    E("V", lambda q, pb=pb, lo=lo, hi=hi, w_=w_: q.tensor_copy(
                        out=latT[:, lo:hi], in_=psA[pb][0:64, 0:w_]),
                      reads=[R(psA[pb])], writes=[R(latT)])

                qfm = ctx.sbuf("l1_qfm", [128, 2 * T], BF16)
                kfm = ctx.sbuf("l1_kfm", [128, 2 * T], BF16)
                vtm = ctx.sbuf("l1_vtm", [128, NT * 256], BF16)
                sqb = ctx.sbuf("l1_sqb", [128, 512], BF16)
                gk = ctx.sbuf("l1_gk", [1, 2], F32)
                gtmp = ctx.sbuf("l1_gtmp", [1, 1], F32)
                bnd = ctx.sbuf("l1_bnd", [128, 1], F32)
                expb = ctx.sbuf("l1_expb", [128, 512], BF16)
                expT = ctx.sbuf("l1_expT", [128, 512], BF16)
                sums = ctx.sbuf("l1_sums", [128, (T + 511) // 512], F32)
                rsum = ctx.sbuf("l1_rsum", [128, 1], F32)
                otile = ctx.sbuf("l1_otile", [128, 256], BF16)
                psN = ctx.psum("l1_psN", [1, 512], F32)
                psT = ctx.psum("l1_psT", [128, 1024], BF16)
                psO = ctx.psum("l1_psO", [128, 512], F32)

                for h in range(8):
                    for m in range(2):
                        for nt, (lo, hi) in enumerate(nch):
                            pb = nt % 2
                            w_ = hi - lo
                            lhs = [wqt[:, kc * 2048 + h * 256 + m * 128:
                                       kc * 2048 + h * 256 + (m + 1) * 128]
                                   for kc in range(8)]
                            rhs = [xnT[:, kc * T + lo: kc * T + hi] for kc in range(8)]
                            mm_tile(psA[pb][:, 0:w_], R(psA[pb]), lhs, rhs, [R(wqt), R(xnT)])
                            E("V", lambda q, pb=pb, m=m, lo=lo, hi=hi, w_=w_: q.tensor_copy(
                                out=qfm[:, m * T + lo: m * T + hi], in_=psA[pb][:, 0:w_]),
                              reads=[R(psA[pb])], writes=[R(qfm)])
                            mm_tile(psA[1 - pb][:, 0:w_], R(psA[1 - pb]),
                                    [wku[0:64, h * 256 + m * 128: h * 256 + (m + 1) * 128]],
                                    [latT[:, lo:hi]], [R(wku), R(latT)])
                            E("S", lambda q, pb=pb, m=m, lo=lo, hi=hi, w_=w_: q.activation(
                                out=kfm[:, m * T + lo: m * T + hi], in_=psA[1 - pb][:, 0:w_],
                                func=AF.Copy),
                              reads=[R(psA[1 - pb])], writes=[R(kfm)])
                    for jt in range(NT):
                        pb = jt % 2
                        mm_tile(psA[pb][:, 0:256], R(psA[pb]),
                                [latT[:, jt * 128:(jt + 1) * 128]],
                                [wvu[0:64, h * 256:(h + 1) * 256]], [R(latT), R(wvu)])
                        E("V", lambda q, pb=pb, jt=jt: q.tensor_copy(
                            out=vtm[:, jt * 256:(jt + 1) * 256], in_=psA[pb][:, 0:256]),
                          reads=[R(psA[pb])], writes=[R(vtm)])
                    E("V", lambda q: q.memset(gk[:], 0.0), writes=[R(gk)])
                    for src, srcnm, gi in ((qfm, R(qfm), 0), (kfm, R(kfm), 1)):
                        for nt, (lo, hi) in enumerate(nch):
                            w_ = hi - lo
                            E("S", lambda q, src=src, lo=lo, hi=hi: q.activation(
                                out=sqb[:, 0:hi - lo], in_=src[:, lo:hi], func=AF.Square),
                              reads=[srcnm], writes=[R(sqb)])
                            E("T", lambda q, w_=w_: q.matmul(psN[:, 0:w_], onesc[:],
                                                             sqb[:, 0:w_], start=True,
                                                             stop=False),
                              reads=[R(onesc), R(sqb)], writes=[R(psN)])
                            E("S", lambda q, src=src, lo=lo, hi=hi: q.activation(
                                out=sqb[:, 0:hi - lo], in_=src[:, T + lo: T + hi],
                                func=AF.Square),
                              reads=[srcnm], writes=[R(sqb)])
                            E("T", lambda q, w_=w_: q.matmul(psN[:, 0:w_], onesc[:],
                                                             sqb[:, 0:w_], start=False,
                                                             stop=True),
                              reads=[R(onesc), R(sqb)], writes=[R(psN)])
                            E("V", lambda q, w_=w_: q.tensor_reduce(
                                out=gtmp[0:1, 0:1], in_=psN[0:1, 0:w_], axis=AX.X,
                                op=ALU.max),
                              reads=[R(psN)], writes=[R(gtmp)])
                            E("V", lambda q, gi=gi: q.tensor_tensor(
                                out=gk[0:1, gi:gi + 1], in0=gk[0:1, gi:gi + 1],
                                in1=gtmp[0:1, 0:1], op=ALU.max),
                              reads=[R(gtmp), R(gk)], writes=[R(gk)])
                    E("V", lambda q: q.tensor_mul(out=gk[0:1, 0:1], in0=gk[0:1, 0:1],
                                                  in1=gk[0:1, 1:2]),
                      reads=[R(gk)], writes=[R(gk)])
                    E("S", lambda q: q.activation(out=gk[0:1, 0:1], in_=gk[0:1, 0:1],
                                                  func=AF.Sqrt, scale=1.0 / 256.0),
                      reads=[R(gk)], writes=[R(gk)])
                    E("V", lambda q: q.tensor_scalar(out=gk[0:1, 0:1], in0=gk[0:1, 0:1],
                                                     scalar1=-1.0, scalar2=None, op0=ALU.mult),
                      reads=[R(gk)], writes=[R(gk)])
                    pbcast(bnd[:, 0:1], R(bnd), gk[0:1, 0:1], [R(gk)], 1)

                    for jt in range(NT):
                        for kc, (klo, khi) in enumerate(nch):
                            kw = khi - klo
                            lhs = [qfm[:, m * T + jt * 128: m * T + (jt + 1) * 128]
                                   for m in range(2)]
                            rhs = [kfm[:, m * T + klo: m * T + khi] for m in range(2)]
                            mm_tile(psA[0][:, 0:kw], R(psA[0]), lhs, rhs, [R(qfm), R(kfm)])
                            E("S", lambda q, kc=kc, kw=kw: q.activation(
                                out=expb[:, 0:kw], in_=psA[0][:, 0:kw], func=AF.Exp,
                                bias=bnd[:, 0:1], scale=1.0 / 16.0,
                                accum_out=sums[:, kc:kc + 1]),
                              reads=[R(psA[0]), R(bnd)], writes=[R(expb), R(sums)])
                            nsub = kw // 128
                            for i in range(nsub):
                                E("T", lambda q, i=i: q.transpose(
                                    psT[:, i * 128:(i + 1) * 128],
                                    expb[:, i * 128:(i + 1) * 128], ident[:]),
                                  reads=[R(expb), R(ident)], writes=[R(psT)])
                            E("V", lambda q, kw=kw: q.tensor_copy(out=expT[:, 0:kw],
                                                                  in_=psT[:, 0:kw]),
                              reads=[R(psT)], writes=[R(expT)])
                            for i in range(nsub):
                                kt = klo // 128 + i
                                E("T", lambda q, i=i, kt=kt, kc=kc: q.matmul(
                                    psO[:, 0:256], expT[:, i * 128:(i + 1) * 128],
                                    vtm[:, kt * 256:(kt + 1) * 256],
                                    start=(kc == 0 and i == 0),
                                    stop=(kt == NT - 1), skip_group_check=True),
                                  reads=[R(expT), R(vtm)], writes=[R(psO)])
                        E("V", lambda q: q.tensor_reduce(out=rsum[:], in_=sums[:, 0:len(nch)],
                                                         axis=AX.X, op=ALU.add),
                          reads=[R(sums)], writes=[R(rsum)])
                        E("V", lambda q: q.reciprocal(out=rsum[:], in_=rsum[:]),
                          reads=[R(rsum)], writes=[R(rsum)])
                        E("S", lambda q: q.activation(out=otile[:], in_=psO[:, 0:256],
                                                      func=AF.Identity, scale=rsum[:, 0:1]),
                          reads=[R(psO), R(rsum)], writes=[R(otile)])
                        E("qA", lambda q, jt=jt, h=h: q.dma_start(
                            out=oT_scr.ap()[jt * 128:(jt + 1) * 128, h * 256:(h + 1) * 256],
                            in_=otile[:]),
                          reads=[R(otile)], writes=["l1_oT"])

            with ctx.scope():
                woT = ctx.sbuf("l1_woT", [128, 16 * 1024], BF16)
                E("qP", lambda q: q.dma_start(
                    out=woT[:].rearrange("p (c n) -> p c n", c=16),
                    in_=din["l1_WoT"].ap().rearrange("(c p) n -> p c n", p=128)),
                  writes=[R(woT)])
                oin = [ctx.sbuf("l1_oin", [128, 2048], BF16) for _ in range(2)]
                otT = ctx.sbuf("l1_otT", [128, 16 * 128], BF16)
                sb = ctx.sbuf("l1_sb", [128, 1024], F32)
                psT2 = [ctx.psum("l1_psT2", [128, 1024], BF16) for _ in range(2)]
                psS = [ctx.psum("l1_psS", [128, 512], F32) for _ in range(2)]
                for jt in range(NT):
                    b = jt % 2
                    E("qS", lambda q, jt=jt, b=b: q.dma_start(
                        out=oin[b][:], in_=oT_scr.ap()[jt * 128:(jt + 1) * 128, :]),
                      reads=["l1_oT"], writes=[R(oin[b])])
                    for g2 in range(2):
                        for i in range(8):
                            E("T", lambda q, b=b, g2=g2, i=i: q.transpose(
                                psT2[g2][:, i * 128:(i + 1) * 128],
                                oin[b][:, g2 * 1024 + i * 128: g2 * 1024 + (i + 1) * 128],
                                ident[:]),
                              reads=[R(oin[b]), R(ident)], writes=[R(psT2[g2])])
                        E("V", lambda q, g2=g2: q.tensor_copy(
                            out=otT[:, g2 * 1024:(g2 + 1) * 1024], in_=psT2[g2][:]),
                          reads=[R(psT2[g2])], writes=[R(otT)])
                    for nh in range(2):
                        lhs = [otT[:, kc * 128:(kc + 1) * 128] for kc in range(16)]
                        rhs = [woT[:, kc * 1024 + nh * 512: kc * 1024 + (nh + 1) * 512]
                               for kc in range(16)]
                        mm_tile(psS[nh][:], R(psS[nh]), lhs, rhs, [R(otT), R(woT)])
                        E("S", lambda q, nh=nh: q.activation(
                            out=sb[:, nh * 512:(nh + 1) * 512], in_=psS[nh][:], func=AF.Copy),
                          reads=[R(psS[nh])], writes=[R(sb)])
                    E("qA", lambda q, jt=jt: q.dma_start(
                        out=s_scr.ap()[jt * 128:(jt + 1) * 128, :], in_=sb[:]),
                      reads=[R(sb)], writes=["s_scr"])

            hT_scr = dram("l1_hT", [4096, T], BF16)
            with ctx.scope():
                q_sc = ctx.sbuf("l1q_sc", [128, 64], F32)
                q_am = ctx.sbuf("l1q_am", [128, 64], F32)
                q_t1 = ctx.sbuf("l1q_t1", [128, 1024], F32)
                q_b = ctx.sbuf("l1q_b", [128, 1024], F32)
                q_c = ctx.sbuf("l1q_c", [128, 1024], F32)
                q_f8a = ctx.sbuf("l1q_f8a", [128, 64], F8E4)
                q_f8b = ctx.sbuf("l1q_f8b", [128, 64], F8E4)
                sgn = ctx.sbuf("l1q_sgn", [128, 1024], F32)
                abf = ctx.sbuf("l1q_abf", [128, 1024], BF16)
                psQ = ctx.psum("l1q_psQ", [128, 1024], BF16)

                def nv_quant(buf, bufnm, lo, tw, gs_col, rgs_col, gsnms):
                    nb = tw // 16
                    bap = buf[:, lo:lo + tw]
                    E("S", lambda q: q.activation(out=sgn[:, 0:tw], in_=bap, func=AF.Sign),
                      reads=[bufnm], writes=[R(sgn)])
                    E("V", lambda q: q.tensor_reduce(
                        out=q_am[:, 0:nb], in_=bap.rearrange("p (b k) -> p b k", k=16),
                        axis=AX.X, op=ALU.max, apply_absolute_value=True),
                      reads=[bufnm], writes=[R(q_am)])
                    E("V", lambda q: q.tensor_scalar(out=q_sc[:, 0:nb], in0=q_am[:, 0:nb],
                                                     scalar1=gs_col, scalar2=1.0 / 6.0,
                                                     op0=ALU.mult, op1=ALU.mult),
                      reads=[R(q_am)] + list(gsnms), writes=[R(q_sc)])
                    E("V", lambda q: q.tensor_scalar(out=q_sc[:, 0:nb], in0=q_sc[:, 0:nb],
                                                     scalar1=0.0, scalar2=448.0,
                                                     op0=ALU.max, op1=ALU.min),
                      reads=[R(q_sc)], writes=[R(q_sc)])
                    e4m3fn(q_sc, R(q_sc), q_t1, q_f8a, q_f8b, nb)
                    E("V", lambda q: q.tensor_scalar(out=q_am[:, 0:nb], in0=q_sc[:, 0:nb],
                                                     scalar1=rgs_col, scalar2=2.0 ** -40,
                                                     op0=ALU.mult, op1=ALU.max),
                      reads=[R(q_sc)] + list(gsnms), writes=[R(q_am)])
                    E("V", lambda q: q.reciprocal(out=q_am[:, 0:nb], in_=q_am[:, 0:nb]),
                      reads=[R(q_am)], writes=[R(q_am)])
                    E("S", lambda q: q.activation(out=bap, in_=bap, func=AF.Abs),
                      reads=[bufnm], writes=[bufnm])
                    E("V", lambda q: q.tensor_tensor(
                        out=bap.rearrange("p (b k) -> p b k", k=16),
                        in0=bap.rearrange("p (b k) -> p b k", k=16),
                        in1=q_am[:, 0:nb].rearrange("p (b o) -> p b o", o=1).broadcast_to([128, nb, 16]),
                        op=ALU.mult),
                      reads=[bufnm, R(q_am)], writes=[bufnm])
                    E("V", lambda q: q.tensor_scalar(out=bap, in0=bap, scalar1=6.0,
                                                     scalar2=None, op0=ALU.min),
                      reads=[bufnm], writes=[bufnm])
                    digitize_fp4_ap(bap, bufnm, q_t1, q_b, q_c, tw)
                    E("V", lambda q: q.tensor_mul(out=bap, in0=bap, in1=sgn[:, 0:tw]),
                      reads=[bufnm, R(sgn)], writes=[bufnm])
                    E("V", lambda q: q.tensor_tensor(
                        out=bap.rearrange("p (b k) -> p b k", k=16),
                        in0=bap.rearrange("p (b k) -> p b k", k=16),
                        in1=q_sc[:, 0:nb].rearrange("p (b o) -> p b o", o=1).broadcast_to([128, nb, 16]),
                        op=ALU.mult),
                      reads=[bufnm, R(q_sc)], writes=[bufnm])

                def digitize_fp4_ap(bap, bufnm, acc1, acc2, acc3, tw):
                    E("V", lambda q: q.tensor_scalar(out=acc1[:, 0:tw], in0=bap, scalar1=0.25,
                                                     scalar2=None, op0=ALU.is_ge),
                      reads=[bufnm], writes=[R(acc1)])
                    for mid in (0.75, 1.25, 1.75):
                        E("V", lambda q, mid=mid: q.scalar_tensor_tensor(
                            out=acc1[:, 0:tw], in0=bap, scalar=mid, in1=acc1[:, 0:tw],
                            op0=ALU.is_ge, op1=ALU.add),
                          reads=[bufnm, R(acc1)], writes=[R(acc1)])
                    E("V", lambda q: q.tensor_scalar(out=acc2[:, 0:tw], in0=bap, scalar1=2.5,
                                                     scalar2=None, op0=ALU.is_ge),
                      reads=[bufnm], writes=[R(acc2)])
                    E("V", lambda q: q.scalar_tensor_tensor(
                        out=acc2[:, 0:tw], in0=bap, scalar=3.5, in1=acc2[:, 0:tw],
                        op0=ALU.is_ge, op1=ALU.add),
                      reads=[bufnm, R(acc2)], writes=[R(acc2)])
                    E("V", lambda q: q.tensor_scalar(out=acc3[:, 0:tw], in0=bap, scalar1=5.0,
                                                     scalar2=2.0, op0=ALU.is_ge, op1=ALU.mult),
                      reads=[bufnm], writes=[R(acc3)])
                    E("V", lambda q: q.scalar_tensor_tensor(
                        out=bap, in0=acc1[:, 0:tw], scalar=0.5, in1=acc2[:, 0:tw],
                        op0=ALU.mult, op1=ALU.add),
                      reads=[R(acc1), R(acc2)], writes=[bufnm])
                    E("V", lambda q: q.tensor_add(out=bap, in0=bap, in1=acc3[:, 0:tw]),
                      reads=[bufnm, R(acc3)], writes=[bufnm])

                wbuf = [ctx.sbuf("l1q_wbuf", [128, 1024], F32) for _ in range(2)]
                gmax = ctx.sbuf("l1q_gmax", [128, 8], F32)
                gs = ctx.sbuf("l1q_gs", [128, 8], F32)
                rgs = ctx.sbuf("l1q_rgs", [128, 8], F32)
                psc = ctx.sbuf("l1q_psc", [128, 8], F32)
                E("V", lambda q: q.memset(gmax[:], 0.0), writes=[R(gmax)])
                for wi, (nm, ntile, nch2) in enumerate((("l1_gw", 32, 1), ("l1_uw", 32, 1),
                                                        ("l1_dw", 8, 4))):
                    for it in range(ntile):
                        for c2 in range(nch2):
                            b = it % 2
                            E("qS", lambda q, nm=nm, it=it, b=b, c2=c2: q.dma_start(
                                out=wbuf[b][:],
                                in_=din[nm].ap()[it * 128:(it + 1) * 128,
                                                 c2 * 1024:(c2 + 1) * 1024]),
                              writes=[R(wbuf[b])])
                            E("V", lambda q, b=b: q.tensor_reduce(
                                out=q_sc[:, 0:1], in_=wbuf[b][:], axis=AX.X, op=ALU.max,
                                apply_absolute_value=True),
                              reads=[R(wbuf[b])], writes=[R(q_sc)])
                            E("V", lambda q, wi=wi: q.tensor_tensor(
                                out=gmax[:, wi:wi + 1], in0=gmax[:, wi:wi + 1],
                                in1=q_sc[:, 0:1], op=ALU.max),
                              reads=[R(gmax), R(q_sc)], writes=[R(gmax)])
                    pmax_col(gmax[:, wi:wi + 1], R(gmax))
                E("V", lambda q: q.reciprocal(out=gs[:, 0:3], in_=gmax[:, 0:3]),
                  reads=[R(gmax)], writes=[R(gs)])
                E("V", lambda q: q.tensor_scalar(out=gs[:, 0:3], in0=gs[:, 0:3],
                                                 scalar1=2688.0, scalar2=None, op0=ALU.mult),
                  reads=[R(gs)], writes=[R(gs)])
                sin_sb = ctx.sbuf("l1_sin", [1, 2], F32)
                E("qS", lambda q: q.dma_start(out=sin_sb[0:1, 0:1], in_=din["l1_s_in"].ap()),
                  writes=[R(sin_sb)])
                E("qS", lambda q: q.dma_start(out=sin_sb[0:1, 1:2],
                                              in_=din["l1_s_in_down"].ap()),
                  reads=[R(sin_sb)], writes=[R(sin_sb)])
                pbcast(gs[:, 3:5], R(gs), sin_sb[0:1, 0:2], [R(sin_sb)], 2)
                E("V", lambda q: q.reciprocal(out=rgs[:, 0:5], in_=gs[:, 0:5]),
                  reads=[R(gs)], writes=[R(rgs)])
                E("V", lambda q: q.tensor_mul(out=psc[:, 0:2], in0=rgs[:, 0:2],
                                              in1=rgs[:, 3:4].broadcast_to([128, 2])),
                  reads=[R(rgs)], writes=[R(psc)])
                E("V", lambda q: q.tensor_mul(out=psc[:, 2:3], in0=rgs[:, 2:3],
                                              in1=rgs[:, 4:5]),
                  reads=[R(rgs)], writes=[R(psc)])

                AgT = ctx.sbuf("l1_AgT", [128, 8 * 4096], BF16)
                AuT = ctx.sbuf("l1_AuT", [128, 8 * 4096], BF16)
                for wi, (nm, AT) in enumerate((("l1_gw", AgT), ("l1_uw", AuT))):
                    for it in range(32):
                        b = it % 2
                        E("qS", lambda q, nm=nm, it=it, b=b: q.dma_start(
                            out=wbuf[b][:], in_=din[nm].ap()[it * 128:(it + 1) * 128, :]),
                          writes=[R(wbuf[b])])
                        nv_quant(wbuf[b], R(wbuf[b]), 0, 1024, gs[:, wi:wi + 1],
                                 rgs[:, wi:wi + 1], [R(gs), R(rgs)])
                        E("V", lambda q, b=b: q.tensor_copy(out=abf[:], in_=wbuf[b][:]),
                          reads=[R(wbuf[b])], writes=[R(abf)])
                        for kc in range(8):
                            E("T", lambda q, kc=kc: q.transpose(
                                psQ[:, kc * 128:(kc + 1) * 128],
                                abf[:, kc * 128:(kc + 1) * 128], ident[:]),
                              reads=[R(abf), R(ident)], writes=[R(psQ)])
                        dst = AT[:].rearrange("p (c n) -> p c n", c=8)[:, :,
                                                                      it * 128:(it + 1) * 128]
                        E("V", lambda q, dst=dst: q.tensor_copy(
                            out=dst, in_=psQ[:].rearrange("p (c n) -> p c n", c=8)),
                          reads=[R(psQ)], writes=[R(AT)])

                xnf2 = ctx.sbuf("l1_xnf2", [128, 1024], F32)
                AxT = ctx.sbuf("l1_AxT", [128, 8 * 128], BF16)
                gbuf = ctx.sbuf("l1_gbuf", [128, 4096], F32)
                AhT = ctx.sbuf("l1_AhT", [128, 32 * 128], BF16)
                psM = [ctx.psum("l1_psM", [128, 512], F32) for _ in range(2)]
                for jt in range(NT):
                    E("qS", lambda q, jt=jt: q.dma_start(
                        out=xnf2[:], in_=xn_f.ap()[jt * 128:(jt + 1) * 128, :]),
                      reads=["xn_f"], writes=[R(xnf2)])
                    nv_quant(xnf2, R(xnf2), 0, 1024, gs[:, 3:4], rgs[:, 3:4],
                             [R(gs), R(rgs)])
                    E("V", lambda q: q.tensor_copy(out=abf[:], in_=xnf2[:]),
                      reads=[R(xnf2)], writes=[R(abf)])
                    for kc in range(8):
                        E("T", lambda q, kc=kc: q.transpose(
                            psQ[:, kc * 128:(kc + 1) * 128], abf[:, kc * 128:(kc + 1) * 128],
                            ident[:]),
                          reads=[R(abf), R(ident)], writes=[R(psQ)])
                    E("V", lambda q: q.tensor_copy(out=AxT[:], in_=psQ[:]),
                      reads=[R(psQ)], writes=[R(AxT)])
                    for nc_ in range(8):
                        pb = nc_ % 2
                        lhs = [AxT[:, kc * 128:(kc + 1) * 128] for kc in range(8)]
                        rhs = [AgT[:, kc * 4096 + nc_ * 512: kc * 4096 + (nc_ + 1) * 512]
                               for kc in range(8)]
                        mm_tile(psM[pb][:], R(psM[pb]), lhs, rhs, [R(AxT), R(AgT)])
                        E("S", lambda q, pb=pb, nc_=nc_: q.activation(
                            out=gbuf[:, nc_ * 512:(nc_ + 1) * 512], in_=psM[pb][:],
                            func=AF.Silu, scale=psc[:, 0:1]),
                          reads=[R(psM[pb]), R(psc)], writes=[R(gbuf)])
                    for nc_ in range(8):
                        pb = nc_ % 2
                        lhs = [AxT[:, kc * 128:(kc + 1) * 128] for kc in range(8)]
                        rhs = [AuT[:, kc * 4096 + nc_ * 512: kc * 4096 + (nc_ + 1) * 512]
                               for kc in range(8)]
                        mm_tile(psM[pb][:], R(psM[pb]), lhs, rhs, [R(AxT), R(AuT)])
                        E("V", lambda q, pb=pb, nc_=nc_: q.scalar_tensor_tensor(
                            out=gbuf[:, nc_ * 512:(nc_ + 1) * 512], in0=psM[pb][:],
                            scalar=psc[:, 1:2], in1=gbuf[:, nc_ * 512:(nc_ + 1) * 512],
                            op0=ALU.mult, op1=ALU.mult),
                          reads=[R(psM[pb]), R(psc), R(gbuf)], writes=[R(gbuf)])
                    for c4 in range(4):
                        nv_quant(gbuf, R(gbuf), c4 * 1024, 1024, gs[:, 4:5], rgs[:, 4:5],
                                 [R(gs), R(rgs)])
                        E("V", lambda q, c4=c4: q.tensor_copy(
                            out=abf[:], in_=gbuf[:, c4 * 1024:(c4 + 1) * 1024]),
                          reads=[R(gbuf)], writes=[R(abf)])
                        for kc in range(8):
                            E("T", lambda q, kc=kc: q.transpose(
                                psQ[:, kc * 128:(kc + 1) * 128],
                                abf[:, kc * 128:(kc + 1) * 128], ident[:]),
                              reads=[R(abf), R(ident)], writes=[R(psQ)])
                        E("V", lambda q, c4=c4: q.tensor_copy(
                            out=AhT[:, c4 * 1024:(c4 + 1) * 1024], in_=psQ[:]),
                          reads=[R(psQ)], writes=[R(AhT)])
                    E("qA", lambda q, jt=jt: q.dma_start(
                        out=hT_scr.ap().rearrange("(c p) t -> p c t", p=128)[:, :,
                                                                            jt * 128:(jt + 1) * 128],
                        in_=AhT[:].rearrange("p (c n) -> p c n", c=32)),
                      reads=[R(AhT)], writes=["l1_hT"])

            l1_dbg = dram("l1_dbg", [T, D])
            with ctx.scope():
                q_sc = ctx.sbuf("l1d_sc", [128, 64], F32)
                q_am = ctx.sbuf("l1d_am", [128, 64], F32)
                q_t1 = ctx.sbuf("l1d_t1", [128, 1024], F32)
                q_b = ctx.sbuf("l1d_b", [128, 1024], F32)
                q_c = ctx.sbuf("l1d_c", [128, 1024], F32)
                q_f8a = ctx.sbuf("l1d_f8a", [128, 64], F8E4)
                q_f8b = ctx.sbuf("l1d_f8b", [128, 64], F8E4)
                sgn = ctx.sbuf("l1d_sgn", [128, 1024], F32)
                abf = ctx.sbuf("l1d_abf", [128, 1024], BF16)
                psQ = ctx.psum("l1d_psQ", [128, 1024], BF16)

                def nv_quant(buf, bufnm, lo, tw, gs_col, rgs_col, gsnms):
                    nb = tw // 16
                    bap = buf[:, lo:lo + tw]
                    E("S", lambda q: q.activation(out=sgn[:, 0:tw], in_=bap, func=AF.Sign),
                      reads=[bufnm], writes=[R(sgn)])
                    E("V", lambda q: q.tensor_reduce(
                        out=q_am[:, 0:nb], in_=bap.rearrange("p (b k) -> p b k", k=16),
                        axis=AX.X, op=ALU.max, apply_absolute_value=True),
                      reads=[bufnm], writes=[R(q_am)])
                    E("V", lambda q: q.tensor_scalar(out=q_sc[:, 0:nb], in0=q_am[:, 0:nb],
                                                     scalar1=gs_col, scalar2=1.0 / 6.0,
                                                     op0=ALU.mult, op1=ALU.mult),
                      reads=[R(q_am)] + list(gsnms), writes=[R(q_sc)])
                    E("V", lambda q: q.tensor_scalar(out=q_sc[:, 0:nb], in0=q_sc[:, 0:nb],
                                                     scalar1=0.0, scalar2=448.0,
                                                     op0=ALU.max, op1=ALU.min),
                      reads=[R(q_sc)], writes=[R(q_sc)])
                    e4m3fn(q_sc, R(q_sc), q_t1, q_f8a, q_f8b, nb)
                    E("V", lambda q: q.tensor_scalar(out=q_am[:, 0:nb], in0=q_sc[:, 0:nb],
                                                     scalar1=rgs_col, scalar2=2.0 ** -40,
                                                     op0=ALU.mult, op1=ALU.max),
                      reads=[R(q_sc)] + list(gsnms), writes=[R(q_am)])
                    E("V", lambda q: q.reciprocal(out=q_am[:, 0:nb], in_=q_am[:, 0:nb]),
                      reads=[R(q_am)], writes=[R(q_am)])
                    E("S", lambda q: q.activation(out=bap, in_=bap, func=AF.Abs),
                      reads=[bufnm], writes=[bufnm])
                    E("V", lambda q: q.tensor_tensor(
                        out=bap.rearrange("p (b k) -> p b k", k=16),
                        in0=bap.rearrange("p (b k) -> p b k", k=16),
                        in1=q_am[:, 0:nb].rearrange("p (b o) -> p b o", o=1).broadcast_to([128, nb, 16]),
                        op=ALU.mult),
                      reads=[bufnm, R(q_am)], writes=[bufnm])
                    E("V", lambda q: q.tensor_scalar(out=bap, in0=bap, scalar1=6.0,
                                                     scalar2=None, op0=ALU.min),
                      reads=[bufnm], writes=[bufnm])
                    E("V", lambda q: q.tensor_scalar(out=q_b[:, 0:tw], in0=bap, scalar1=0.25,
                                                     scalar2=None, op0=ALU.is_ge),
                      reads=[bufnm], writes=[R(q_b)])
                    for mid in (0.75, 1.25, 1.75):
                        E("V", lambda q, mid=mid: q.scalar_tensor_tensor(
                            out=q_b[:, 0:tw], in0=bap, scalar=mid, in1=q_b[:, 0:tw],
                            op0=ALU.is_ge, op1=ALU.add),
                          reads=[bufnm, R(q_b)], writes=[R(q_b)])
                    E("V", lambda q: q.tensor_scalar(out=q_c[:, 0:tw], in0=bap, scalar1=2.5,
                                                     scalar2=None, op0=ALU.is_ge),
                      reads=[bufnm], writes=[R(q_c)])
                    E("V", lambda q: q.scalar_tensor_tensor(
                        out=q_c[:, 0:tw], in0=bap, scalar=3.5, in1=q_c[:, 0:tw],
                        op0=ALU.is_ge, op1=ALU.add),
                      reads=[bufnm, R(q_c)], writes=[R(q_c)])
                    E("V", lambda q: q.tensor_scalar(out=q_t1[:, 0:tw], in0=bap, scalar1=5.0,
                                                     scalar2=2.0, op0=ALU.is_ge, op1=ALU.mult),
                      reads=[bufnm], writes=[R(q_t1)])
                    E("V", lambda q: q.scalar_tensor_tensor(
                        out=bap, in0=q_b[:, 0:tw], scalar=0.5, in1=q_c[:, 0:tw],
                        op0=ALU.mult, op1=ALU.add),
                      reads=[R(q_b), R(q_c)], writes=[bufnm])
                    E("V", lambda q: q.tensor_add(out=bap, in0=bap, in1=q_t1[:, 0:tw]),
                      reads=[bufnm, R(q_t1)], writes=[bufnm])
                    E("V", lambda q: q.tensor_mul(out=bap, in0=bap, in1=sgn[:, 0:tw]),
                      reads=[bufnm, R(sgn)], writes=[bufnm])
                    E("V", lambda q: q.tensor_tensor(
                        out=bap.rearrange("p (b k) -> p b k", k=16),
                        in0=bap.rearrange("p (b k) -> p b k", k=16),
                        in1=q_sc[:, 0:nb].rearrange("p (b o) -> p b o", o=1).broadcast_to([128, nb, 16]),
                        op=ALU.mult),
                      reads=[bufnm, R(q_sc)], writes=[bufnm])

                gs2 = ctx.sbuf("l1d_gs", [128, 2], F32)
                rgs2 = ctx.sbuf("l1d_rgs", [128, 2], F32)
                psc2 = ctx.sbuf("l1d_psc", [128, 1], F32)
                wbuf = [ctx.sbuf("l1d_wbuf", [128, 1024], F32) for _ in range(2)]
                E("V", lambda q: q.memset(gs2[:], 0.0), writes=[R(gs2)])
                for it in range(8):
                    for c2 in range(4):
                        b = it % 2
                        E("qS", lambda q, it=it, b=b, c2=c2: q.dma_start(
                            out=wbuf[b][:],
                            in_=din["l1_dw"].ap()[it * 128:(it + 1) * 128,
                                                  c2 * 1024:(c2 + 1) * 1024]),
                          writes=[R(wbuf[b])])
                        E("V", lambda q, b=b: q.tensor_reduce(
                            out=q_sc[:, 0:1], in_=wbuf[b][:], axis=AX.X, op=ALU.max,
                            apply_absolute_value=True),
                          reads=[R(wbuf[b])], writes=[R(q_sc)])
                        E("V", lambda q: q.tensor_tensor(
                            out=gs2[:, 0:1], in0=gs2[:, 0:1], in1=q_sc[:, 0:1], op=ALU.max),
                          reads=[R(gs2), R(q_sc)], writes=[R(gs2)])
                pmax_col(gs2[:, 0:1], R(gs2))
                E("V", lambda q: q.reciprocal(out=gs2[:, 0:1], in_=gs2[:, 0:1]),
                  reads=[R(gs2)], writes=[R(gs2)])
                E("V", lambda q: q.tensor_scalar(out=gs2[:, 0:1], in0=gs2[:, 0:1],
                                                 scalar1=2688.0, scalar2=None, op0=ALU.mult),
                  reads=[R(gs2)], writes=[R(gs2)])
                sin2 = ctx.sbuf("l1d_sin", [1, 1], F32)
                E("qS", lambda q: q.dma_start(out=sin2[0:1, 0:1],
                                              in_=din["l1_s_in_down"].ap()),
                  writes=[R(sin2)])
                pbcast(gs2[:, 1:2], R(gs2), sin2[0:1, 0:1], [R(sin2)], 1)
                E("V", lambda q: q.reciprocal(out=rgs2[:], in_=gs2[:]),
                  reads=[R(gs2)], writes=[R(rgs2)])
                E("V", lambda q: q.tensor_mul(out=psc2[:, 0:1], in0=rgs2[:, 0:1],
                                              in1=rgs2[:, 1:2]),
                  reads=[R(rgs2)], writes=[R(psc2)])

                AdT = ctx.sbuf("l1_AdT", [128, 32 * 1024], BF16)
                for it in range(8):
                    for c2 in range(4):
                        b = (it * 4 + c2) % 2
                        E("qS", lambda q, it=it, b=b, c2=c2: q.dma_start(
                            out=wbuf[b][:],
                            in_=din["l1_dw"].ap()[it * 128:(it + 1) * 128,
                                                  c2 * 1024:(c2 + 1) * 1024]),
                          writes=[R(wbuf[b])])
                        nv_quant(wbuf[b], R(wbuf[b]), 0, 1024, gs2[:, 0:1], rgs2[:, 0:1],
                                 [R(gs2), R(rgs2)])
                        E("V", lambda q, b=b: q.tensor_copy(out=abf[:], in_=wbuf[b][:]),
                          reads=[R(wbuf[b])], writes=[R(abf)])
                        for kc in range(8):
                            E("T", lambda q, kc=kc: q.transpose(
                                psQ[:, kc * 128:(kc + 1) * 128],
                                abf[:, kc * 128:(kc + 1) * 128], ident[:]),
                              reads=[R(abf), R(ident)], writes=[R(psQ)])
                        dst = AdT[:].rearrange("p (c n) -> p c n", c=32)[
                            :, c2 * 8:(c2 + 1) * 8, it * 128:(it + 1) * 128]
                        E("V", lambda q, dst=dst: q.tensor_copy(
                            out=dst, in_=psQ[:].rearrange("p (c n) -> p c n", c=8)),
                          reads=[R(psQ)], writes=[R(AdT)])
                AhT2 = ctx.sbuf("l1_AhT2", [128, 32 * 128], BF16)
                fout = ctx.sbuf("l1_fout", [128, 1024], F32)
                stile2 = ctx.sbuf("l1_stile2", [128, 1024], F32)
                xtile2 = ctx.sbuf("l1_xtile2", [128, 1024], F32)
                psM = [ctx.psum("l1d_psM", [128, 512], F32) for _ in range(2)]
                for jt in range(NT):
                    E("qS", lambda q, jt=jt: q.dma_start(
                        out=AhT2[:].rearrange("p (c n) -> p c n", c=32),
                        in_=hT_scr.ap().rearrange("(c p) t -> p c t", p=128)[
                            :, :, jt * 128:(jt + 1) * 128]),
                      reads=["l1_hT"], writes=[R(AhT2)])
                    for nh in range(2):
                        lhs = [AhT2[:, kc * 128:(kc + 1) * 128] for kc in range(32)]
                        rhs = [AdT[:, kc * 1024 + nh * 512: kc * 1024 + (nh + 1) * 512]
                               for kc in range(32)]
                        mm_tile(psM[nh][:], R(psM[nh]), lhs, rhs, [R(AhT2), R(AdT)])
                        E("S", lambda q, nh=nh: q.activation(
                            out=fout[:, nh * 512:(nh + 1) * 512], in_=psM[nh][:],
                            func=AF.Identity, scale=psc2[:, 0:1]),
                          reads=[R(psM[nh]), R(psc2)], writes=[R(fout)])
                    E("qA", lambda q, jt=jt: q.dma_start(
                        out=l1_dbg.ap()[jt * 128:(jt + 1) * 128, :], in_=fout[:]),
                      reads=[R(fout)], writes=["l1_dbg"])
                    E("S", lambda q: q.activation(out=fout[:], in_=fout[:], func=AF.Silu),
                      reads=[R(fout)], writes=[R(fout)])
                    residual_epilogue(jt, fout, R(fout), dst_dram, dst_nm, stile2, xtile2,
                                      xsrc, xsrc_nm)


        def layer3(xsrc, xsrc_nm, dst_dram, dst_nm):
            NC = n_cores
            k_loc = dram("l3_kloc", [64, T], BF16)
            v_loc = dram("l3_vloc", [T, 64], BF16)
            k_ag = dram("l3_kag", [64 * NC, T], BF16)
            v_ag = dram("l3_vag", [NC * T, 64], BF16)
            s64 = ctx.sbuf("l3_s64", [128, NT * 64], F32)
            xnmax = ctx.sbuf("l3_xnmax", [128, 1], F32)
            E("V", lambda q: q.memset(xnmax[:], 0.0), writes=[R(xnmax)])

            with ctx.scope():
                xnT = ctx.sbuf("xnT3", [128, 8 * T], BF16)
                layernorm_phase(xnT, xsrc, xsrc_nm, write_xn_f=True,
                                absmax_col=xnmax[:, 0:1], absmax_nm=R(xnmax))
                onesc = ctx.sbuf("l3_onesc", [128, 1], BF16)
                E("V", lambda q: q.memset(onesc[:], 1.0), writes=[R(onesc)])
                wq3 = ctx.sbuf("l3_wq", [128, 8 * 64], BF16)
                wk3 = ctx.sbuf("l3_wk", [128, 8 * 64], BF16)
                wv3 = ctx.sbuf("l3_wv", [128, 8 * 64], BF16)
                for nm, sb in (("l3_WqT", wq3), ("l3_WkT", wk3), ("l3_WvT", wv3)):
                    E("qP", lambda q, nm=nm, sb=sb: q.dma_start(
                        out=sb[:].rearrange("p (c n) -> p c n", c=8),
                        in_=din[nm].ap().rearrange("(c p) n -> p c n", p=128)),
                      writes=[R(sb)])
                bq3 = ctx.sbuf("l3_bq", [64, 1], F32)
                bk3 = ctx.sbuf("l3_bk", [64, 1], F32)
                bvr = ctx.sbuf("l3_bvr", [1, 64], F32)
                bvb = ctx.sbuf("l3_bvb", [128, 64], F32)
                E("qS", lambda q: q.dma_start(out=bq3[:], in_=din["l3_bq"].ap()),
                  writes=[R(bq3)])
                E("qS", lambda q: q.dma_start(out=bk3[:], in_=din["l3_bk"].ap()),
                  writes=[R(bk3)])
                E("qS", lambda q: q.dma_start(out=bvr[:], in_=din["l3_bv"].ap()),
                  writes=[R(bvr)])
                pbcast(bvb[:], R(bvb), bvr[0:1, :], [R(bvr)], 64)

                qT3 = ctx.sbuf("l3_qT", [128, T], BF16)
                kT3 = ctx.sbuf("l3_kT", [64, T], BF16)
                vtm3 = ctx.sbuf("l3_vtm", [128, NT * 64], BF16)
                vf32 = ctx.sbuf("l3_vf32", [128, 64], F32)
                psA = [ctx.psum("l3_psA", [128, 512], F32) for _ in range(2)]
                nch = [(s, min(s + 512, T)) for s in range(0, T, 512)]
                for wi, (wsb, dstT, bcol) in enumerate(((wq3, qT3, bq3), (wk3, kT3, bk3))):
                    for nt, (lo, hi) in enumerate(nch):
                        pb = nt % 2
                        w_ = hi - lo
                        lhs = [wsb[:, kc * 64:(kc + 1) * 64] for kc in range(8)]
                        rhs = [xnT[:, kc * T + lo: kc * T + hi] for kc in range(8)]
                        mm_tile(psA[pb][0:64, 0:w_], R(psA[pb]), lhs, rhs, [R(wsb), R(xnT)])
                        E("S", lambda q, pb=pb, lo=lo, hi=hi, w_=w_, dstT=dstT, bcol=bcol:
                          q.activation(out=dstT[0:64, lo:hi], in_=psA[pb][0:64, 0:w_],
                                       func=AF.Identity, bias=bcol[:, 0:1], scale=1.0),
                          reads=[R(psA[pb]), R(bcol)], writes=[R(dstT)])
                        if dstT is qT3:
                            mm_tile(psA[pb][64:128, 0:w_], R(psA[pb]), lhs, rhs,
                                    [R(wsb), R(xnT)])
                            E("S", lambda q, pb=pb, lo=lo, hi=hi, w_=w_, bcol=bcol:
                              q.activation(out=qT3[64:128, lo:hi], in_=psA[pb][64:128, 0:w_],
                                           func=AF.Identity, bias=bcol[:, 0:1], scale=1.0),
                              reads=[R(psA[pb]), R(bcol)], writes=[R(qT3)])
                for jt in range(NT):
                    pb = jt % 2
                    lhs = [xnT[:, kc * T + jt * 128: kc * T + (jt + 1) * 128]
                           for kc in range(8)]
                    rhs = [wv3[:, kc * 64:(kc + 1) * 64] for kc in range(8)]
                    mm_tile(psA[pb][:, 0:64], R(psA[pb]), lhs, rhs, [R(xnT), R(wv3)])
                    E("V", lambda q, pb=pb: q.tensor_add(out=vf32[:], in0=psA[pb][:, 0:64],
                                                         in1=bvb[:]),
                      reads=[R(psA[pb]), R(bvb)], writes=[R(vf32)])
                    E("V", lambda q, jt=jt: q.tensor_copy(out=vtm3[:, jt * 64:(jt + 1) * 64],
                                                          in_=vf32[:]),
                      reads=[R(vf32)], writes=[R(vtm3)])
                gk3 = ctx.sbuf("l3_gk", [1, 2], F32)
                gtmp3 = ctx.sbuf("l3_gtmp", [1, 1], F32)
                sqb3 = ctx.sbuf("l3_sqb", [64, 512], BF16)
                psN3 = ctx.psum("l3_psN", [1, 512], F32)
                E("V", lambda q: q.memset(gk3[:], 0.0), writes=[R(gk3)])
                for gi, srcT in ((0, qT3), (1, kT3)):
                    for nt, (lo, hi) in enumerate(nch):
                        w_ = hi - lo
                        E("S", lambda q, srcT=srcT, lo=lo, hi=hi: q.activation(
                            out=sqb3[:, 0:hi - lo], in_=srcT[0:64, lo:hi], func=AF.Square),
                          reads=[R(srcT)], writes=[R(sqb3)])
                        E("T", lambda q, w_=w_: q.matmul(psN3[:, 0:w_], onesc[0:64, :],
                                                         sqb3[:, 0:w_], start=True, stop=True, skip_group_check=True),
                          reads=[R(onesc), R(sqb3)], writes=[R(psN3)])
                        E("V", lambda q, w_=w_: q.tensor_reduce(
                            out=gtmp3[0:1, 0:1], in_=psN3[0:1, 0:w_], axis=AX.X, op=ALU.max),
                          reads=[R(psN3)], writes=[R(gtmp3)])
                        E("V", lambda q, gi=gi: q.tensor_tensor(
                            out=gk3[0:1, gi:gi + 1], in0=gk3[0:1, gi:gi + 1],
                            in1=gtmp3[0:1, 0:1], op=ALU.max),
                          reads=[R(gtmp3), R(gk3)], writes=[R(gk3)])
                E("qP", lambda q: q.dma_start(out=k_loc.ap(), in_=kT3[:]),
                  reads=[R(kT3)], writes=["l3_kloc"])
                E("qP", lambda q: q.dma_start(
                    out=v_loc.ap().rearrange("(j p) d -> p j d", p=128),
                    in_=vtm3[:].rearrange("p (j d) -> p j d", d=64)),
                  reads=[R(vtm3)], writes=["l3_vloc"])
                E("P", lambda q: q.collective_compute(
                    "AllGather", ALU.bypass, replica_groups=[list(range(NC))],
                    ins=[k_loc.ap()], outs=[k_ag.ap()]),
                  reads=["l3_kloc"], writes=["l3_kag"])
                E("P", lambda q: q.collective_compute(
                    "AllGather", ALU.bypass, replica_groups=[list(range(NC))],
                    ins=[v_loc.ap()], outs=[v_ag.ap()]),
                  reads=["l3_vloc"], writes=["l3_vag"])
                gkb = dram("l3_gkb_i", [1, 1])
                gko = dram("l3_gkb_o", [1, 1])
                E("qP", lambda q: q.dma_start(out=gkb.ap(), in_=gk3[0:1, 1:2]),
                  reads=[R(gk3)], writes=["l3_gkb_i"])
                E("P", lambda q: q.collective_compute(
                    "AllReduce", ALU.max, replica_groups=[list(range(NC))],
                    ins=[gkb.ap()], outs=[gko.ap()]),
                  reads=["l3_gkb_i"], writes=["l3_gkb_o"])
                E("qP", lambda q: q.dma_start(out=gk3[0:1, 1:2], in_=gko.ap()),
                  reads=["l3_gkb_o"], writes=[R(gk3)])
                bnd3 = ctx.sbuf("l3_bnd", [128, 1], F32)
                E("V", lambda q: q.tensor_mul(out=gk3[0:1, 0:1], in0=gk3[0:1, 0:1],
                                              in1=gk3[0:1, 1:2]),
                  reads=[R(gk3)], writes=[R(gk3)])
                E("S", lambda q: q.activation(out=gk3[0:1, 0:1], in_=gk3[0:1, 0:1],
                                              func=AF.Sqrt, scale=1.0 / 1024.0),
                  reads=[R(gk3)], writes=[R(gk3)])
                E("V", lambda q: q.tensor_scalar(out=gk3[0:1, 0:1], in0=gk3[0:1, 0:1],
                                                 scalar1=-1.0, scalar2=None, op0=ALU.mult),
                  reads=[R(gk3)], writes=[R(gk3)])
                pbcast(bnd3[:, 0:1], R(bnd3), gk3[0:1, 0:1], [R(gk3)], 1)
                NPAIR = (NC + 1) // 2
                kall = ctx.sbuf("l3_kall", [128, NPAIR * T], BF16)
                vall = ctx.sbuf("l3_vall", [128, NC * NT * 64], BF16)
                for j in range(NC):
                    po = 64 * (j % 2)
                    E("qS", lambda q, j=j, po=po: q.dma_start(
                        out=kall[po:po + 64, (j // 2) * T:(j // 2 + 1) * T],
                        in_=k_ag.ap()[j * 64:(j + 1) * 64, :]),
                      reads=["l3_kag"], writes=[R(kall)])
                E("qS", lambda q: q.dma_start(
                    out=vall[:].rearrange("p (j d) -> p j d", d=64),
                    in_=v_ag.ap().rearrange("(j p) d -> p j d", p=128)),
                  reads=["l3_vag"], writes=[R(vall)])
                expb3 = ctx.sbuf("l3_expb", [128, 512], BF16)
                expT3 = ctx.sbuf("l3_expT", [128, 512], BF16)
                sums3 = ctx.sbuf("l3_sums", [128, NC * ((T + 511) // 512)], F32)
                rsum3 = ctx.sbuf("l3_rsum", [128, 1], F32)
                psT3 = ctx.psum("l3_psT", [128, 1024], BF16)
                psO3 = ctx.psum("l3_psO", [128, 512], F32)
                nkt_g = NC * (T // 128)
                for jt in range(NT):
                    ci = 0
                    for j in range(NC):
                        po = 64 * (j % 2)
                        for (klo, khi) in nch:
                            kw = khi - klo
                            E("T", lambda q, jt=jt, j=j, po=po, klo=klo, kw=kw: q.matmul(
                                psA[0][:, 0:kw],
                                qT3[po:po + 64, jt * 128:(jt + 1) * 128],
                                kall[po:po + 64, (j // 2) * T + klo:(j // 2) * T + klo + kw],
                                start=True, stop=True, skip_group_check=True),
                              reads=[R(qT3), R(kall)], writes=[R(psA[0])])
                            E("S", lambda q, ci=ci, kw=kw: q.activation(
                                out=expb3[:, 0:kw], in_=psA[0][:, 0:kw], func=AF.Exp,
                                bias=bnd3[:, 0:1], scale=1.0 / 32.0,
                                accum_out=sums3[:, ci:ci + 1]),
                              reads=[R(psA[0]), R(bnd3)], writes=[R(expb3), R(sums3)])
                            nsub = kw // 128
                            for i in range(nsub):
                                E("T", lambda q, i=i: q.transpose(
                                    psT3[:, i * 128:(i + 1) * 128],
                                    expb3[:, i * 128:(i + 1) * 128], ident[:]),
                                  reads=[R(expb3), R(ident)], writes=[R(psT3)])
                            E("V", lambda q, kw=kw: q.tensor_copy(out=expT3[:, 0:kw],
                                                                  in_=psT3[:, 0:kw]),
                              reads=[R(psT3)], writes=[R(expT3)])
                            for i in range(nsub):
                                kt = j * NT + klo // 128 + i
                                E("T", lambda q, i=i, kt=kt: q.matmul(
                                    psO3[:, 0:64], expT3[:, i * 128:(i + 1) * 128],
                                    vall[:, kt * 64:(kt + 1) * 64],
                                    start=(kt == 0), stop=(kt == nkt_g - 1), skip_group_check=True),
                                  reads=[R(expT3), R(vall)], writes=[R(psO3)])
                            ci += 1
                    E("V", lambda q, ci=ci: q.tensor_reduce(out=rsum3[:], in_=sums3[:, 0:ci],
                                                            axis=AX.X, op=ALU.add),
                      reads=[R(sums3)], writes=[R(rsum3)])
                    E("V", lambda q: q.reciprocal(out=rsum3[:], in_=rsum3[:]),
                      reads=[R(rsum3)], writes=[R(rsum3)])
                    E("S", lambda q, jt=jt: q.activation(
                        out=s64[:, jt * 64:(jt + 1) * 64], in_=psO3[:, 0:64],
                        func=AF.Identity, scale=rsum3[:, 0:1]),
                      reads=[R(psO3), R(rsum3)], writes=[R(s64)])

            h3_scr = dram("l3_h", [T, 2816])
            hmax = ctx.sbuf("l3_hmax", [128, 1], F32)
            E("V", lambda q: q.memset(hmax[:], 0.0), writes=[R(hmax)])

            def qdq_cols(buf, bufnm, tmp, rs_col, s_col, scnms, qmin, qmax, w):
                E("V", lambda q: q.tensor_scalar(out=buf[:, 0:w], in0=buf[:, 0:w],
                                                 scalar1=rs_col, scalar2=None, op0=ALU.mult),
                  reads=[bufnm] + list(scnms), writes=[bufnm])
                round_ste(buf, bufnm, buf[:, 0:w], [bufnm], tmp, w)
                E("V", lambda q: q.tensor_scalar(out=buf[:, 0:w], in0=buf[:, 0:w],
                                                 scalar1=float(qmin), scalar2=float(qmax),
                                                 op0=ALU.max, op1=ALU.min),
                  reads=[bufnm], writes=[bufnm])
                E("V", lambda q: q.tensor_scalar(out=buf[:, 0:w], in0=buf[:, 0:w],
                                                 scalar1=s_col, scalar2=None, op0=ALU.mult),
                  reads=[bufnm] + list(scnms), writes=[bufnm])

            with ctx.scope():
                pmax_col(xnmax[:, 0:1], R(xnmax))
                xb_i = dram("l3_xb_i", [1, 1])
                xb_o = dram("l3_xb_o", [1, 1])
                E("qP", lambda q: q.dma_start(out=xb_i.ap(), in_=xnmax[0:1, :]),
                  reads=[R(xnmax)], writes=["l3_xb_i"])
                E("P", lambda q: q.collective_compute(
                    "AllReduce", ALU.max, replica_groups=[list(range(NC))],
                    ins=[xb_i.ap()], outs=[xb_o.ap()]),
                  reads=["l3_xb_i"], writes=["l3_xb_o"])
                E("qP", lambda q: q.dma_start(out=xnmax[0:1, :], in_=xb_o.ap()),
                  reads=["l3_xb_o"], writes=[R(xnmax)])
                clips = ctx.sbuf("l3_clips", [1, 2], F32)
                E("qS", lambda q: q.dma_start(out=clips[0:1, 0:1], in_=din["l3_clip_x"].ap()),
                  writes=[R(clips)])
                E("qS", lambda q: q.dma_start(out=clips[0:1, 1:2],
                                              in_=din["l3_clip_down"].ap()),
                  reads=[R(clips)], writes=[R(clips)])
                E("S", lambda q: q.activation(out=clips[0:1, :], in_=clips[0:1, :],
                                              func=AF.Sigmoid),
                  reads=[R(clips)], writes=[R(clips)])
                sx = ctx.sbuf("l3_sx", [128, 2], F32)
                E("V", lambda q: q.tensor_scalar(out=clips[0:1, 0:1], in0=clips[0:1, 0:1],
                                                 scalar1=xnmax[0:1, 0:1], scalar2=1.0 / 127.0,
                                                 op0=ALU.mult, op1=ALU.mult),
                  reads=[R(clips), R(xnmax)], writes=[R(clips)])
                pbcast(sx[:, 0:1], R(sx), clips[0:1, 0:1], [R(clips)], 1)
                E("V", lambda q: q.reciprocal(out=sx[:, 1:2], in_=sx[:, 0:1]),
                  reads=[R(sx)], writes=[R(sx)])

                wgT = ctx.sbuf("l3_wgT", [128, 8 * 2816], BF16)
                wuT = ctx.sbuf("l3_wuT", [128, 8 * 2816], BF16)
                wbuf = [ctx.sbuf("l3_wbuf", [128, 1024], F32) for _ in range(2)]
                wrow = ctx.sbuf("l3_wrow", [128, 2], F32)
                tmpq = ctx.sbuf("l3_tmpq", [128, 2816], F32)
                abf3 = ctx.sbuf("l3_abf", [128, 1024], BF16)
                psQ3 = ctx.psum("l3_psQ", [128, 1024], BF16)
                for nm, WT in (("l3_gw", wgT), ("l3_uw", wuT)):
                    for it in range(22):
                        b = it % 2
                        E("qS", lambda q, nm=nm, it=it, b=b: q.dma_start(
                            out=wbuf[b][:], in_=din[nm].ap()[it * 128:(it + 1) * 128, :]),
                          writes=[R(wbuf[b])])
                        E("V", lambda q, b=b: q.tensor_reduce(
                            out=wrow[:, 1:2], in_=wbuf[b][:], axis=AX.X, op=ALU.max,
                            apply_absolute_value=True),
                          reads=[R(wbuf[b])], writes=[R(wrow)])
                        E("V", lambda q: q.tensor_scalar(out=wrow[:, 1:2], in0=wrow[:, 1:2],
                                                         scalar1=1.0 / 127.0, scalar2=None,
                                                         op0=ALU.mult),
                          reads=[R(wrow)], writes=[R(wrow)])
                        E("V", lambda q: q.reciprocal(out=wrow[:, 0:1], in_=wrow[:, 1:2]),
                          reads=[R(wrow)], writes=[R(wrow)])
                        qdq_cols(wbuf[b], R(wbuf[b]), tmpq, wrow[:, 0:1], wrow[:, 1:2],
                                 [R(wrow)], -128, 127, 1024)
                        E("V", lambda q, b=b: q.tensor_copy(out=abf3[:], in_=wbuf[b][:]),
                          reads=[R(wbuf[b])], writes=[R(abf3)])
                        for kc in range(8):
                            E("T", lambda q, kc=kc: q.transpose(
                                psQ3[:, kc * 128:(kc + 1) * 128],
                                abf3[:, kc * 128:(kc + 1) * 128], ident[:]),
                              reads=[R(abf3), R(ident)], writes=[R(psQ3)])
                        dst = WT[:].rearrange("p (c n) -> p c n", c=8)[:, :,
                                                                      it * 128:(it + 1) * 128]
                        E("V", lambda q, dst=dst: q.tensor_copy(
                            out=dst, in_=psQ3[:].rearrange("p (c n) -> p c n", c=8)),
                          reads=[R(psQ3)], writes=[R(WT)])

                xnf3 = ctx.sbuf("l3_xnf", [128, 1024], F32)
                xqT3 = ctx.sbuf("l3_xqT", [128, 8 * 128], BF16)
                gbuf3 = ctx.sbuf("l3_gbuf", [128, 2816], F32)
                mcol = ctx.sbuf("l3_mcol", [128, 1], F32)
                psM3 = [ctx.psum("l3_psM", [128, 512], F32) for _ in range(2)]
                nch28 = [(s, min(s + 512, 2816)) for s in range(0, 2816, 512)]
                for jt in range(NT):
                    E("qS", lambda q, jt=jt: q.dma_start(
                        out=xnf3[:], in_=xn_f.ap()[jt * 128:(jt + 1) * 128, :]),
                      reads=["xn_f"], writes=[R(xnf3)])
                    qdq_cols(xnf3, R(xnf3), tmpq, sx[:, 1:2], sx[:, 0:1], [R(sx)],
                             -128, 127, 1024)
                    E("V", lambda q: q.tensor_copy(out=abf3[:], in_=xnf3[:]),
                      reads=[R(xnf3)], writes=[R(abf3)])
                    for kc in range(8):
                        E("T", lambda q, kc=kc: q.transpose(
                            psQ3[:, kc * 128:(kc + 1) * 128],
                            abf3[:, kc * 128:(kc + 1) * 128], ident[:]),
                          reads=[R(abf3), R(ident)], writes=[R(psQ3)])
                    E("V", lambda q: q.tensor_copy(out=xqT3[:], in_=psQ3[:]),
                      reads=[R(psQ3)], writes=[R(xqT3)])
                    for nc_, (lo, hi) in enumerate(nch28):
                        pb = nc_ % 2
                        w_ = hi - lo
                        lhs = [xqT3[:, kc * 128:(kc + 1) * 128] for kc in range(8)]
                        rhs = [wgT[:, kc * 2816 + lo: kc * 2816 + hi] for kc in range(8)]
                        mm_tile(psM3[pb][:, 0:w_], R(psM3[pb]), lhs, rhs, [R(xqT3), R(wgT)])
                        E("S", lambda q, pb=pb, lo=lo, hi=hi, w_=w_: q.activation(
                            out=gbuf3[:, lo:hi], in_=psM3[pb][:, 0:w_], func=AF.Silu),
                          reads=[R(psM3[pb])], writes=[R(gbuf3)])
                    for nc_, (lo, hi) in enumerate(nch28):
                        pb = nc_ % 2
                        w_ = hi - lo
                        lhs = [xqT3[:, kc * 128:(kc + 1) * 128] for kc in range(8)]
                        rhs = [wuT[:, kc * 2816 + lo: kc * 2816 + hi] for kc in range(8)]
                        mm_tile(psM3[pb][:, 0:w_], R(psM3[pb]), lhs, rhs, [R(xqT3), R(wuT)])
                        E("V", lambda q, pb=pb, lo=lo, hi=hi, w_=w_: q.tensor_mul(
                            out=gbuf3[:, lo:hi], in0=gbuf3[:, lo:hi], in1=psM3[pb][:, 0:w_]),
                          reads=[R(psM3[pb]), R(gbuf3)], writes=[R(gbuf3)])
                    E("V", lambda q: q.tensor_reduce(out=mcol[:], in_=gbuf3[:], axis=AX.X,
                                                     op=ALU.max, apply_absolute_value=True),
                      reads=[R(gbuf3)], writes=[R(mcol)])
                    E("V", lambda q: q.tensor_tensor(out=hmax[:], in0=hmax[:], in1=mcol[:],
                                                     op=ALU.max),
                      reads=[R(mcol), R(hmax)], writes=[R(hmax)])
                    E("qA", lambda q, jt=jt: q.dma_start(
                        out=h3_scr.ap()[jt * 128:(jt + 1) * 128, :], in_=gbuf3[:]),
                      reads=[R(gbuf3)], writes=["l3_h"])

            with ctx.scope():
                pmax_col(hmax[:, 0:1], R(hmax))
                hb_i = dram("l3_hb_i", [1, 1])
                hb_o = dram("l3_hb_o", [1, 1])
                E("qP", lambda q: q.dma_start(out=hb_i.ap(), in_=hmax[0:1, :]),
                  reads=[R(hmax)], writes=["l3_hb_i"])
                E("P", lambda q: q.collective_compute(
                    "AllReduce", ALU.max, replica_groups=[list(range(NC))],
                    ins=[hb_i.ap()], outs=[hb_o.ap()]),
                  reads=["l3_hb_i"], writes=["l3_hb_o"])
                E("qP", lambda q: q.dma_start(out=hmax[0:1, :], in_=hb_o.ap()),
                  reads=["l3_hb_o"], writes=[R(hmax)])
                clips2 = ctx.sbuf("l3_clips2", [1, 1], F32)
                E("qS", lambda q: q.dma_start(out=clips2[0:1, 0:1],
                                              in_=din["l3_clip_down"].ap()),
                  writes=[R(clips2)])
                E("S", lambda q: q.activation(out=clips2[0:1, :], in_=clips2[0:1, :],
                                              func=AF.Sigmoid),
                  reads=[R(clips2)], writes=[R(clips2)])
                sh3 = ctx.sbuf("l3_sh", [128, 2], F32)
                E("V", lambda q: q.tensor_scalar(out=clips2[0:1, 0:1], in0=clips2[0:1, 0:1],
                                                 scalar1=hmax[0:1, 0:1], scalar2=1.0 / 127.0,
                                                 op0=ALU.mult, op1=ALU.mult),
                  reads=[R(clips2), R(hmax)], writes=[R(clips2)])
                pbcast(sh3[:, 0:1], R(sh3), clips2[0:1, 0:1], [R(clips2)], 1)
                E("V", lambda q: q.reciprocal(out=sh3[:, 1:2], in_=sh3[:, 0:1]),
                  reads=[R(sh3)], writes=[R(sh3)])
                dwT = ctx.sbuf("l3_dwT", [128, 22 * 1024], BF16)
                wbuf2 = [ctx.sbuf("l3_wbuf2", [128, 2816], F32) for _ in range(2)]
                wrow2 = ctx.sbuf("l3_wrow2", [128, 2], F32)
                tmpq2 = ctx.sbuf("l3_tmpq2", [128, 2816], F32)
                abf4 = ctx.sbuf("l3_abf4", [128, 2816], BF16)
                psQ4 = ctx.psum("l3_psQ4", [128, 1024], BF16)
                for it in range(8):
                    b = it % 2
                    E("qS", lambda q, it=it, b=b: q.dma_start(
                        out=wbuf2[b][:], in_=din["l3_dw"].ap()[it * 128:(it + 1) * 128, :]),
                      writes=[R(wbuf2[b])])
                    E("V", lambda q, b=b: q.tensor_reduce(
                        out=wrow2[:, 1:2], in_=wbuf2[b][:], axis=AX.X, op=ALU.max,
                        apply_absolute_value=True),
                      reads=[R(wbuf2[b])], writes=[R(wrow2)])
                    E("V", lambda q: q.tensor_scalar(out=wrow2[:, 1:2], in0=wrow2[:, 1:2],
                                                     scalar1=1.0 / 127.0, scalar2=None,
                                                     op0=ALU.mult),
                      reads=[R(wrow2)], writes=[R(wrow2)])
                    E("V", lambda q: q.reciprocal(out=wrow2[:, 0:1], in_=wrow2[:, 1:2]),
                      reads=[R(wrow2)], writes=[R(wrow2)])
                    qdq_cols(wbuf2[b], R(wbuf2[b]), tmpq2, wrow2[:, 0:1], wrow2[:, 1:2],
                             [R(wrow2)], -128, 127, 2816)
                    E("V", lambda q, b=b: q.tensor_copy(out=abf4[:], in_=wbuf2[b][:]),
                      reads=[R(wbuf2[b])], writes=[R(abf4)])
                    for kc in range(22):
                        E("T", lambda q, kc=kc, it=it: q.transpose(
                            psQ4[:, (kc % 8) * 128:(kc % 8 + 1) * 128],
                            abf4[:, kc * 128:(kc + 1) * 128], ident[:]),
                          reads=[R(abf4), R(ident)], writes=[R(psQ4)])
                        E("V", lambda q, kc=kc, it=it: q.tensor_copy(
                            out=dwT[:, kc * 1024 + it * 128: kc * 1024 + (it + 1) * 128],
                            in_=psQ4[:, (kc % 8) * 128:(kc % 8 + 1) * 128]),
                          reads=[R(psQ4)], writes=[R(dwT)])
                hbufB = ctx.sbuf("l3_hbufB", [128, 2816], F32)
                hT3 = ctx.sbuf("l3_hT3", [128, 22 * 128], BF16)
                fout3 = ctx.sbuf("l3_fout", [128, 1024], F32)
                sg3 = ctx.sbuf("l3_sg", [128, 1024], F32)
                xtile3 = ctx.sbuf("l3_xtile", [128, 1024], F32)
                psM4 = [ctx.psum("l3_psM4", [128, 512], F32) for _ in range(2)]
                for jt in range(NT):
                    E("qS", lambda q, jt=jt: q.dma_start(
                        out=hbufB[:], in_=h3_scr.ap()[jt * 128:(jt + 1) * 128, :]),
                      reads=["l3_h"], writes=[R(hbufB)])
                    qdq_cols(hbufB, R(hbufB), tmpq2, sh3[:, 1:2], sh3[:, 0:1], [R(sh3)],
                             -128, 127, 2816)
                    E("V", lambda q: q.tensor_copy(out=abf4[:], in_=hbufB[:]),
                      reads=[R(hbufB)], writes=[R(abf4)])
                    for kc in range(22):
                        E("T", lambda q, kc=kc: q.transpose(
                            psQ4[:, (kc % 8) * 128:(kc % 8 + 1) * 128],
                            abf4[:, kc * 128:(kc + 1) * 128], ident[:]),
                          reads=[R(abf4), R(ident)], writes=[R(psQ4)])
                        E("V", lambda q, kc=kc: q.tensor_copy(
                            out=hT3[:, kc * 128:(kc + 1) * 128],
                            in_=psQ4[:, (kc % 8) * 128:(kc % 8 + 1) * 128]),
                          reads=[R(psQ4)], writes=[R(hT3)])
                    for nh in range(2):
                        lhs = [hT3[:, kc * 128:(kc + 1) * 128] for kc in range(22)]
                        rhs = [dwT[:, kc * 1024 + nh * 512: kc * 1024 + (nh + 1) * 512]
                               for kc in range(22)]
                        mm_tile(psM4[nh][:], R(psM4[nh]), lhs, rhs, [R(hT3), R(dwT)])
                        E("S", lambda q, nh=nh: q.activation(
                            out=fout3[:, nh * 512:(nh + 1) * 512], in_=psM4[nh][:],
                            func=AF.Silu),
                          reads=[R(psM4[nh])], writes=[R(fout3)])
                    E("V", lambda q, jt=jt: q.tensor_tensor(
                        out=sg3[:].rearrange("p (r c) -> p r c", c=64),
                        in0=fout3[:].rearrange("p (r c) -> p r c", c=64),
                        in1=s64[:, jt * 64:(jt + 1) * 64].rearrange(
                            "p (o c) -> p o c", o=1).broadcast_to([128, 16, 64]),
                        op=ALU.mult),
                      reads=[R(fout3), R(s64)], writes=[R(sg3)])
                    E("qS", lambda q, jt=jt: q.dma_start(
                        out=xtile3[:], in_=xsrc.ap()[jt * 128:(jt + 1) * 128, :]),
                      reads=[xsrc_nm], writes=[R(xtile3)])
                    E("V", lambda q: q.tensor_add(out=sg3[:], in0=sg3[:], in1=xtile3[:]),
                      reads=[R(sg3), R(xtile3)], writes=[R(sg3)])
                    E("qA", lambda q, jt=jt: q.dma_start(
                        out=dst_dram.ap()[jt * 128:(jt + 1) * 128, :], in_=sg3[:]),
                      reads=[R(sg3)], writes=[dst_nm])


        def layer2(xsrc, xsrc_nm, dst_dram, dst_nm):
            NC = n_cores
            oT2_scr = dram("l2_oT", [T, 1024], BF16)
            xnmax2 = ctx.sbuf("l2_xnmax", [128, 1], F32)
            E("V", lambda q: q.memset(xnmax2[:], 0.0), writes=[R(xnmax2)])

            with ctx.scope():
                xnT = ctx.sbuf("xnT2", [128, 8 * T], BF16)
                layernorm_phase(xnT, xsrc, xsrc_nm, write_xn_f=True,
                                absmax_col=xnmax2[:, 0:1], absmax_nm=R(xnmax2))
                onesc = ctx.sbuf("l2_onesc", [128, 1], BF16)
                E("V", lambda q: q.memset(onesc[:], 1.0), writes=[R(onesc)])
                wq2 = ctx.sbuf("l2_wq", [128, 8 * 1024], BF16)
                wqr2 = ctx.sbuf("l2_wqr", [128, 8 * 512], BF16)
                wkr2 = ctx.sbuf("l2_wkr", [128, 8 * 32], BF16)
                wkvd2 = ctx.sbuf("l2_wkvd", [128, 8 * 256], BF16)
                wku2 = ctx.sbuf("l2_wku", [128, 2 * 1024], BF16)
                wvu2 = ctx.sbuf("l2_wvu", [128, 2 * 1024], BF16)
                for nm, sb, c_ in (("l2_Wq", wq2, 8), ("l2_Wqr", wqr2, 8), ("l2_Wkr", wkr2, 8),
                                   ("l2_Wkvd", wkvd2, 8), ("l2_Wku", wku2, 2),
                                   ("l2_Wvu", wvu2, 2)):
                    E("qP", lambda q, nm=nm, sb=sb, c_=c_: q.dma_start(
                        out=sb[:].rearrange("p (c n) -> p c n", c=c_),
                        in_=din[nm].ap().rearrange("(c p) n -> p c n", p=128)),
                      writes=[R(sb)])
                ropeC = ctx.sbuf("l2_ropeC", [128, NT * 16], F32)
                ropeS = ctx.sbuf("l2_ropeS", [128, NT * 16], F32)
                E("qS", lambda q: q.dma_start(
                    out=ropeC[:].rearrange("p (j i) -> p j i", i=16),
                    in_=din["ropeC"].ap().rearrange("(j p) i -> p j i", p=128)),
                  writes=[R(ropeC)])
                E("qS", lambda q: q.dma_start(
                    out=ropeS[:].rearrange("p (j i) -> p j i", i=16),
                    in_=din["ropeS"].ap().rearrange("(j p) i -> p j i", p=128)),
                  writes=[R(ropeS)])
                maskB = ctx.sbuf("l2_maskB", [128, 128], F32)
                E("qS", lambda q: q.dma_start(out=maskB[:], in_=din["causmaskB"].ap()),
                  writes=[R(maskB)])
                lobo = ctx.sbuf("l2_lobo", [1, 16], F32)
                E("qS", lambda q: q.dma_start(out=lobo[:], in_=din["l2_lobo"].ap()),
                  writes=[R(lobo)])
                E("S", lambda q: q.activation(out=lobo[:], in_=lobo[:], func=AF.Exp),
                  reads=[R(lobo)], writes=[R(lobo)])
                chc = ctx.sbuf("l2_chc", [128, 16], F32)
                pbcast(chc[:, 0:16], R(chc), lobo[0:1, 0:16], [R(lobo)], 16)

                latT2 = ctx.sbuf("l2_latT", [128, 2 * T], BF16)
                psA = [ctx.psum("l2_psA", [128, 512], F32) for _ in range(2)]
                nch = [(s, min(s + 512, T)) for s in range(0, T, 512)]
                for m in range(2):
                    for nt, (lo, hi) in enumerate(nch):
                        pb = nt % 2
                        w_ = hi - lo
                        lhs = [wkvd2[:, kc * 256 + m * 128: kc * 256 + (m + 1) * 128]
                               for kc in range(8)]
                        rhs = [xnT[:, kc * T + lo: kc * T + hi] for kc in range(8)]
                        mm_tile(psA[pb][:, 0:w_], R(psA[pb]), lhs, rhs, [R(wkvd2), R(xnT)])
                        E("V", lambda q, pb=pb, m=m, lo=lo, hi=hi, w_=w_: q.tensor_copy(
                            out=latT2[:, m * T + lo: m * T + hi], in_=psA[pb][:, 0:w_]),
                          reads=[R(psA[pb])], writes=[R(latT2)])

                qr_fmh = ctx.sbuf("l2_qrfmh", [32, T], BF16)
                kr_fm = ctx.sbuf("l2_krfm", [32, T], BF16)
                qrt = ctx.sbuf("l2_qrt", [128, 512], F32)
                qrt2 = ctx.sbuf("l2_qrt2", [128, 512], F32)
                qrt3 = ctx.sbuf("l2_qrt3", [128, 512], F32)
                krt = ctx.sbuf("l2_krt", [128, 32], F32)
                krt2 = ctx.sbuf("l2_krt2", [128, 32], F32)
                rbf = ctx.sbuf("l2_rbf", [128, 512], BF16)
                psR = ctx.psum("l2_psR", [128, 512], F32)
                psT2 = ctx.psum("l2_psT2", [128, 1024], BF16)

                def rope_tm(dst, dstnm, srcv, srcnm, jt, nh):
                    cosb = ropeC[:, jt * 16:(jt + 1) * 16].rearrange("p (o i) -> p o i", o=1)
                    sinb = ropeS[:, jt * 16:(jt + 1) * 16].rearrange("p (o i) -> p o i", o=1)
                    s3 = srcv.rearrange("p (h z) -> p h z", z=32)
                    d3 = dst[:, 0:nh * 32].rearrange("p (h z) -> p h z", z=32)
                    E("V", lambda q: q.tensor_tensor(out=d3[:, :, 0:16], in0=s3[:, :, 0:16],
                                                     in1=cosb.broadcast_to([128, nh, 16]),
                                                     op=ALU.mult),
                      reads=[srcnm, R(ropeC)], writes=[dstnm])
                    E("V", lambda q: q.tensor_tensor(out=d3[:, :, 16:32], in0=s3[:, :, 0:16],
                                                     in1=sinb.broadcast_to([128, nh, 16]),
                                                     op=ALU.mult),
                      reads=[srcnm, R(ropeS)], writes=[dstnm])
                    E("V", lambda q: q.tensor_tensor(out=qrt2[:, 0:nh * 32].rearrange(
                        "p (h z) -> p h z", z=32)[:, :, 0:16], in0=s3[:, :, 16:32],
                        in1=sinb.broadcast_to([128, nh, 16]), op=ALU.mult),
                      reads=[srcnm, R(ropeS)], writes=[R(qrt2)])
                    E("V", lambda q: q.tensor_tensor(out=qrt2[:, 0:nh * 32].rearrange(
                        "p (h z) -> p h z", z=32)[:, :, 16:32], in0=s3[:, :, 16:32],
                        in1=cosb.broadcast_to([128, nh, 16]), op=ALU.mult),
                      reads=[srcnm, R(ropeC)], writes=[R(qrt2)])
                    E("V", lambda q: q.tensor_sub(
                        out=d3[:, :, 0:16], in0=d3[:, :, 0:16],
                        in1=qrt2[:, 0:nh * 32].rearrange("p (h z) -> p h z", z=32)[:, :, 0:16]),
                      reads=[dstnm, R(qrt2)], writes=[dstnm])
                    E("V", lambda q: q.tensor_add(
                        out=d3[:, :, 16:32], in0=d3[:, :, 16:32],
                        in1=qrt2[:, 0:nh * 32].rearrange("p (h z) -> p h z", z=32)[:, :, 16:32]),
                      reads=[dstnm, R(qrt2)], writes=[dstnm])

                for jt in range(NT):
                    lhs = [xnT[:, kc * T + jt * 128: kc * T + (jt + 1) * 128] for kc in range(8)]
                    rhs = [wkr2[:, kc * 32:(kc + 1) * 32] for kc in range(8)]
                    mm_tile(psR[:, 0:32], R(psR), lhs, rhs, [R(xnT), R(wkr2)])
                    E("V", lambda q: q.tensor_copy(out=krt[:], in_=psR[:, 0:32]),
                      reads=[R(psR)], writes=[R(krt)])
                    rope_tm(krt2, R(krt2), krt[:], R(krt), jt, 1)
                    E("V", lambda q: q.tensor_copy(out=rbf[:, 0:32], in_=krt2[:]),
                      reads=[R(krt2)], writes=[R(rbf)])
                    E("T", lambda q: q.transpose(psT2[0:32, 0:128], rbf[:, 0:32], ident[:]),
                      reads=[R(rbf), R(ident)], writes=[R(psT2)])
                    E("V", lambda q, jt=jt: q.tensor_copy(
                        out=kr_fm[0:32, jt * 128:(jt + 1) * 128], in_=psT2[0:32, 0:128]),
                      reads=[R(psT2)], writes=[R(kr_fm)])

                qcb = ctx.sbuf("l2_qcb", [64, T], BF16)
                kcb = ctx.sbuf("l2_kcb", [64, T], BF16)
                vtm2 = ctx.sbuf("l2_vtm", [128, NT * 64], BF16)
                scbuf = ctx.sbuf("l2_scbuf", [128, T], F32)
                mcol2 = ctx.sbuf("l2_mcol2", [128, 1], F32)
                expb2 = ctx.sbuf("l2_expb", [128, 512], BF16)
                expT2 = ctx.sbuf("l2_expT", [128, 512], BF16)
                sums2 = ctx.sbuf("l2_sums", [128, (T + 511) // 512], F32)
                rsum2 = ctx.sbuf("l2_rsum", [128, 1], F32)
                otile2 = ctx.sbuf("l2_otile", [128, 64], BF16)
                psE = ctx.psum("l2_psE", [128, 512], F32)
                psTx = ctx.psum("l2_psTx", [128, 1024], BF16)
                psO2 = ctx.psum("l2_psO", [128, 512], F32)
                for h in range(16):
                    for jt in range(NT):
                        lhs = [xnT[:, kc * T + jt * 128: kc * T + (jt + 1) * 128]
                               for kc in range(8)]
                        rhs = [wqr2[:, kc * 512 + h * 32: kc * 512 + (h + 1) * 32]
                               for kc in range(8)]
                        mm_tile(psR[:, 0:32], R(psR), lhs, rhs, [R(xnT), R(wqr2)])
                        E("V", lambda q: q.tensor_copy(out=krt[:], in_=psR[:, 0:32]),
                          reads=[R(psR)], writes=[R(krt)])
                        rope_tm(krt2, R(krt2), krt[:], R(krt), jt, 1)
                        E("V", lambda q: q.tensor_copy(out=rbf[:, 0:32], in_=krt2[:]),
                          reads=[R(krt2)], writes=[R(rbf)])
                        E("T", lambda q: q.transpose(psT2[0:32, 0:128], rbf[:, 0:32],
                                                     ident[:]),
                          reads=[R(rbf), R(ident)], writes=[R(psT2)])
                        E("V", lambda q, jt=jt: q.tensor_copy(
                            out=qr_fmh[0:32, jt * 128:(jt + 1) * 128], in_=psT2[0:32, 0:128]),
                          reads=[R(psT2)], writes=[R(qr_fmh)])
                    for nt, (lo, hi) in enumerate(nch):
                        pb = nt % 2
                        w_ = hi - lo
                        lhs = [wq2[:, kc * 1024 + h * 64: kc * 1024 + (h + 1) * 64]
                               for kc in range(8)]
                        rhs = [xnT[:, kc * T + lo: kc * T + hi] for kc in range(8)]
                        mm_tile(psA[pb][0:64, 0:w_], R(psA[pb]), lhs, rhs, [R(wq2), R(xnT)])
                        E("V", lambda q, pb=pb, lo=lo, hi=hi, w_=w_: q.tensor_copy(
                            out=qcb[:, lo:hi], in_=psA[pb][0:64, 0:w_]),
                          reads=[R(psA[pb])], writes=[R(qcb)])
                        lhs = [wku2[:, kc * 1024 + h * 64: kc * 1024 + (h + 1) * 64]
                               for kc in range(2)]
                        rhs = [latT2[:, kc * T + lo: kc * T + hi] for kc in range(2)]
                        mm_tile(psA[1 - pb][0:64, 0:w_], R(psA[1 - pb]), lhs, rhs,
                                [R(wku2), R(latT2)])
                        E("S", lambda q, pb=pb, lo=lo, hi=hi, w_=w_: q.activation(
                            out=kcb[:, lo:hi], in_=psA[1 - pb][0:64, 0:w_], func=AF.Copy),
                          reads=[R(psA[1 - pb])], writes=[R(kcb)])
                    for jt in range(NT):
                        pb = jt % 2
                        lhs = [latT2[:, kc * T + jt * 128: kc * T + (jt + 1) * 128]
                               for kc in range(2)]
                        rhs = [wvu2[:, kc * 1024 + h * 64: kc * 1024 + (h + 1) * 64]
                               for kc in range(2)]
                        mm_tile(psA[pb][:, 0:64], R(psA[pb]), lhs, rhs, [R(latT2), R(wvu2)])
                        E("V", lambda q, pb=pb, jt=jt: q.tensor_copy(
                            out=vtm2[:, jt * 64:(jt + 1) * 64], in_=psA[pb][:, 0:64]),
                          reads=[R(psA[pb])], writes=[R(vtm2)])
                    invsq96 = 1.0 / (96.0 ** 0.5)
                    for jt in range(NT):
                        kend = (jt + 1) * 128
                        chunks2 = [(s, min(s + 512, kend)) for s in range(0, kend, 512)]
                        nkt_end = kend // 128
                        for ci, (klo, khi) in enumerate(chunks2):
                            kw = khi - klo
                            mm_tile(psE[:, 0:kw], R(psE),
                                    [qcb[:, jt * 128:(jt + 1) * 128]],
                                    [kcb[:, klo:khi]], [R(qcb), R(kcb)], last=False)
                            E("T", lambda q, jt=jt, klo=klo, kw=kw: q.matmul(
                                psE[:, 0:kw],
                                qr_fmh[0:32, jt * 128:(jt + 1) * 128],
                                kr_fm[0:32, klo:klo + kw], start=False, stop=True, skip_group_check=True),
                              reads=[R(qr_fmh), R(kr_fm)], writes=[R(psE)])
                            E("S", lambda q, klo=klo, khi=khi, kw=kw: q.activation(
                                out=scbuf[:, klo:khi], in_=psE[:, 0:kw], func=AF.Identity,
                                scale=invsq96),
                              reads=[R(psE)], writes=[R(scbuf)])
                        E("V", lambda q, kend=kend: q.tensor_sub(
                            out=scbuf[:, kend - 128:kend], in0=scbuf[:, kend - 128:kend],
                            in1=maskB[:]),
                          reads=[R(scbuf), R(maskB)], writes=[R(scbuf)])
                        E("V", lambda q, kend=kend: q.tensor_reduce(
                            out=mcol2[:], in_=scbuf[:, 0:kend], axis=AX.X, op=ALU.max),
                          reads=[R(scbuf)], writes=[R(mcol2)])
                        E("V", lambda q: q.tensor_scalar(out=mcol2[:], in0=mcol2[:],
                                                         scalar1=-1.0, scalar2=None,
                                                         op0=ALU.mult),
                          reads=[R(mcol2)], writes=[R(mcol2)])
                        for ci, (klo, khi) in enumerate(chunks2):
                            kw = khi - klo
                            E("S", lambda q, ci=ci, klo=klo, khi=khi, kw=kw: q.activation(
                                out=expb2[:, 0:kw], in_=scbuf[:, klo:khi], func=AF.Exp,
                                bias=mcol2[:, 0:1], scale=1.0,
                                accum_out=sums2[:, ci:ci + 1]),
                              reads=[R(scbuf), R(mcol2)], writes=[R(expb2), R(sums2)])
                            nsub = kw // 128
                            for i in range(nsub):
                                E("T", lambda q, i=i: q.transpose(
                                    psTx[:, i * 128:(i + 1) * 128],
                                    expb2[:, i * 128:(i + 1) * 128], ident[:]),
                                  reads=[R(expb2), R(ident)], writes=[R(psTx)])
                            E("V", lambda q, kw=kw: q.tensor_copy(out=expT2[:, 0:kw],
                                                                  in_=psTx[:, 0:kw]),
                              reads=[R(psTx)], writes=[R(expT2)])
                            for i in range(nsub):
                                kt = klo // 128 + i
                                E("T", lambda q, i=i, kt=kt, nkt_end=nkt_end: q.matmul(
                                    psO2[:, 0:64], expT2[:, i * 128:(i + 1) * 128],
                                    vtm2[:, kt * 64:(kt + 1) * 64],
                                    start=(kt == 0), stop=(kt == nkt_end - 1), skip_group_check=True),
                                  reads=[R(expT2), R(vtm2)], writes=[R(psO2)])
                        E("V", lambda q, n2=len(chunks2): q.tensor_reduce(
                            out=rsum2[:], in_=sums2[:, 0:n2], axis=AX.X, op=ALU.add),
                          reads=[R(sums2)], writes=[R(rsum2)])
                        E("V", lambda q, h=h: q.tensor_scalar(
                            out=rsum2[:], in0=rsum2[:], scalar1=chc[:, h:h + 1],
                            scalar2=None, op0=ALU.add),
                          reads=[R(rsum2), R(chc)], writes=[R(rsum2)])
                        E("V", lambda q: q.reciprocal(out=rsum2[:], in_=rsum2[:]),
                          reads=[R(rsum2)], writes=[R(rsum2)])
                        E("S", lambda q: q.activation(out=otile2[:], in_=psO2[:, 0:64],
                                                      func=AF.Identity, scale=rsum2[:, 0:1]),
                          reads=[R(psO2), R(rsum2)], writes=[R(otile2)])
                        E("qA", lambda q, jt=jt, h=h: q.dma_start(
                            out=oT2_scr.ap()[jt * 128:(jt + 1) * 128, h * 64:(h + 1) * 64],
                            in_=otile2[:]),
                          reads=[R(otile2)], writes=["l2_oT"])

            with ctx.scope():
                wo2 = ctx.sbuf("l2_wo", [128, 8 * 1024], BF16)
                E("qP", lambda q: q.dma_start(
                    out=wo2[:].rearrange("p (c n) -> p c n", c=8),
                    in_=din["l2_Wo"].ap().rearrange("(c p) n -> p c n", p=128)),
                  writes=[R(wo2)])
                oin = [ctx.sbuf("l2_oin", [128, 1024], BF16) for _ in range(2)]
                otT = ctx.sbuf("l2_otT", [128, 8 * 128], BF16)
                sb = ctx.sbuf("l2_sb", [128, 1024], F32)
                psT3b = ctx.psum("l2_psT3", [128, 1024], BF16)
                psS = [ctx.psum("l2_psS", [128, 512], F32) for _ in range(2)]
                for jt in range(NT):
                    b = jt % 2
                    E("qS", lambda q, jt=jt, b=b: q.dma_start(
                        out=oin[b][:], in_=oT2_scr.ap()[jt * 128:(jt + 1) * 128, :]),
                      reads=["l2_oT"], writes=[R(oin[b])])
                    for i in range(8):
                        E("T", lambda q, b=b, i=i: q.transpose(
                            psT3b[:, i * 128:(i + 1) * 128],
                            oin[b][:, i * 128:(i + 1) * 128], ident[:]),
                          reads=[R(oin[b]), R(ident)], writes=[R(psT3b)])
                    E("V", lambda q: q.tensor_copy(out=otT[:], in_=psT3b[:]),
                      reads=[R(psT3b)], writes=[R(otT)])
                    for nh in range(2):
                        lhs = [otT[:, kc * 128:(kc + 1) * 128] for kc in range(8)]
                        rhs = [wo2[:, kc * 1024 + nh * 512: kc * 1024 + (nh + 1) * 512]
                               for kc in range(8)]
                        mm_tile(psS[nh][:], R(psS[nh]), lhs, rhs, [R(otT), R(wo2)])
                        E("S", lambda q, nh=nh: q.activation(
                            out=sb[:, nh * 512:(nh + 1) * 512], in_=psS[nh][:], func=AF.Copy),
                          reads=[R(psS[nh])], writes=[R(sb)])
                    E("qA", lambda q, jt=jt: q.dma_start(
                        out=s_scr.ap()[jt * 128:(jt + 1) * 128, :], in_=sb[:]),
                      reads=[R(sb)], writes=["s_scr"])

            upmax = ctx.sbuf("l2_upmax", [128, 2], F32)
            E("V", lambda q: q.memset(upmax[:], 0.0), writes=[R(upmax)])
            scrA = dram("l2_scrA", [T, 3072])
            scrB = dram("l2_scrB", [T, 3072])
            scrC = dram("l2_scrC", [T, 3072])

            def arscalar(col, colnm, bname, nv=1):
                for j in range(nv):
                    pmax_col(col[:, j:j + 1], colnm)
                bi = dram(bname + "_i", [1, nv])
                bo = dram(bname + "_o", [1, nv])
                E("qP", lambda q: q.dma_start(out=bi.ap(), in_=col[0:1, 0:nv]),
                  reads=[colnm], writes=[bname + "_i"])
                E("P", lambda q: q.collective_compute(
                    "AllReduce", ALU.max, replica_groups=[list(range(NC))],
                    ins=[bi.ap()], outs=[bo.ap()]),
                  reads=[bname + "_i"], writes=[bname + "_o"])
                E("qS", lambda q: q.dma_start(out=pm_row[:, 0:nv], in_=bo.ap()),
                  reads=[bname + "_o"], writes=[R(pm_row)])
                pbcast(col[:, 0:nv], colnm, pm_row[0:1, 0:nv], [R(pm_row)], nv)

            def mk_scale(dst, dstnm, src_ap, srcnms):
                E("V", lambda q: q.tensor_scalar(out=dst[:, 0:1], in0=src_ap,
                                                 scalar1=1.0 / 32767.0, scalar2=1e-12,
                                                 op0=ALU.mult, op1=ALU.max),
                  reads=list(srcnms), writes=[dstnm])
                E("V", lambda q: q.reciprocal(out=dst[:, 1:2], in_=dst[:, 0:1]),
                  reads=[dstnm], writes=[dstnm])

            with ctx.scope():
                arscalar(xnmax2, R(xnmax2), "l2_xmax")
                sxc = ctx.sbuf("l2_sxc", [128, 2], F32)
                mk_scale(sxc, R(sxc), xnmax2[:, 0:1], [R(xnmax2)])
                tmpr = ctx.sbuf("l2_tmpr", [128, 3072], F32)
                xqT2 = ctx.sbuf("l2_xqT", [128, 8 * T], BF16)
                xnf4 = ctx.sbuf("l2_xnf4", [128, 1024], F32)
                abf5 = ctx.sbuf("l2_abf5", [128, 3072], BF16)
                psQ5 = ctx.psum("l2_psQ5", [128, 1024], BF16)

                def qdq2(buf, bufnm, rs_ap, s_ap, scnms, qmin, qmax, w):
                    E("V", lambda q: q.tensor_scalar(out=buf[:, 0:w], in0=buf[:, 0:w],
                                                     scalar1=rs_ap, scalar2=None,
                                                     op0=ALU.mult),
                      reads=[bufnm] + list(scnms), writes=[bufnm])
                    round_ste(buf, bufnm, buf[:, 0:w], [bufnm], tmpr, w)
                    E("V", lambda q: q.tensor_scalar(out=buf[:, 0:w], in0=buf[:, 0:w],
                                                     scalar1=float(qmin), scalar2=float(qmax),
                                                     op0=ALU.max, op1=ALU.min),
                      reads=[bufnm], writes=[bufnm])
                    E("V", lambda q: q.tensor_scalar(out=buf[:, 0:w], in0=buf[:, 0:w],
                                                     scalar1=s_ap, scalar2=None, op0=ALU.mult),
                      reads=[bufnm] + list(scnms), writes=[bufnm])

                for jt in range(NT):
                    E("qS", lambda q, jt=jt: q.dma_start(
                        out=xnf4[:], in_=xn_f.ap()[jt * 128:(jt + 1) * 128, :]),
                      reads=["xn_f"], writes=[R(xnf4)])
                    qdq2(xnf4, R(xnf4), sxc[:, 1:2], sxc[:, 0:1], [R(sxc)], -32768, 32767,
                         1024)
                    E("V", lambda q: q.tensor_copy(out=abf5[:, 0:1024], in_=xnf4[:]),
                      reads=[R(xnf4)], writes=[R(abf5)])
                    for kc in range(8):
                        E("T", lambda q, kc=kc: q.transpose(
                            psQ5[:, kc * 128:(kc + 1) * 128],
                            abf5[:, kc * 128:(kc + 1) * 128], ident[:]),
                          reads=[R(abf5), R(ident)], writes=[R(psQ5)])
                    dstx = xqT2[:].rearrange("p (c t) -> p c t", c=8)[:, :,
                                                                     jt * 128:(jt + 1) * 128]
                    E("V", lambda q, dstx=dstx: q.tensor_copy(
                        out=dstx, in_=psQ5[:].rearrange("p (c t) -> p c t", c=8)),
                      reads=[R(psQ5)], writes=[R(xqT2)])

                wgT2 = ctx.sbuf("l2_wgT", [128, 8 * 3072], BF16)
                wuT2 = ctx.sbuf("l2_wuT", [128, 8 * 3072], BF16)
                wbuf5 = [ctx.sbuf("l2_wbuf5", [128, 1024], F32) for _ in range(2)]
                am5 = ctx.sbuf("l2_am5", [128, 96], F32)
                rs5 = ctx.sbuf("l2_rs5", [128, 96], F32)

                def lpbq(buf, bufnm, w):
                    nb = w // 32
                    E("V", lambda q, nb=nb, w=w: q.tensor_reduce(
                        out=am5[:, 0:nb], in_=buf[:, 0:w].rearrange("p (b k) -> p b k", k=32),
                        axis=AX.X, op=ALU.max, apply_absolute_value=True),
                      reads=[bufnm], writes=[R(am5)])
                    E("V", lambda q, nb=nb: q.tensor_scalar(out=am5[:, 0:nb],
                                                            in0=am5[:, 0:nb],
                                                            scalar1=1.0 / 7.0, scalar2=1e-12,
                                                            op0=ALU.mult, op1=ALU.max),
                      reads=[R(am5)], writes=[R(am5)])
                    E("V", lambda q, nb=nb: q.reciprocal(out=rs5[:, 0:nb], in_=am5[:, 0:nb]),
                      reads=[R(am5)], writes=[R(rs5)])
                    E("V", lambda q, nb=nb, w=w: q.tensor_tensor(
                        out=buf[:, 0:w].rearrange("p (b k) -> p b k", k=32),
                        in0=buf[:, 0:w].rearrange("p (b k) -> p b k", k=32),
                        in1=rs5[:, 0:nb].rearrange("p (b o) -> p b o", o=1).broadcast_to(
                            [128, nb, 32]), op=ALU.mult),
                      reads=[bufnm, R(rs5)], writes=[bufnm])
                    round_ste(buf, bufnm, buf[:, 0:w], [bufnm], tmpr, w)
                    E("V", lambda q, w=w: q.tensor_scalar(out=buf[:, 0:w], in0=buf[:, 0:w],
                                                          scalar1=-8.0, scalar2=7.0,
                                                          op0=ALU.max, op1=ALU.min),
                      reads=[bufnm], writes=[bufnm])
                    E("V", lambda q, nb=nb, w=w: q.tensor_tensor(
                        out=buf[:, 0:w].rearrange("p (b k) -> p b k", k=32),
                        in0=buf[:, 0:w].rearrange("p (b k) -> p b k", k=32),
                        in1=am5[:, 0:nb].rearrange("p (b o) -> p b o", o=1).broadcast_to(
                            [128, nb, 32]), op=ALU.mult),
                      reads=[bufnm, R(am5)], writes=[bufnm])

                for nm, WT in (("l2_gw", wgT2), ("l2_uw", wuT2)):
                    for it in range(24):
                        b = it % 2
                        E("qS", lambda q, nm=nm, it=it, b=b: q.dma_start(
                            out=wbuf5[b][:], in_=din[nm].ap()[it * 128:(it + 1) * 128, :]),
                          writes=[R(wbuf5[b])])
                        lpbq(wbuf5[b], R(wbuf5[b]), 1024)
                        E("V", lambda q, b=b: q.tensor_copy(out=abf5[:, 0:1024],
                                                            in_=wbuf5[b][:]),
                          reads=[R(wbuf5[b])], writes=[R(abf5)])
                        for kc in range(8):
                            E("T", lambda q, kc=kc: q.transpose(
                                psQ5[:, kc * 128:(kc + 1) * 128],
                                abf5[:, kc * 128:(kc + 1) * 128], ident[:]),
                              reads=[R(abf5), R(ident)], writes=[R(psQ5)])
                        dstw = WT[:].rearrange("p (c n) -> p c n", c=8)[:, :,
                                                                       it * 128:(it + 1) * 128]
                        E("V", lambda q, dstw=dstw: q.tensor_copy(
                            out=dstw, in_=psQ5[:].rearrange("p (c n) -> p c n", c=8)),
                          reads=[R(psQ5)], writes=[R(WT)])

                ubig = ctx.sbuf("l2_ubig", [128, 3072], F32)
                mcol5 = ctx.sbuf("l2_mcol5", [128, 1], F32)
                psM5 = [ctx.psum("l2_psM5", [128, 512], F32) for _ in range(2)]
                nch30 = [(s, min(s + 512, 3072)) for s in range(0, 3072, 512)]
                for wi, (WT, scr, scr_nm) in enumerate(((wuT2, scrA, "l2_scrA"),
                                                        (wgT2, scrB, "l2_scrB"))):
                    for jt in range(NT):
                        for nc_, (lo, hi) in enumerate(nch30):
                            pb = nc_ % 2
                            w_ = hi - lo
                            lhs = [xqT2[:, kc * T + jt * 128: kc * T + (jt + 1) * 128]
                                   for kc in range(8)]
                            rhs = [WT[:, kc * 3072 + lo: kc * 3072 + hi] for kc in range(8)]
                            mm_tile(psM5[pb][:, 0:w_], R(psM5[pb]), lhs, rhs,
                                    [R(xqT2), R(WT)])
                            E("S", lambda q, pb=pb, lo=lo, hi=hi, w_=w_: q.activation(
                                out=ubig[:, lo:hi], in_=psM5[pb][:, 0:w_], func=AF.Copy),
                              reads=[R(psM5[pb])], writes=[R(ubig)])
                        E("V", lambda q: q.tensor_reduce(out=mcol5[:], in_=ubig[:], axis=AX.X,
                                                         op=ALU.max,
                                                         apply_absolute_value=True),
                          reads=[R(ubig)], writes=[R(mcol5)])
                        E("V", lambda q, wi=wi: q.tensor_tensor(
                            out=upmax[:, wi:wi + 1], in0=upmax[:, wi:wi + 1], in1=mcol5[:],
                            op=ALU.max),
                          reads=[R(mcol5), R(upmax)], writes=[R(upmax)])
                        E("qA", lambda q, jt=jt, scr=scr, scr_nm=scr_nm: q.dma_start(
                            out=scr.ap()[jt * 128:(jt + 1) * 128, :], in_=ubig[:]),
                          reads=[R(ubig)], writes=[scr_nm])

            g2max = ctx.sbuf("l2_g2max", [128, 1], F32)
            E("V", lambda q: q.memset(g2max[:], 0.0), writes=[R(g2max)])
            with ctx.scope():
                arscalar(upmax, R(upmax), "l2_ugmax", nv=2)
                suc = ctx.sbuf("l2_suc", [128, 2], F32)
                sgc = ctx.sbuf("l2_sgc", [128, 2], F32)
                mk_scale(suc, R(suc), upmax[:, 0:1], [R(upmax)])
                mk_scale(sgc, R(sgc), upmax[:, 1:2], [R(upmax)])
                gbig = ctx.sbuf("l2_gbig", [128, 3072], F32)
                sbig = ctx.sbuf("l2_sbig", [128, 3072], F32)
                tmpr2 = ctx.sbuf("l2_tmpr2", [128, 3072], F32)
                mcol6 = ctx.sbuf("l2_mcol6", [128, 1], F32)

                def qdq3(buf, bufnm, rs_ap, s_ap, scnms, qmin, qmax, w):
                    E("V", lambda q: q.tensor_scalar(out=buf[:, 0:w], in0=buf[:, 0:w],
                                                     scalar1=rs_ap, scalar2=None,
                                                     op0=ALU.mult),
                      reads=[bufnm] + list(scnms), writes=[bufnm])
                    round_ste(buf, bufnm, buf[:, 0:w], [bufnm], tmpr2, w)
                    E("V", lambda q: q.tensor_scalar(out=buf[:, 0:w], in0=buf[:, 0:w],
                                                     scalar1=float(qmin), scalar2=float(qmax),
                                                     op0=ALU.max, op1=ALU.min),
                      reads=[bufnm], writes=[bufnm])
                    E("V", lambda q: q.tensor_scalar(out=buf[:, 0:w], in0=buf[:, 0:w],
                                                     scalar1=s_ap, scalar2=None, op0=ALU.mult),
                      reads=[bufnm] + list(scnms), writes=[bufnm])

                for jt in range(NT):
                    E("qS", lambda q, jt=jt: q.dma_start(
                        out=gbig[:], in_=scrB.ap()[jt * 128:(jt + 1) * 128, :]),
                      reads=["l2_scrB"], writes=[R(gbig)])
                    qdq3(gbig, R(gbig), sgc[:, 1:2], sgc[:, 0:1], [R(sgc)], -32768, 32767,
                         3072)
                    E("S", lambda q: q.activation(out=sbig[:], in_=gbig[:], func=AF.Sigmoid),
                      reads=[R(gbig)], writes=[R(sbig)])
                    E("V", lambda q: q.tensor_scalar(out=sbig[:], in0=sbig[:],
                                                     scalar1=65536.0, scalar2=None,
                                                     op0=ALU.mult),
                      reads=[R(sbig)], writes=[R(sbig)])
                    round_ste(sbig, R(sbig), sbig[:], [R(sbig)], tmpr2, 3072)
                    E("V", lambda q: q.tensor_scalar(out=sbig[:], in0=sbig[:], scalar1=0.0,
                                                     scalar2=65535.0, op0=ALU.max,
                                                     op1=ALU.min),
                      reads=[R(sbig)], writes=[R(sbig)])
                    E("V", lambda q: q.tensor_scalar(out=sbig[:], in0=sbig[:],
                                                     scalar1=1.0 / 65536.0, scalar2=None,
                                                     op0=ALU.mult),
                      reads=[R(sbig)], writes=[R(sbig)])
                    E("V", lambda q: q.tensor_mul(out=gbig[:], in0=gbig[:], in1=sbig[:]),
                      reads=[R(gbig), R(sbig)], writes=[R(gbig)])
                    E("V", lambda q: q.tensor_reduce(out=mcol6[:], in_=gbig[:], axis=AX.X,
                                                     op=ALU.max, apply_absolute_value=True),
                      reads=[R(gbig)], writes=[R(mcol6)])
                    E("V", lambda q: q.tensor_tensor(out=g2max[:], in0=g2max[:], in1=mcol6[:],
                                                     op=ALU.max),
                      reads=[R(mcol6), R(g2max)], writes=[R(g2max)])
                    E("qA", lambda q, jt=jt: q.dma_start(
                        out=scrC.ap()[jt * 128:(jt + 1) * 128, :], in_=gbig[:]),
                      reads=[R(gbig)], writes=["l2_scrC"])

            omax = ctx.sbuf("l2_omax", [128, 1], F32)
            E("V", lambda q: q.memset(omax[:], 0.0), writes=[R(omax)])
            with ctx.scope():
                arscalar(g2max, R(g2max), "l2_g2m")
                s2c = ctx.sbuf("l2_s2c", [128, 2], F32)
                suc2 = ctx.sbuf("l2_suc2", [128, 2], F32)
                mk_scale(s2c, R(s2c), g2max[:, 0:1], [R(g2max)])
                mk_scale(suc2, R(suc2), upmax[:, 0:1], [R(upmax)])
                gbig = ctx.sbuf("l2_gbigC", [128, 3072], F32)
                ubig2 = ctx.sbuf("l2_ubigC", [128, 3072], F32)
                tmpr3 = ctx.sbuf("l2_tmpr3", [128, 3072], F32)
                mcol7 = ctx.sbuf("l2_mcol7", [128, 1], F32)

                def qdq4(buf, bufnm, rs_ap, s_ap, scnms, w):
                    E("V", lambda q: q.tensor_scalar(out=buf[:, 0:w], in0=buf[:, 0:w],
                                                     scalar1=rs_ap, scalar2=None,
                                                     op0=ALU.mult),
                      reads=[bufnm] + list(scnms), writes=[bufnm])
                    round_ste(buf, bufnm, buf[:, 0:w], [bufnm], tmpr3, w)
                    E("V", lambda q: q.tensor_scalar(out=buf[:, 0:w], in0=buf[:, 0:w],
                                                     scalar1=-32768.0, scalar2=32767.0,
                                                     op0=ALU.max, op1=ALU.min),
                      reads=[bufnm], writes=[bufnm])
                    E("V", lambda q: q.tensor_scalar(out=buf[:, 0:w], in0=buf[:, 0:w],
                                                     scalar1=s_ap, scalar2=None, op0=ALU.mult),
                      reads=[bufnm] + list(scnms), writes=[bufnm])

                for jt in range(NT):
                    E("qS", lambda q, jt=jt: q.dma_start(
                        out=gbig[:], in_=scrC.ap()[jt * 128:(jt + 1) * 128, :]),
                      reads=["l2_scrC"], writes=[R(gbig)])
                    E("qS", lambda q, jt=jt: q.dma_start(
                        out=ubig2[:], in_=scrA.ap()[jt * 128:(jt + 1) * 128, :]),
                      reads=["l2_scrA"], writes=[R(ubig2)])
                    qdq4(gbig, R(gbig), s2c[:, 1:2], s2c[:, 0:1], [R(s2c)], 3072)
                    qdq4(ubig2, R(ubig2), suc2[:, 1:2], suc2[:, 0:1], [R(suc2)], 3072)
                    E("V", lambda q: q.tensor_mul(out=gbig[:], in0=gbig[:], in1=ubig2[:]),
                      reads=[R(gbig), R(ubig2)], writes=[R(gbig)])
                    E("V", lambda q: q.tensor_reduce(out=mcol7[:], in_=gbig[:], axis=AX.X,
                                                     op=ALU.max, apply_absolute_value=True),
                      reads=[R(gbig)], writes=[R(mcol7)])
                    E("V", lambda q: q.tensor_tensor(out=omax[:], in0=omax[:], in1=mcol7[:],
                                                     op=ALU.max),
                      reads=[R(mcol7), R(omax)], writes=[R(omax)])
                    E("qA", lambda q, jt=jt: q.dma_start(
                        out=scrA.ap()[jt * 128:(jt + 1) * 128, :], in_=gbig[:]),
                      reads=[R(gbig)], writes=["l2_scrA2"])

            with ctx.scope():
                arscalar(omax, R(omax), "l2_om")
                soc = ctx.sbuf("l2_soc", [128, 2], F32)
                mk_scale(soc, R(soc), omax[:, 0:1], [R(omax)])
                dwT2 = ctx.sbuf("l2_dwT", [128, 24 * 1024], BF16)
                wbuf6 = [ctx.sbuf("l2_wbuf6", [128, 3072], F32) for _ in range(2)]
                am6 = ctx.sbuf("l2_am6", [128, 96], F32)
                rs6 = ctx.sbuf("l2_rs6", [128, 96], F32)
                tmpr4 = ctx.sbuf("l2_tmpr4", [128, 3072], F32)
                abf6 = ctx.sbuf("l2_abf6", [128, 3072], BF16)
                psQ6 = ctx.psum("l2_psQ6", [128, 1024], BF16)

                def lpbq2(buf, bufnm, w):
                    nb = w // 32
                    E("V", lambda q, nb=nb, w=w: q.tensor_reduce(
                        out=am6[:, 0:nb], in_=buf[:, 0:w].rearrange("p (b k) -> p b k", k=32),
                        axis=AX.X, op=ALU.max, apply_absolute_value=True),
                      reads=[bufnm], writes=[R(am6)])
                    E("V", lambda q, nb=nb: q.tensor_scalar(out=am6[:, 0:nb],
                                                            in0=am6[:, 0:nb],
                                                            scalar1=1.0 / 7.0, scalar2=1e-12,
                                                            op0=ALU.mult, op1=ALU.max),
                      reads=[R(am6)], writes=[R(am6)])
                    E("V", lambda q, nb=nb: q.reciprocal(out=rs6[:, 0:nb], in_=am6[:, 0:nb]),
                      reads=[R(am6)], writes=[R(rs6)])
                    E("V", lambda q, nb=nb, w=w: q.tensor_tensor(
                        out=buf[:, 0:w].rearrange("p (b k) -> p b k", k=32),
                        in0=buf[:, 0:w].rearrange("p (b k) -> p b k", k=32),
                        in1=rs6[:, 0:nb].rearrange("p (b o) -> p b o", o=1).broadcast_to(
                            [128, nb, 32]), op=ALU.mult),
                      reads=[bufnm, R(rs6)], writes=[bufnm])
                    round_ste(buf, bufnm, buf[:, 0:w], [bufnm], tmpr4, w)
                    E("V", lambda q, w=w: q.tensor_scalar(out=buf[:, 0:w], in0=buf[:, 0:w],
                                                          scalar1=-8.0, scalar2=7.0,
                                                          op0=ALU.max, op1=ALU.min),
                      reads=[bufnm], writes=[bufnm])
                    E("V", lambda q, nb=nb, w=w: q.tensor_tensor(
                        out=buf[:, 0:w].rearrange("p (b k) -> p b k", k=32),
                        in0=buf[:, 0:w].rearrange("p (b k) -> p b k", k=32),
                        in1=am6[:, 0:nb].rearrange("p (b o) -> p b o", o=1).broadcast_to(
                            [128, nb, 32]), op=ALU.mult),
                      reads=[bufnm, R(am6)], writes=[bufnm])

                for it in range(8):
                    b = it % 2
                    E("qS", lambda q, it=it, b=b: q.dma_start(
                        out=wbuf6[b][:], in_=din["l2_dw"].ap()[it * 128:(it + 1) * 128, :]),
                      writes=[R(wbuf6[b])])
                    lpbq2(wbuf6[b], R(wbuf6[b]), 3072)
                    E("V", lambda q, b=b: q.tensor_copy(out=abf6[:], in_=wbuf6[b][:]),
                      reads=[R(wbuf6[b])], writes=[R(abf6)])
                    for kc in range(24):
                        E("T", lambda q, kc=kc: q.transpose(
                            psQ6[:, (kc % 8) * 128:(kc % 8 + 1) * 128],
                            abf6[:, kc * 128:(kc + 1) * 128], ident[:]),
                          reads=[R(abf6), R(ident)], writes=[R(psQ6)])
                        E("V", lambda q, kc=kc, it=it: q.tensor_copy(
                            out=dwT2[:, kc * 1024 + it * 128: kc * 1024 + (it + 1) * 128],
                            in_=psQ6[:, (kc % 8) * 128:(kc % 8 + 1) * 128]),
                          reads=[R(psQ6)], writes=[R(dwT2)])
                obig = ctx.sbuf("l2_obig", [128, 3072], F32)
                oT6 = ctx.sbuf("l2_oT6", [128, 24 * 128], BF16)
                fout6 = ctx.sbuf("l2_fout6", [128, 1024], F32)
                stile6 = ctx.sbuf("l2_stile6", [128, 1024], F32)
                xtile6 = ctx.sbuf("l2_xtile6", [128, 1024], F32)
                psM6 = [ctx.psum("l2_psM6", [128, 512], F32) for _ in range(2)]

                def qdq5(buf, bufnm, rs_ap, s_ap, scnms, w):
                    E("V", lambda q: q.tensor_scalar(out=buf[:, 0:w], in0=buf[:, 0:w],
                                                     scalar1=rs_ap, scalar2=None,
                                                     op0=ALU.mult),
                      reads=[bufnm] + list(scnms), writes=[bufnm])
                    round_ste(buf, bufnm, buf[:, 0:w], [bufnm], tmpr4, w)
                    E("V", lambda q: q.tensor_scalar(out=buf[:, 0:w], in0=buf[:, 0:w],
                                                     scalar1=-32768.0, scalar2=32767.0,
                                                     op0=ALU.max, op1=ALU.min),
                      reads=[bufnm], writes=[bufnm])
                    E("V", lambda q: q.tensor_scalar(out=buf[:, 0:w], in0=buf[:, 0:w],
                                                     scalar1=s_ap, scalar2=None, op0=ALU.mult),
                      reads=[bufnm] + list(scnms), writes=[bufnm])

                for jt in range(NT):
                    E("qS", lambda q, jt=jt: q.dma_start(
                        out=obig[:], in_=scrA.ap()[jt * 128:(jt + 1) * 128, :]),
                      reads=["l2_scrA2"], writes=[R(obig)])
                    qdq5(obig, R(obig), soc[:, 1:2], soc[:, 0:1], [R(soc)], 3072)
                    E("V", lambda q: q.tensor_copy(out=abf6[:], in_=obig[:]),
                      reads=[R(obig)], writes=[R(abf6)])
                    for kc in range(24):
                        E("T", lambda q, kc=kc: q.transpose(
                            psQ6[:, (kc % 8) * 128:(kc % 8 + 1) * 128],
                            abf6[:, kc * 128:(kc + 1) * 128], ident[:]),
                          reads=[R(abf6), R(ident)], writes=[R(psQ6)])
                        E("V", lambda q, kc=kc: q.tensor_copy(
                            out=oT6[:, kc * 128:(kc + 1) * 128],
                            in_=psQ6[:, (kc % 8) * 128:(kc % 8 + 1) * 128]),
                          reads=[R(psQ6)], writes=[R(oT6)])
                    for nh in range(2):
                        lhs = [oT6[:, kc * 128:(kc + 1) * 128] for kc in range(24)]
                        rhs = [dwT2[:, kc * 1024 + nh * 512: kc * 1024 + (nh + 1) * 512]
                               for kc in range(24)]
                        mm_tile(psM6[nh][:], R(psM6[nh]), lhs, rhs, [R(oT6), R(dwT2)])
                        E("S", lambda q, nh=nh: q.activation(
                            out=fout6[:, nh * 512:(nh + 1) * 512], in_=psM6[nh][:],
                            func=AF.Silu),
                          reads=[R(psM6[nh])], writes=[R(fout6)])
                    residual_epilogue(jt, fout6, R(fout6), dst_dram, dst_nm, stile6, xtile6,
                                      xsrc, xsrc_nm)

        last = max(layers) if layers else -1
        if 0 in layers:
            with ctx.scope():
                layer0(x_in, "x_in", out if last == 0 else xres,
                       "out_hbm" if last == 0 else "xres")
        if 1 in layers:
            src0 = xres if 0 in layers else x_in
            src0nm = "xres" if 0 in layers else "x_in"
            layer1(src0, src0nm, out if last == 1 else xres,
                   "out_hbm" if last == 1 else "xres")
        if 2 in layers:
            src0 = xres if len([l for l in layers if l < 2]) else x_in
            src0nm = "xres" if len([l for l in layers if l < 2]) else "x_in"
            layer2(src0, src0nm, out if last == 2 else xres,
                   "out_hbm" if last == 2 else "xres")
        if 3 in layers:
            src0 = xres if len([l for l in layers if l < 3]) else x_in
            src0nm = "xres" if len([l for l in layers if l < 3]) else "x_in"
            layer3(src0, src0nm, out if last == 3 else xres,
                   "out_hbm" if last == 3 else "xres")
        if not layers:
            fin = ctx.sbuf("fin", [128, D], F32)
            for jt in range(NT):
                E("qS", lambda q, jt=jt: q.dma_start(
                    out=fin[:], in_=x_in.ap()[jt * 128:(jt + 1) * 128, :]),
                  writes=[R(fin)])
                E("qS", lambda q, jt=jt: q.dma_start(
                    out=out.ap()[jt * 128:(jt + 1) * 128, :], in_=fin[:]),
                  reads=[R(fin)], writes=["out_hbm"])
        ctx.barrier()
        ctx.stack.close()
    return nc


def _consts_arr():
    a = np.zeros((128, len(CONSTS)), np.float32)
    for v, i in CIDX.items():
        a[:, i] = v
    return a


def host_inputs(inputs):
    g = {}
    g["ident"] = np.eye(128, dtype=np.float32)
    g["consts"] = _consts_arr()
    g["ones_row"] = np.ones((1, 128), np.float32)
    tr = lambda a: np.ascontiguousarray(np.asarray(a).T)
    g["l0_WkT"] = tr(inputs["L0_attn_Wk"])
    g["l0_WvT"] = tr(inputs["L0_attn_Wv"])
    g["l0_WrT"] = tr(inputs["L0_attn_Wr"])
    g["l0_WoT"] = tr(inputs["L0_attn_Wo"])
    g["l0_td"] = np.ascontiguousarray(np.asarray(inputs["L0_attn_time_decay"]).reshape(8, 128).T)
    g["l0_tf"] = np.ascontiguousarray(np.asarray(inputs["L0_attn_time_first"]).reshape(8, 128).T)
    g["l0_gwT"] = tr(inputs["L0_ffn_gate_w"])
    g["l0_We1T"] = np.ascontiguousarray(np.transpose(np.asarray(inputs["L0_ffn_We1"]), (0, 2, 1)))
    g["l0_We3T"] = np.ascontiguousarray(np.transpose(np.asarray(inputs["L0_ffn_We3"]), (0, 2, 1)))
    g["l0_We2T"] = np.ascontiguousarray(np.transpose(np.asarray(inputs["L0_ffn_We2"]), (0, 2, 1)))
    g["l0_be1"] = np.ascontiguousarray(np.asarray(inputs["L0_ffn_be1"]))
    g["l0_be3"] = np.ascontiguousarray(np.asarray(inputs["L0_ffn_be3"]))
    g["l0_be2"] = np.ascontiguousarray(np.asarray(inputs["L0_ffn_be2"]))
    g["l0_Ws1T"] = tr(inputs["L0_ffn_Ws1"])
    g["l0_Ws3T"] = tr(inputs["L0_ffn_Ws3"])
    g["l0_Ws2T"] = tr(inputs["L0_ffn_Ws2"])
    g["l0_bs1"] = np.ascontiguousarray(np.asarray(inputs["L0_ffn_bs1"]))[None, :]
    g["l0_bs3"] = np.ascontiguousarray(np.asarray(inputs["L0_ffn_bs3"]))[None, :]
    g["l0_bs2"] = np.ascontiguousarray(np.asarray(inputs["L0_ffn_bs2"]))[None, :]
    g["l1_WqT"] = tr(inputs["L1_attn_Wq"])
    g["l1_WkvdT"] = tr(inputs["L1_attn_Wkv_down"])
    g["l1_WkuT"] = tr(inputs["L1_attn_Wk_up"])
    g["l1_WvuT"] = tr(inputs["L1_attn_Wv_up"])
    g["l1_WoT"] = tr(inputs["L1_attn_Wo"])
    g["l1_gw"] = np.ascontiguousarray(np.asarray(inputs["L1_ffn_gate_w"]))
    g["l1_uw"] = np.ascontiguousarray(np.asarray(inputs["L1_ffn_up_w"]))
    g["l1_dw"] = np.ascontiguousarray(np.asarray(inputs["L1_ffn_down_w"]))
    g["l1_s_in"] = np.ascontiguousarray(np.asarray(inputs["L1_ffn_s_in"]).reshape(1, 1))
    g["l1_s_in_down"] = np.ascontiguousarray(np.asarray(inputs["L1_ffn_s_in_down"]).reshape(1, 1))
    g["l2_Wq"] = np.ascontiguousarray(np.asarray(inputs["L2_attn_Wq"]))
    g["l2_Wqr"] = np.ascontiguousarray(np.asarray(inputs["L2_attn_Wqr"]))
    g["l2_Wkr"] = np.ascontiguousarray(np.asarray(inputs["L2_attn_Wkr"]))
    g["l2_Wkvd"] = np.ascontiguousarray(np.asarray(inputs["L2_attn_Wkvd"]))
    g["l2_Wku"] = np.ascontiguousarray(np.asarray(inputs["L2_attn_Wku"]))
    g["l2_Wvu"] = np.ascontiguousarray(np.asarray(inputs["L2_attn_Wvu"]))
    g["l2_Wo"] = np.ascontiguousarray(np.asarray(inputs["L2_attn_Wo"]))
    g["l2_lobo"] = np.ascontiguousarray(np.asarray(inputs["L2_attn_lobo_log"]).reshape(1, 16))
    g["l2_gw"] = np.ascontiguousarray(np.asarray(inputs["L2_ffn_w_gate"]))
    g["l2_uw"] = np.ascontiguousarray(np.asarray(inputs["L2_ffn_w_up"]))
    g["l2_dw"] = np.ascontiguousarray(np.asarray(inputs["L2_ffn_w_down"]))
    T_ = np.asarray(inputs["L0_attn_x"]).shape[1] if "L0_attn_x" in inputs else 4096
    half = 16
    inv = 1.0 / (10000.0 ** (np.arange(half, dtype=np.float64) / half))
    ang = np.arange(T_, dtype=np.float64)[:, None] * inv[None, :]
    g["ropeC"] = np.cos(ang).astype(np.float32)
    g["ropeS"] = np.sin(ang).astype(np.float32)
    g["causmaskB"] = ((1.0 - np.tril(np.ones((128, 128)))) * 1e30).astype(np.float32)
    g["l3_WqT"] = tr(inputs["L3_attn_Wq"])
    g["l3_WkT"] = tr(inputs["L3_attn_Wk"])
    g["l3_WvT"] = tr(inputs["L3_attn_Wv"])
    g["l3_bq"] = np.ascontiguousarray(np.asarray(inputs["L3_attn_bq"]).reshape(64, 1))
    g["l3_bk"] = np.ascontiguousarray(np.asarray(inputs["L3_attn_bk"]).reshape(64, 1))
    g["l3_bv"] = np.ascontiguousarray(np.asarray(inputs["L3_attn_bv"]).reshape(1, 64))
    g["l3_clip_x"] = np.ascontiguousarray(np.asarray(inputs["L3_ffn_clip_x"]).reshape(1, 1))
    g["l3_clip_down"] = np.ascontiguousarray(np.asarray(inputs["L3_ffn_clip_down"]).reshape(1, 1))
    g["l3_gw"] = np.ascontiguousarray(np.asarray(inputs["L3_ffn_w_gate"]))
    g["l3_uw"] = np.ascontiguousarray(np.asarray(inputs["L3_ffn_w_up"]))
    g["l3_dw"] = np.ascontiguousarray(np.asarray(inputs["L3_ffn_w_down"]))
    return g


def kernel(**inputs):
    x = np.asarray(inputs["L0_attn_x"])
    B, T, _ = x.shape
    g = host_inputs(inputs)
    nc = build_nc(T=T, layers=(0, 1, 2, 3))
    in_maps = []
    for i in range(B):
        m = dict(g)
        m["x"] = np.ascontiguousarray(x[i])
        in_maps.append(m)
    res = run_bass_kernel_spmd(nc, in_maps, core_ids=list(range(B)))
    return np.stack([np.asarray(r["out"]) for r in res.results], axis=0)


# version 08
# speedup vs baseline: 1.0443x; 1.0443x over previous
import numpy as np
from contextlib import ExitStack, contextmanager

import concourse.bass as bass
import concourse.mybir as mybir
import bass_rust
from concourse.bass_utils import run_bass_kernel_spmd

F32 = mybir.dt.float32
BF16 = mybir.dt.bfloat16
F8E4 = mybir.dt.float8e4
AF = mybir.ActivationFunctionType
ALU = mybir.AluOpType
AX = mybir.AxisListType
RO = bass_rust.ReduceOp

D = 1024
NCORES = 8

CONSTS = [1.0, 0.5, 1e-6, 1e-5, 2.0, -1.0, 0.0, 1.0 / D, 6.0, 448.0, 2.25, 4.5,
          127.0, -128.0, 32767.0, -32768.0, 65535.0, 1.0 / 65536.0, 2.0 ** -40]
CIDX = {v: i for i, v in enumerate(CONSTS)}


class Ctx:

    NDMASEM = 8

    def __init__(self, nc, block):
        self.nc = nc
        self.block = block
        self.stack = ExitStack()
        self.scopes = [self.stack]
        self.engs = {}
        self.regions = {}
        self.region_of = {}
        self._uid = 0
        for name, sec in (("T", block.tensor), ("V", block.vector), ("S", block.scalar),
                          ("P", block.gpsimd)):
            sem = self.stack.enter_context(nc.semaphore("sem_" + name))
            self.engs[name] = dict(sec=sec, sem=sem, ops=[], wm={}, pend=[])
        self.dmaq = {}
        for name, sec in (("qS", block.sync), ("qA", block.scalar), ("qP", block.gpsimd)):
            sems = [self.stack.enter_context(nc.semaphore(f"sem_{name}_{i}"))
                    for i in range(self.NDMASEM)]
            self.dmaq[name] = dict(sec=sec, sems=sems, tot=[0] * self.NDMASEM, n=0,
                                   pend=[], wm={})
        self.dmaq["qA"]["pend"] = self.engs["S"]["pend"]
        self.dmaq["qP"]["pend"] = self.engs["P"]["pend"]

    def sbuf(self, name, shape, dtype):
        self._uid += 1
        t = self.scopes[-1].enter_context(
            self.nc.sbuf_tensor(f"{name}_{self._uid}", shape, dtype))
        self.region_of[id(t)] = f"{name}_{self._uid}"
        return t

    def psum(self, name, shape, dtype):
        self._uid += 1
        t = self.scopes[-1].enter_context(
            self.nc.psum_tensor(f"{name}_{self._uid}", shape, dtype))
        self.region_of[id(t)] = f"{name}_{self._uid}"
        return t

    @contextmanager
    def scope(self):
        es = ExitStack()
        self.scopes.append(es)
        try:
            yield
        finally:
            self.barrier()
            self.scopes.pop()
            es.close()

    def _wait_value(self, pe, idx):
        return idx + 1

    def _flush(self):
        for name, e in list(self.engs.items()) + [("qS", self.dmaq["qS"])]:
            pend = e.get("pend")
            if not pend:
                continue
            batch = list(pend)
            pend.clear()

            def run(q, batch=batch):
                for item in batch:
                    item(q)
            e["sec"](run)

    def _collect_waits(self, ename, reads, writes):
        waits = {}
        for r in reads:
            reg = self.regions.get(r)
            if reg:
                for pe, idx in reg["w"].items():
                    if not (pe == ename == "T") and waits.get(pe, -1) < idx:
                        waits[pe] = idx
        for w in writes:
            reg = self.regions.get(w)
            if reg:
                for dd in (reg["w"], reg["r"]):
                    for pe, idx in dd.items():
                        if not (pe == ename == "T") and waits.get(pe, -1) < idx:
                            waits[pe] = idx
        return waits

    def _do_waits(self, ent, waits, self_name=None):
        wm = ent["wm"]
        pend = ent["pend"]
        self_idx = waits.pop(self_name, None) if self_name else None
        for pe, idx in sorted(waits.items()):
            if "#" in pe:
                qn, slot = pe.split("#")
                slot = int(slot)
                sem = self.dmaq[qn]["sems"][slot]
                tgt = idx
            else:
                sem = self.engs[pe]["sem"]
                tgt = self._wait_value(pe, idx)
            if wm.get(pe, -1) < tgt:
                pend.append(lambda q, s=sem, t=tgt: q.wait_ge(s, t))
                wm[pe] = tgt
        if self_idx is not None:
            tgt = self._wait_value(self_name, self_idx)
            e = self.engs[self_name]
            if wm.get(self_name, -1) < tgt:
                pend.append(lambda q, s=e["sem"], t=tgt: q.wait_ge(s, t))
                wm[self_name] = tgt

    def emit(self, ename, fn, reads=(), writes=()):
        if ename.startswith("q"):
            return self._dma(ename, fn, reads, writes)
        e = self.engs[ename]
        waits = self._collect_waits(ename, reads, writes)
        self._do_waits(e, waits, self_name=ename)
        opidx = len(e["ops"])
        e["ops"].append(None)

        def runner(q, fn=fn, e=e, opidx=opidx):
            inst = fn(q)
            inst.then_inc(e["sem"], 1)
            e["ops"][opidx] = inst
        e["pend"].append(runner)
        for w in writes:
            self.regions[w] = {"w": {ename: opidx}, "r": {}}
        for r in reads:
            reg = self.regions.setdefault(r, {"w": {}, "r": {}})
            if reg["r"].get(ename, -1) < opidx:
                reg["r"][ename] = opidx
        return None

    def _dma(self, qname, fn, reads=(), writes=()):
        q = self.dmaq[qname]
        slot = q["n"] % self.NDMASEM
        waits = self._collect_waits(qname, reads, writes)
        if q["tot"][slot] > 0:
            key = f"{qname}#{slot}"
            if waits.get(key, -1) < q["tot"][slot]:
                waits[key] = q["tot"][slot]
        self._do_waits(q, waits)

        def runner(qq, fn=fn, sem=q["sems"][slot]):
            fn(qq).then_inc(sem, 16)
        q["pend"].append(runner)
        q["tot"][slot] += 16
        q["n"] += 1
        token = f"{qname}#{slot}"
        tot = q["tot"][slot]
        for w in writes:
            self.regions[w] = {"w": {token: tot}, "r": {}}
        for r in reads:
            reg = self.regions.setdefault(r, {"w": {}, "r": {}})
            if reg["r"].get(token, -1) < tot:
                reg["r"][token] = tot
        return None

    def barrier(self):
        targets = {}
        for pe, e in self.engs.items():
            if e["ops"]:
                targets[pe] = len(e["ops"]) - 1
        for qn, q in self.dmaq.items():
            for slot in range(self.NDMASEM):
                if q["tot"][slot] > 0:
                    targets[f"{qn}#{slot}"] = q["tot"][slot]
        for ename, e in self.engs.items():
            self._do_waits(e, dict(targets), self_name=ename)
        for qn, q in self.dmaq.items():
            self._do_waits(q, dict(targets))
        self.regions = {}
        self._flush()


def build_nc(T=4096, layers=(0, 1, 2, 3), taps=(), n_cores=NCORES):
    import os
    nc = bass.Bass("TRN2", target_bir_lowering=False, debug=bool(os.environ.get("KDBG")))
    NT = T // 128
    taps = set(taps)

    def dram(name, shape, dtype=F32):
        kind = "ExternalOutput" if name in taps else "Internal"
        return nc.dram_tensor(name, shape, dtype, kind=kind)

    din = {}

    def inp(name, shape, dtype=F32):
        din[name] = nc.dram_tensor(name, shape, dtype, kind="ExternalInput")
        return din[name]

    x_in = inp("x", [T, D])
    inp("ident", [128, 128])
    inp("consts", [128, len(CONSTS)])
    inp("ones_row", [1, 128])
    if 0 in layers:
        inp("l0_WkT", [D, D]); inp("l0_WvT", [D, D]); inp("l0_WrT", [D, D]); inp("l0_WoT", [D, D])
        inp("l0_td", [128, 8]); inp("l0_tf", [128, 8])
        inp("l0_gwT", [D, 16])
        inp("l0_We1T", [16, D, 511]); inp("l0_We3T", [16, D, 511]); inp("l0_We2T", [16, 512, 1023])
        inp("l0_be1", [16, 511]); inp("l0_be3", [16, 511]); inp("l0_be2", [16, 1023])
        inp("l0_Ws1T", [D, 1023]); inp("l0_Ws3T", [D, 1023]); inp("l0_Ws2T", [1024, 1023])
        inp("l0_bs1", [1, 1023]); inp("l0_bs3", [1, 1023]); inp("l0_bs2", [1, 1023])

    if 1 in layers:
        inp("l1_WqT", [D, 2048]); inp("l1_WkvdT", [D, 64])
        inp("l1_WkuT", [64, 2048]); inp("l1_WvuT", [64, 2048]); inp("l1_WoT", [2048, D])
        inp("l1_gw", [4096, D]); inp("l1_uw", [4096, D]); inp("l1_dw", [D, 4096])
        inp("l1_s_in", [1, 1]); inp("l1_s_in_down", [1, 1])

    if 2 in layers:
        inp("l2_Wq", [D, 1024]); inp("l2_Wqr", [D, 512]); inp("l2_Wkr", [D, 32])
        inp("l2_Wkvd", [D, 256]); inp("l2_Wku", [256, 1024]); inp("l2_Wvu", [256, 1024])
        inp("l2_Wo", [D, 1024]); inp("l2_lobo", [1, 16])
        inp("ropeC", [T, 16]); inp("ropeS", [T, 16]); inp("causmaskB", [128, 128])
        inp("l2_gw", [3072, D]); inp("l2_uw", [3072, D]); inp("l2_dw", [D, 3072])
    if 3 in layers:
        inp("l3_WqT", [D, 64]); inp("l3_WkT", [D, 64]); inp("l3_WvT", [D, 64])
        inp("l3_bq", [64, 1]); inp("l3_bk", [64, 1]); inp("l3_bv", [1, 64])
        inp("l3_clip_x", [1, 1]); inp("l3_clip_down", [1, 1])
        inp("l3_gw", [2816, D]); inp("l3_uw", [2816, D]); inp("l3_dw", [D, 2816])

    out = nc.dram_tensor("out", [T, D], F32, kind="ExternalOutput")
    xres = dram("xres", [T, D])
    xn_f = dram("xn_f", [T, D])
    s_scr = dram("s_scr", [T, D])

    with nc.Block() as block:
        ctx = Ctx(nc, block)
        E = ctx.emit

        cc = ctx.sbuf("cc", [128, len(CONSTS)], F32)
        ident = ctx.sbuf("ident", [128, 128], BF16)
        onesb = ctx.sbuf("onesb", [1, 128], BF16)

        def C(v):
            return cc[:, CIDX[v]:CIDX[v] + 1]

        def R(t):
            return ctx.region_of[id(t)]

        E("qS", lambda q: q.dma_start(out=cc[:], in_=din["consts"].ap()), writes=[R(cc)])
        E("qP", lambda q: q.dma_start(out=ident[:], in_=din["ident"].ap()), writes=[R(ident)])
        E("qP", lambda q: q.dma_start(out=onesb[:], in_=din["ones_row"].ap()), writes=[R(onesb)])

        def mm_tile(ps, ps_nm, lhs, rhs, rdeps, first=True, last=True):
            n = len(lhs)
            for i, (la, ra) in enumerate(zip(lhs, rhs)):
                st, sp = first and i == 0, last and i == n - 1
                E("T", lambda q, la=la, ra=ra, st=st, sp=sp: q.matmul(ps, la, ra, start=st, stop=sp, skip_group_check=True),
                  reads=list(rdeps), writes=[ps_nm])

        def layernorm_phase(xnT, src_dram, src_nm, write_xn_f, absmax_col=None,
                            absmax_nm=None):
            with ctx.scope():
                xt = [ctx.sbuf("ln_xt", [128, D], F32) for _ in range(2)]
                xnb = [ctx.sbuf("ln_xnb", [128, D], BF16) for _ in range(2)]
                xnf = ctx.sbuf("ln_xnf", [128, D], F32)
                sumx = ctx.sbuf("ln_sumx", [128, 1], F32)
                sq = ctx.sbuf("ln_sq", [128, D], F32)
                ssq = ctx.sbuf("ln_ssq", [128, 1], F32)
                m = ctx.sbuf("ln_m", [128, 1], F32)
                var = ctx.sbuf("ln_var", [128, 1], F32)
                rstd = ctx.sbuf("ln_rstd", [128, 1], F32)
                nmr = ctx.sbuf("ln_nmr", [128, 1], F32)
                pt = [ctx.psum("ln_pt", [128, 1024], BF16) for _ in range(2)]
                for jt in range(NT):
                    b = jt % 2
                    E("qS", lambda q, jt=jt, b=b: q.dma_start(
                        out=xt[b][:], in_=src_dram.ap()[jt * 128:(jt + 1) * 128, :]),
                      reads=[src_nm], writes=[R(xt[b])])
                    E("V", lambda q, b=b: q.tensor_reduce(out=sumx[:], in_=xt[b][:],
                                                          axis=AX.X, op=ALU.add),
                      reads=[R(xt[b])], writes=[R(sumx)])
                    E("S", lambda q, b=b: q.activation(out=sq[:], in_=xt[b][:], func=AF.Square,
                                                       accum_out=ssq[:, 0:1]),
                      reads=[R(xt[b])], writes=[R(sq), R(ssq)])
                    E("V", lambda q: q.tensor_scalar(out=m[:], in0=sumx[:], scalar1=1.0 / D,
                                                     scalar2=None, op0=ALU.mult),
                      reads=[R(sumx)], writes=[R(m)])
                    E("V", lambda q: q.scalar_tensor_tensor(out=var[:], in0=m[:], scalar=-1.0,
                                                            in1=m[:], op0=ALU.mult, op1=ALU.mult),
                      reads=[R(m)], writes=[R(var)])
                    E("V", lambda q: q.scalar_tensor_tensor(out=var[:], in0=ssq[:], scalar=1.0 / D,
                                                            in1=var[:], op0=ALU.mult, op1=ALU.add),
                      reads=[R(ssq), R(var)], writes=[R(var)])
                    E("S", lambda q: q.activation(out=rstd[:], in_=var[:], func=AF.Sqrt,
                                                  bias=C(1e-5), scale=1.0),
                      reads=[R(var), R(cc)], writes=[R(rstd)])
                    E("V", lambda q: q.reciprocal(out=rstd[:], in_=rstd[:]),
                      reads=[R(rstd)], writes=[R(rstd)])
                    E("V", lambda q: q.scalar_tensor_tensor(out=nmr[:], in0=m[:], scalar=-1.0,
                                                            in1=rstd[:], op0=ALU.mult, op1=ALU.mult),
                      reads=[R(m), R(rstd)], writes=[R(nmr)])
                    E("S", lambda q, b=b: q.activation(out=xnb[b][:], in_=xt[b][:], func=AF.Identity,
                                                       bias=nmr[:, 0:1], scale=rstd[:, 0:1]),
                      reads=[R(xt[b]), R(nmr), R(rstd)], writes=[R(xnb[b])])
                    if write_xn_f:
                        E("S", lambda q, b=b: q.activation(out=xnf[:], in_=xt[b][:], func=AF.Identity,
                                                           bias=nmr[:, 0:1], scale=rstd[:, 0:1]),
                          reads=[R(xt[b]), R(nmr), R(rstd)], writes=[R(xnf)])
                        E("qA", lambda q, jt=jt: q.dma_start(
                            out=xn_f.ap()[jt * 128:(jt + 1) * 128, :], in_=xnf[:]),
                          reads=[R(xnf)], writes=["xn_f"])
                        if absmax_col is not None:
                            E("V", lambda q: q.tensor_reduce(out=m[:], in_=xnf[:], axis=AX.X,
                                                             op=ALU.max,
                                                             apply_absolute_value=True),
                              reads=[R(xnf)], writes=[R(m)])
                            E("V", lambda q: q.tensor_tensor(out=absmax_col, in0=absmax_col,
                                                             in1=m[:], op=ALU.max),
                              reads=[R(m), absmax_nm], writes=[absmax_nm])
                    for c in range(8):
                        E("T", lambda q, b=b, c=c: q.transpose(
                            pt[b][:, c * 128:(c + 1) * 128], xnb[b][:, c * 128:(c + 1) * 128],
                            ident[:]),
                          reads=[R(xnb[b]), R(ident)], writes=[R(pt[b])])
                    dst = xnT[:].rearrange("p (c t) -> p c t", c=8)[:, :, jt * 128:(jt + 1) * 128]
                    E("V", lambda q, b=b, dst=dst: q.tensor_copy(
                        out=dst, in_=pt[b][:].rearrange("p (c t) -> p c t", c=8)),
                      reads=[R(pt[b])], writes=[R(xnT)])

        def residual_epilogue(jt, g_src, g_nm, dst_dram, dst_nm, stile, xtile, xsrc, xsrc_nm):
            E("qS", lambda q, jt=jt: q.dma_start(
                out=stile[:], in_=s_scr.ap()[jt * 128:(jt + 1) * 128, :]),
              reads=["s_scr"], writes=[R(stile)])
            E("qS", lambda q, jt=jt: q.dma_start(
                out=xtile[:], in_=xsrc.ap()[jt * 128:(jt + 1) * 128, :]),
              reads=[xsrc_nm], writes=[R(xtile)])
            E("V", lambda q: q.tensor_mul(out=g_src[:], in0=g_src[:], in1=stile[:]),
              reads=[g_nm, R(stile)], writes=[g_nm])
            E("V", lambda q: q.tensor_add(out=g_src[:], in0=g_src[:], in1=xtile[:]),
              reads=[g_nm, R(xtile)], writes=[g_nm])
            E("qA", lambda q, jt=jt: q.dma_start(
                out=dst_dram.ap()[jt * 128:(jt + 1) * 128, :], in_=g_src[:]),
              reads=[g_nm], writes=[dst_nm])

        def layer0(xsrc, xsrc_nm, dst_dram, dst_nm):
            xnT = ctx.sbuf("xnT0", [128, 8 * T], BF16)
            layernorm_phase(xnT, xsrc, xsrc_nm, write_xn_f=False)

            with ctx.scope():
                sc = ctx.sbuf("l0_sc", [128, T], F32)
                outT = ctx.sbuf("l0_outT", [128, 8 * T], BF16)
                with ctx.scope():
                    td = ctx.sbuf("l0_td", [128, 8], F32)
                    eu = ctx.sbuf("l0_eu", [128, 8], F32)
                    dec = ctx.sbuf("l0_dec", [128, 8], F32)
                    E("qS", lambda q: q.dma_start(out=td[:], in_=din["l0_td"].ap()),
                      writes=[R(td)])
                    E("qS", lambda q: q.dma_start(out=eu[:], in_=din["l0_tf"].ap()),
                      writes=[R(eu)])
                    E("S", lambda q: q.activation(out=dec[:], in_=td[:], func=AF.Exp),
                      reads=[R(td)], writes=[R(dec)])
                    E("S", lambda q: q.activation(out=dec[:], in_=dec[:], func=AF.Exp, scale=-1.0),
                      reads=[R(dec)], writes=[R(dec)])
                    E("S", lambda q: q.activation(out=eu[:], in_=eu[:], func=AF.Exp),
                      reads=[R(eu)], writes=[R(eu)])

                    wk = ctx.sbuf("l0_wk", [128, 8 * D], BF16)
                    wv = ctx.sbuf("l0_wv", [128, 8 * D], BF16)
                    wr = ctx.sbuf("l0_wr", [128, 8 * D], BF16)
                    for nm, sb in (("l0_WkT", wk), ("l0_WvT", wv), ("l0_WrT", wr)):
                        E("qP", lambda q, nm=nm, sb=sb: q.dma_start(
                            out=sb[:].rearrange("p (c d) -> p c d", c=8),
                            in_=din[nm].ap().rearrange("(c p) d -> p c d", p=128)),
                          writes=[R(sb)])

                    H = T if T <= 128 else T // 2
                    kb = ctx.sbuf("l0_kb", [128, H], F32)
                    rb5 = ctx.sbuf("l0_rb5", [128, 512], F32)
                    wk5 = ctx.sbuf("l0_wk5", [128, 512], F32)
                    E("V", lambda q: q.memset(sc[:], 0.0), writes=[R(sc)])
                    ps = [ctx.psum("l0_ps", [128, 512], F32) for _ in range(4)]
                    for c in range(8):
                        for h in range(T // H):
                            base = h * H
                            chunks = [(s, min(s + 512, H)) for s in range(0, H, 512)]
                            for nt, (lo, hi) in enumerate(chunks):
                                w_ = hi - lo
                                pbk, pbv = (2 * nt) % 4, (2 * nt + 1) % 4
                                lhs = [wk[:, kc * D + c * 128: kc * D + (c + 1) * 128]
                                       for kc in range(8)]
                                rhs = [xnT[:, kc * T + base + lo: kc * T + base + hi]
                                       for kc in range(8)]
                                mm_tile(ps[pbk][:, 0:w_], R(ps[pbk]), lhs, rhs, [R(wk), R(xnT)])
                                E("S", lambda q, pbk=pbk, lo=lo, hi=hi, w_=w_: q.activation(
                                    out=kb[:, lo:hi], in_=ps[pbk][:, 0:w_], func=AF.Copy),
                                  reads=[R(ps[pbk])], writes=[R(kb)])
                                lhsv = [wv[:, kc * D + c * 128: kc * D + (c + 1) * 128]
                                        for kc in range(8)]
                                mm_tile(ps[pbv][:, 0:w_], R(ps[pbv]), lhsv, rhs, [R(wv), R(xnT)])
                                E("V", lambda q, pbv=pbv, lo=lo, hi=hi, w_=w_: q.tensor_mul(
                                    out=kb[:, lo:hi], in0=kb[:, lo:hi], in1=ps[pbv][:, 0:w_]),
                                  reads=[R(ps[pbv]), R(kb)], writes=[R(kb)])
                            if h == 0:
                                E("V", lambda q, c=c: q.tensor_tensor_scan(
                                    out=sc[:, 0:H], data0=dec[:, c:c + 1].broadcast_to([128, H]),
                                    data1=kb[:], initial=0.0, op0=ALU.mult, op1=ALU.add),
                                  reads=[R(kb), R(dec)], writes=[R(sc)])
                            else:
                                E("V", lambda q, c=c, base=base: q.tensor_tensor_scan(
                                    out=sc[:, base:base + H],
                                    data0=dec[:, c:c + 1].broadcast_to([128, H]),
                                    data1=kb[:], initial=sc[:, base - 1:base],
                                    op0=ALU.mult, op1=ALU.add),
                                  reads=[R(kb), R(dec), R(sc)], writes=[R(sc)])
                            for nt, (lo, hi) in enumerate(chunks):
                                w_ = hi - lo
                                gl = base + lo
                                pbr = nt % 4
                                lhsr = [wr[:, kc * D + c * 128: kc * D + (c + 1) * 128]
                                        for kc in range(8)]
                                rhs = [xnT[:, kc * T + gl: kc * T + gl + w_] for kc in range(8)]
                                mm_tile(ps[pbr][:, 0:w_], R(ps[pbr]), lhsr, rhs, [R(wr), R(xnT)])
                                E("S", lambda q, pbr=pbr, w_=w_: q.activation(
                                    out=rb5[:, 0:w_], in_=ps[pbr][:, 0:w_], func=AF.Sigmoid),
                                  reads=[R(ps[pbr])], writes=[R(rb5)])
                                if gl == 0:
                                    E("V", lambda q, c=c, lo=lo, hi=hi, gl=gl: q.scalar_tensor_tensor(
                                        out=wk5[:, 1:hi - lo], in0=kb[:, lo + 1:hi],
                                        scalar=eu[:, c:c + 1], in1=sc[:, gl:gl + hi - lo - 1],
                                        op0=ALU.mult, op1=ALU.add),
                                      reads=[R(kb), R(sc), R(eu)], writes=[R(wk5)])
                                    E("V", lambda q, c=c: q.tensor_scalar(
                                        out=wk5[:, 0:1], in0=kb[:, 0:1], scalar1=eu[:, c:c + 1],
                                        scalar2=None, op0=ALU.mult),
                                      reads=[R(kb), R(eu)], writes=[R(wk5)])
                                else:
                                    E("V", lambda q, c=c, lo=lo, w_=w_, gl=gl: q.scalar_tensor_tensor(
                                        out=wk5[:, 0:w_], in0=kb[:, lo:lo + w_],
                                        scalar=eu[:, c:c + 1], in1=sc[:, gl - 1:gl + w_ - 1],
                                        op0=ALU.mult, op1=ALU.add),
                                      reads=[R(kb), R(sc), R(eu)], writes=[R(wk5)])
                                E("V", lambda q, c=c, gl=gl, w_=w_: q.tensor_tensor(
                                    out=outT[:, c * T + gl: c * T + gl + w_], in0=rb5[:, 0:w_],
                                    in1=wk5[:, 0:w_], op=ALU.mult),
                                  reads=[R(rb5), R(wk5)], writes=[R(outT)])
                with ctx.scope():
                    wo = ctx.sbuf("l0_wo", [128, 8 * D], BF16)
                    E("qP", lambda q: q.dma_start(
                        out=wo[:].rearrange("p (c d) -> p c d", c=8),
                        in_=din["l0_WoT"].ap().rearrange("(c p) d -> p c d", p=128)),
                      writes=[R(wo)])
                    sbuf_s = ctx.sbuf("l0_sout", [128, D], F32)
                    po = [ctx.psum("l0_po", [128, 512], F32) for _ in range(2)]
                    for jt in range(NT):
                        for nh in range(2):
                            lhs = [outT[:, c * T + jt * 128: c * T + (jt + 1) * 128]
                                   for c in range(8)]
                            rhs = [wo[:, c * D + nh * 512: c * D + (nh + 1) * 512]
                                   for c in range(8)]
                            mm_tile(po[nh][:], R(po[nh]), lhs, rhs, [R(outT), R(wo)])
                            E("S", lambda q, nh=nh: q.activation(
                                out=sbuf_s[:, nh * 512:(nh + 1) * 512], in_=po[nh][:],
                                func=AF.Copy),
                              reads=[R(po[nh])], writes=[R(sbuf_s)])
                        E("qA", lambda q, jt=jt: q.dma_start(
                            out=s_scr.ap()[jt * 128:(jt + 1) * 128, :], in_=sbuf_s[:]),
                          reads=[R(sbuf_s)], writes=["s_scr"])

            wte = ctx.sbuf("l0_wte", [128, NT * 16], F32)
            hte = ctx.sbuf("l0_hte", [128, NT * 16], F32)
            with ctx.scope():
                gw = ctx.sbuf("l0_gw", [128, 8 * 16], BF16)
                E("qP", lambda q: q.dma_start(
                    out=gw[:].rearrange("p (c n) -> p c n", c=8),
                    in_=din["l0_gwT"].ap().rearrange("(c p) n -> p c n", p=128)),
                  writes=[R(gw)])
                pg = ctx.psum("l0_pg", [128, 512], F32)
                lg = ctx.sbuf("l0_lg", [128, 16], F32)
                mx = ctx.sbuf("l0_mx", [128, 1], F32)
                sm = ctx.sbuf("l0_sm", [128, 1], F32)
                eq1 = ctx.sbuf("l0_eq1", [128, 16], F32)
                msk = ctx.sbuf("l0_msk", [128, 16], F32)
                for jt in range(NT):
                    lhs = [xnT[:, c * T + jt * 128: c * T + (jt + 1) * 128] for c in range(8)]
                    rhs = [gw[:, c * 16:(c + 1) * 16] for c in range(8)]
                    mm_tile(pg[:, 0:16], R(pg), lhs, rhs, [R(xnT), R(gw)])
                    E("V", lambda q: q.tensor_reduce(out=mx[:], in_=pg[:, 0:16], axis=AX.X,
                                                     op=ALU.max),
                      reads=[R(pg)], writes=[R(mx)])
                    E("V", lambda q: q.tensor_scalar(out=mx[:], in0=mx[:], scalar1=-1.0,
                                                     scalar2=None, op0=ALU.mult),
                      reads=[R(mx)], writes=[R(mx)])
                    E("S", lambda q: q.activation(out=lg[:], in_=pg[:, 0:16], func=AF.Exp,
                                                  bias=mx[:, 0:1], scale=1.0,
                                                  accum_out=sm[:, 0:1]),
                      reads=[R(pg), R(mx)], writes=[R(lg), R(sm)])
                    E("V", lambda q: q.reciprocal(out=sm[:], in_=sm[:]),
                      reads=[R(sm)], writes=[R(sm)])
                    E("V", lambda q: q.tensor_scalar(out=lg[:], in0=lg[:], scalar1=sm[:, 0:1],
                                                     scalar2=None, op0=ALU.mult),
                      reads=[R(lg), R(sm)], writes=[R(lg)])
                    E("V", lambda q: q.tensor_reduce(out=mx[:], in_=lg[:], axis=AX.X, op=ALU.max),
                      reads=[R(lg)], writes=[R(mx)])
                    E("V", lambda q: q.tensor_scalar(out=eq1[:], in0=lg[:], scalar1=mx[:, 0:1],
                                                     scalar2=None, op0=ALU.is_ge),
                      reads=[R(lg), R(mx)], writes=[R(eq1)])
                    E("V", lambda q: q.scalar_tensor_tensor(out=msk[:], in0=eq1[:], scalar=-2.0,
                                                            in1=lg[:], op0=ALU.mult, op1=ALU.add),
                      reads=[R(eq1), R(lg)], writes=[R(msk)])
                    E("V", lambda q: q.tensor_reduce(out=mx[:], in_=msk[:], axis=AX.X, op=ALU.max),
                      reads=[R(msk)], writes=[R(mx)])
                    E("V", lambda q, jt=jt: q.tensor_scalar(
                        out=hte[:, jt * 16:(jt + 1) * 16], in0=msk[:], scalar1=mx[:, 0:1],
                        scalar2=None, op0=ALU.is_ge),
                      reads=[R(msk), R(mx)], writes=[R(hte)])
                    E("V", lambda q, jt=jt: q.tensor_add(
                        out=hte[:, jt * 16:(jt + 1) * 16], in0=hte[:, jt * 16:(jt + 1) * 16],
                        in1=eq1[:]),
                      reads=[R(hte), R(eq1)], writes=[R(hte)])
                    E("V", lambda q, jt=jt: q.tensor_mul(
                        out=wte[:, jt * 16:(jt + 1) * 16], in0=lg[:],
                        in1=hte[:, jt * 16:(jt + 1) * 16]),
                      reads=[R(lg), R(hte)], writes=[R(wte)])

            with ctx.scope():
                TBLK = min(1024, T)
                NBLK = T // TBLK
                NJB = TBLK // 128
                yb = ctx.sbuf("l0_yb", [128, TBLK * 8], F32)
                abuf = ctx.sbuf("l0_abuf", [128, 1023], F32)
                hbuf = ctx.sbuf("l0_hbuf", [128, 1024], F32)
                hbf = ctx.sbuf("l0_hbf", [128, 1024], BF16)
                hT = ctx.sbuf("l0_hT", [128, 8 * 128], BF16)
                oeb = ctx.sbuf("l0_oeb", [128, 1024], F32)
                zb = ctx.sbuf("l0_zb", [128, 1024], F32)
                sqs = ctx.sbuf("l0_sqs", [128, 1024], F32)
                ssc = ctx.sbuf("l0_ssc", [128, 1], F32)
                tcol = ctx.sbuf("l0_tcol", [128, 1], F32)
                p1 = ctx.psum("l0_p1", [128, 1024], F32)
                p2 = ctx.psum("l0_p2", [128, 1024], F32)
                p3 = ctx.psum("l0_p3", [128, 1024], F32)
                pt4 = ctx.psum("l0_pt4", [128, 1024], BF16)
                gbuf = ctx.sbuf("l0_gbuf", [128, 1024], F32)
                stile = ctx.sbuf("l0_stile", [128, 1024], F32)
                xtile = ctx.sbuf("l0_xtile", [128, 1024], F32)

                def lmlp(jt, n1, w1, b1, w3, b3, w2, b2):
                    nh = n1 + 1
                    nkh = nh // 128
                    ncol = [(0, min(512, n1))] + ([(512, n1)] if n1 > 512 else [])
                    lhs = [xnT[:, c * T + jt * 128: c * T + (jt + 1) * 128] for c in range(8)]
                    for (lo, hi) in ncol:
                        rhs = [w1[:, c * n1 + lo: c * n1 + hi] for c in range(8)]
                        mm_tile(p1[:, lo:hi], R(p1), lhs, rhs, [R(xnT), R(w1)], last=False)
                        E("T", lambda q, lo=lo, hi=hi: q.matmul(
                            p1[:, lo:hi], onesb[:], b1[:, lo:hi], start=False, stop=True, skip_group_check=True),
                          reads=[R(onesb), R(b1)], writes=[R(p1)])
                        E("S", lambda q, lo=lo, hi=hi: q.activation(
                            out=abuf[:, lo:hi], in_=p1[:, lo:hi], func=AF.Silu),
                          reads=[R(p1)], writes=[R(abuf)])
                        rhs = [w3[:, c * n1 + lo: c * n1 + hi] for c in range(8)]
                        mm_tile(p2[:, lo:hi], R(p2), lhs, rhs, [R(xnT), R(w3)], last=False)
                        E("T", lambda q, lo=lo, hi=hi: q.matmul(
                            p2[:, lo:hi], onesb[:], b3[:, lo:hi], start=False, stop=True, skip_group_check=True),
                          reads=[R(onesb), R(b3)], writes=[R(p2)])
                        E("V", lambda q, lo=lo, hi=hi: q.tensor_mul(
                            out=hbuf[:, 1 + lo:1 + hi], in0=abuf[:, lo:hi], in1=p2[:, lo:hi]),
                          reads=[R(abuf), R(p2)], writes=[R(hbuf)])
                    E("S", lambda q, n1=n1: q.activation(out=sqs[:, 0:n1], in_=hbuf[:, 1:1 + n1],
                                                         func=AF.Square, accum_out=ssc[:, 0:1]),
                      reads=[R(hbuf)], writes=[R(sqs), R(ssc)])
                    E("S", lambda q: q.activation(out=hbuf[:, 0:1], in_=ssc[:], func=AF.Sqrt,
                                                  bias=C(1.0), scale=1.0),
                      reads=[R(ssc), R(cc)], writes=[R(hbuf)])
                    E("V", lambda q, nh=nh: q.tensor_copy(out=hbf[:, 0:nh], in_=hbuf[:, 0:nh]),
                      reads=[R(hbuf)], writes=[R(hbf)])
                    for c in range(nkh):
                        E("T", lambda q, c=c: q.transpose(
                            pt4[:, c * 128:(c + 1) * 128], hbf[:, c * 128:(c + 1) * 128], ident[:]),
                          reads=[R(hbf), R(ident)], writes=[R(pt4)])
                    E("V", lambda q, nkh=nkh: q.tensor_copy(out=hT[:, 0:nkh * 128],
                                                            in_=pt4[:, 0:nkh * 128]),
                      reads=[R(pt4)], writes=[R(hT)])
                    lhs2 = [hT[:, c * 128:(c + 1) * 128] for c in range(nkh)]
                    for (lo, hi) in ((0, 512), (512, 1023)):
                        rhs = [w2[:, c * 1023 + lo: c * 1023 + hi] for c in range(nkh)]
                        mm_tile(p3[:, lo:hi], R(p3), lhs2, rhs, [R(hT), R(w2)], last=False)
                        E("T", lambda q, lo=lo, hi=hi: q.matmul(
                            p3[:, lo:hi], onesb[:], b2[:, lo:hi], start=False, stop=True, skip_group_check=True),
                          reads=[R(onesb), R(b2)], writes=[R(p3)])
                    E("S", lambda q: q.activation(out=oeb[:, 1:1024], in_=p3[:, 0:1023],
                                                  func=AF.Copy),
                      reads=[R(p3)], writes=[R(oeb)])
                    E("S", lambda q: q.activation(out=sqs[:, 0:1023], in_=p3[:, 0:1023],
                                                  func=AF.Square, accum_out=ssc[:, 0:1]),
                      reads=[R(p3)], writes=[R(sqs), R(ssc)])
                    E("S", lambda q: q.activation(out=oeb[:, 0:1], in_=ssc[:], func=AF.Sqrt,
                                                  bias=C(1.0), scale=1.0),
                      reads=[R(ssc), R(cc)], writes=[R(oeb)])

                def lresnet_update(y_ap, w_ap, wdeps, hit_ap, hitdeps):
                    E("V", lambda q: q.scalar_tensor_tensor(out=zb[:], in0=oeb[:], scalar=w_ap,
                                                            in1=y_ap, op0=ALU.mult, op1=ALU.add),
                      reads=[R(oeb), R(yb)] + wdeps, writes=[R(zb)])
                    E("S", lambda q: q.activation(out=sqs[:], in_=zb[:], func=AF.Square,
                                                  accum_out=ssc[:, 0:1]),
                      reads=[R(zb)], writes=[R(sqs), R(ssc)])
                    E("V", lambda q: q.scalar_tensor_tensor(out=tcol[:], in0=sqs[:, 0:1],
                                                            scalar=2.0, in1=ssc[:],
                                                            op0=ALU.mult, op1=ALU.subtract),
                      reads=[R(sqs), R(ssc)], writes=[R(tcol)])
                    E("V", lambda q: q.tensor_scalar(out=tcol[:], in0=tcol[:], scalar1=1e-6,
                                                     scalar2=None, op0=ALU.max),
                      reads=[R(tcol)], writes=[R(tcol)])
                    E("S", lambda q: q.activation(out=tcol[:], in_=tcol[:], func=AF.Sqrt),
                      reads=[R(tcol)], writes=[R(tcol)])
                    E("V", lambda q: q.reciprocal(out=tcol[:], in_=tcol[:]),
                      reads=[R(tcol)], writes=[R(tcol)])
                    E("V", lambda q: q.tensor_scalar(out=zb[:], in0=zb[:], scalar1=tcol[:, 0:1],
                                                     scalar2=None, op0=ALU.mult),
                      reads=[R(zb), R(tcol)], writes=[R(zb)])
                    if hit_ap is None:
                        return
                    E("V", lambda q: q.tensor_sub(out=zb[:], in0=zb[:], in1=y_ap),
                      reads=[R(zb), R(yb)], writes=[R(zb)])
                    E("V", lambda q: q.scalar_tensor_tensor(out=y_ap, in0=zb[:], scalar=hit_ap,
                                                            in1=y_ap, op0=ALU.mult, op1=ALU.add),
                      reads=[R(zb), R(yb)] + hitdeps, writes=[R(yb)])

                for blk in range(NBLK):
                    E("V", lambda q: q.memset(yb[:], 0.0), writes=[R(yb)])
                    E("V", lambda q: q.memset(
                        yb[:].rearrange("p (j d) -> p j d", j=NJB)[:, :, 0:1], 1.0),
                      reads=[R(yb)], writes=[R(yb)])
                    with ctx.scope():
                        we1 = [ctx.sbuf("l0_we1", [128, 8 * 511], BF16) for _ in range(2)]
                        we3 = [ctx.sbuf("l0_we3", [128, 8 * 511], BF16) for _ in range(2)]
                        we2 = [ctx.sbuf("l0_we2", [128, 4 * 1023], BF16) for _ in range(2)]
                        be1 = [ctx.sbuf("l0_be1", [1, 511], BF16) for _ in range(2)]
                        be3 = [ctx.sbuf("l0_be3", [1, 511], BF16) for _ in range(2)]
                        be2 = [ctx.sbuf("l0_be2", [1, 1023], BF16) for _ in range(2)]

                        def load_expert(e, b):
                            for nm, sb, c_ in (("l0_We1T", we1[b], 8), ("l0_We3T", we3[b], 8),
                                               ("l0_We2T", we2[b], 4)):
                                E("qP", lambda q, nm=nm, sb=sb, c_=c_, e=e: q.dma_start(
                                    out=sb[:].rearrange("p (c n) -> p c n", c=c_),
                                    in_=din[nm].ap()[e].rearrange("(c p) n -> p c n", p=128)),
                                  writes=[R(sb)])
                            for nm, sb in (("l0_be1", be1[b]), ("l0_be3", be3[b]),
                                           ("l0_be2", be2[b])):
                                E("qP", lambda q, nm=nm, sb=sb, e=e: q.dma_start(
                                    out=sb[:], in_=din[nm].ap()[e:e + 1, :]),
                                  writes=[R(sb)])

                        load_expert(0, 0)
                        for e in range(16):
                            b = e % 2
                            if e < 15:
                                load_expert(e + 1, 1 - b)
                            for jl in range(NJB):
                                jt = blk * NJB + jl
                                lmlp(jt, 511, we1[b], be1[b], we3[b], be3[b], we2[b], be2[b])
                                y_ap = yb[:, jl * 1024:(jl + 1) * 1024]
                                lresnet_update(y_ap, wte[:, jt * 16 + e: jt * 16 + e + 1],
                                               [R(wte)], hte[:, jt * 16 + e: jt * 16 + e + 1],
                                               [R(hte)])
                    with ctx.scope():
                        ws1 = ctx.sbuf("l0_ws1", [128, 8 * 1023], BF16)
                        ws3 = ctx.sbuf("l0_ws3", [128, 8 * 1023], BF16)
                        ws2 = ctx.sbuf("l0_ws2", [128, 8 * 1023], BF16)
                        bs1 = ctx.sbuf("l0_bs1", [1, 1023], BF16)
                        bs3 = ctx.sbuf("l0_bs3", [1, 1023], BF16)
                        bs2 = ctx.sbuf("l0_bs2", [1, 1023], BF16)
                        for nm, sb in (("l0_Ws1T", ws1), ("l0_Ws3T", ws3), ("l0_Ws2T", ws2)):
                            E("qP", lambda q, nm=nm, sb=sb: q.dma_start(
                                out=sb[:].rearrange("p (c n) -> p c n", c=8),
                                in_=din[nm].ap().rearrange("(c p) n -> p c n", p=128)),
                              writes=[R(sb)])
                        for nm, sb in (("l0_bs1", bs1), ("l0_bs3", bs3), ("l0_bs2", bs2)):
                            E("qP", lambda q, nm=nm, sb=sb: q.dma_start(
                                out=sb[:], in_=din[nm].ap()),
                              writes=[R(sb)])
                        for jl in range(NJB):
                            jt = blk * NJB + jl
                            lmlp(jt, 1023, ws1, bs1, ws3, bs3, ws2, bs2)
                            y_ap = yb[:, jl * 1024:(jl + 1) * 1024]
                            lresnet_update(y_ap, 1.0, [], None, None)
                            E("S", lambda q: q.activation(out=gbuf[:], in_=zb[:], func=AF.Silu),
                              reads=[R(zb)], writes=[R(gbuf)])
                            residual_epilogue(jt, gbuf, R(gbuf), dst_dram, dst_nm, stile,
                                              xtile, xsrc, xsrc_nm)


        onesf = ctx.sbuf("onesf", [1, 128], F32)
        E("qS", lambda q: q.dma_start(out=onesf[:], in_=din["ones_row"].ap()),
          writes=[R(onesf)])
        pbc_ps = ctx.psum("pbc_ps", [128, 512], F32)
        pm_row = ctx.sbuf("pm_row", [1, 128], F32)
        pm_val = ctx.sbuf("pm_val", [1, 1], F32)
        _pm_n = [0]

        def pbcast(dst_ap, dstnm, src_ap, srcnms, n=1):
            E("T", lambda q: q.matmul(pbc_ps[:, 0:n], onesf[:], src_ap, start=True,
                                      stop=True, skip_group_check=True),
              reads=list(srcnms) + [R(onesf)], writes=[R(pbc_ps)])
            E("V", lambda q: q.tensor_copy(out=dst_ap, in_=pbc_ps[:, 0:n]),
              reads=[R(pbc_ps)], writes=[dstnm])

        def pmax_col(colap, colnm):
            _pm_n[0] += 1
            scr = dram(f"pm_scr{_pm_n[0]}", [128])
            snm = f"pm_scr{_pm_n[0]}"
            E("qP", lambda q: q.dma_start(
                out=scr.ap().rearrange("(p o) -> p o", o=1), in_=colap),
              reads=[colnm], writes=[snm])
            E("qS", lambda q: q.dma_start(
                out=pm_row[:], in_=scr.ap().rearrange("(o p) -> o p", o=1)),
              reads=[snm], writes=[R(pm_row)])
            E("V", lambda q: q.tensor_reduce(out=pm_val[:], in_=pm_row[:], axis=AX.X,
                                             op=ALU.max),
              reads=[R(pm_row)], writes=[R(pm_val)])
            pbcast(colap, colnm, pm_val[0:1, 0:1], [R(pm_val)], 1)

        def armax_global(col, colnm, bname):
            pmax_col(col[:, 0:1], colnm)
            bi = dram(bname + "_i", [1, 1])
            bo = dram(bname + "_o", [1, 1])
            E("qP", lambda q: q.dma_start(out=bi.ap(), in_=col[0:1, 0:1]),
              reads=[colnm], writes=[bname + "_i"])
            E("P", lambda q: q.collective_compute(
                "AllReduce", ALU.max, replica_groups=[list(range(n_cores))],
                ins=[bi.ap()], outs=[bo.ap()]),
              reads=[bname + "_i"], writes=[bname + "_o"])
            E("qS", lambda q: q.dma_start(out=pm_val[:], in_=bo.ap()),
              reads=[bname + "_o"], writes=[R(pm_val)])
            pbcast(col[:, 0:1], colnm, pm_val[0:1, 0:1], [R(pm_val)], 1)

        def round_ste(dst, dstnm, src_ap, srcnms, tmp, w):
            E("V", lambda q: q.tensor_scalar(out=dst[:, 0:w], in0=src_ap, scalar1=2.0 ** 23,
                                             scalar2=2.0 ** 23, op0=ALU.add,
                                             op1=ALU.subtract),
              reads=list(srcnms), writes=[dstnm])

        def qdq(buf, bufnm, tmp, rs_col, s_col, scnms, qmin, qmax, w):
            E("V", lambda q: q.tensor_scalar(out=buf[:, 0:w], in0=buf[:, 0:w],
                                             scalar1=rs_col, scalar2=None, op0=ALU.mult),
              reads=[bufnm] + list(scnms), writes=[bufnm])
            round_ste(buf, bufnm, buf[:, 0:w], [bufnm], tmp, w)
            E("V", lambda q: q.tensor_scalar(out=buf[:, 0:w], in0=buf[:, 0:w], scalar1=qmin,
                                             scalar2=qmax, op0=ALU.max, op1=ALU.min),
              reads=[bufnm], writes=[bufnm])
            E("V", lambda q: q.tensor_scalar(out=buf[:, 0:w], in0=buf[:, 0:w],
                                             scalar1=s_col, scalar2=None, op0=ALU.mult),
              reads=[bufnm] + list(scnms), writes=[bufnm])

        def e4m3fn(buf, bufnm, t1, tf8a, tf8b, w):
            E("V", lambda q: q.tensor_scalar(out=t1[:, 0:w], in0=buf[:, 0:w], scalar1=0.5,
                                             scalar2=None, op0=ALU.mult),
              reads=[bufnm], writes=[R(t1)])
            E("P", lambda q: q.tensor_copy(out=tf8a[:, 0:w], in_=t1[:, 0:w]),
              reads=[R(t1)], writes=[R(tf8a)])
            E("V", lambda q: q.tensor_scalar(out=t1[:, 0:w], in0=buf[:, 0:w], scalar1=1.0,
                                             scalar2=None, op0=ALU.min),
              reads=[bufnm], writes=[R(t1)])
            E("P", lambda q: q.tensor_copy(out=tf8b[:, 0:w], in_=t1[:, 0:w]),
              reads=[R(t1)], writes=[R(tf8b)])
            E("V", lambda q: q.tensor_scalar(out=t1[:, 0:w], in0=buf[:, 0:w], scalar1=1.0,
                                             scalar2=None, op0=ALU.is_ge),
              reads=[bufnm], writes=[R(t1)])
            E("P", lambda q: q.tensor_copy(out=buf[:, 0:w], in_=tf8a[:, 0:w]),
              reads=[R(tf8a)], writes=[bufnm])
            E("V", lambda q: q.tensor_scalar(out=buf[:, 0:w], in0=buf[:, 0:w], scalar1=2.0,
                                             scalar2=None, op0=ALU.mult),
              reads=[bufnm], writes=[bufnm])
            E("V", lambda q: q.tensor_sub(out=buf[:, 0:w], in0=buf[:, 0:w], in1=tf8b[:, 0:w]),
              reads=[bufnm, R(tf8b)], writes=[bufnm])
            E("V", lambda q: q.tensor_mul(out=buf[:, 0:w], in0=buf[:, 0:w], in1=t1[:, 0:w]),
              reads=[bufnm, R(t1)], writes=[bufnm])
            E("V", lambda q: q.tensor_add(out=buf[:, 0:w], in0=buf[:, 0:w], in1=tf8b[:, 0:w]),
              reads=[bufnm, R(tf8b)], writes=[bufnm])

        def digitize_fp4(a, anm, acc1, acc2, acc3, w):
            E("V", lambda q: q.tensor_scalar(out=acc1[:, 0:w], in0=a[:, 0:w], scalar1=0.25,
                                             scalar2=None, op0=ALU.is_ge),
              reads=[anm], writes=[R(acc1)])
            for mid in (0.75, 1.25, 1.75):
                E("V", lambda q, mid=mid: q.scalar_tensor_tensor(
                    out=acc1[:, 0:w], in0=a[:, 0:w], scalar=mid, in1=acc1[:, 0:w],
                    op0=ALU.is_ge, op1=ALU.add),
                  reads=[anm, R(acc1)], writes=[R(acc1)])
            E("V", lambda q: q.tensor_scalar(out=acc2[:, 0:w], in0=a[:, 0:w], scalar1=2.5,
                                             scalar2=None, op0=ALU.is_ge),
              reads=[anm], writes=[R(acc2)])
            E("V", lambda q: q.scalar_tensor_tensor(out=acc2[:, 0:w], in0=a[:, 0:w],
                                                    scalar=3.5, in1=acc2[:, 0:w],
                                                    op0=ALU.is_ge, op1=ALU.add),
              reads=[anm, R(acc2)], writes=[R(acc2)])
            E("V", lambda q: q.tensor_scalar(out=acc3[:, 0:w], in0=a[:, 0:w], scalar1=5.0,
                                             scalar2=2.0, op0=ALU.is_ge, op1=ALU.mult),
              reads=[anm], writes=[R(acc3)])
            E("V", lambda q: q.scalar_tensor_tensor(out=a[:, 0:w], in0=acc1[:, 0:w],
                                                    scalar=0.5, in1=acc2[:, 0:w],
                                                    op0=ALU.mult, op1=ALU.add),
              reads=[R(acc1), R(acc2)], writes=[anm])
            E("V", lambda q: q.tensor_add(out=a[:, 0:w], in0=a[:, 0:w], in1=acc3[:, 0:w]),
              reads=[anm, R(acc3)], writes=[anm])

        def layer1(xsrc, xsrc_nm, dst_dram, dst_nm):
            oT_scr = dram("l1_oT", [T, 2048], BF16)

            with ctx.scope():
                xnT = ctx.sbuf("xnT1", [128, 8 * T], BF16)
                layernorm_phase(xnT, xsrc, xsrc_nm, write_xn_f=True)
                onesc = ctx.sbuf("onesc", [128, 1], BF16)
                E("V", lambda q: q.memset(onesc[:], 1.0), writes=[R(onesc)])
                wqt = ctx.sbuf("l1_wqt", [128, 8 * 2048], BF16)
                E("qP", lambda q: q.dma_start(
                    out=wqt[:].rearrange("p (c n) -> p c n", c=8),
                    in_=din["l1_WqT"].ap().rearrange("(c p) n -> p c n", p=128)),
                  writes=[R(wqt)])
                wkvd = ctx.sbuf("l1_wkvd", [128, 8 * 64], BF16)
                E("qP", lambda q: q.dma_start(
                    out=wkvd[:].rearrange("p (c n) -> p c n", c=8),
                    in_=din["l1_WkvdT"].ap().rearrange("(c p) n -> p c n", p=128)),
                  writes=[R(wkvd)])
                wku = ctx.sbuf("l1_wku", [64, 2048], BF16)
                wvu = ctx.sbuf("l1_wvu", [64, 2048], BF16)
                E("qP", lambda q: q.dma_start(out=wku[:], in_=din["l1_WkuT"].ap()),
                  writes=[R(wku)])
                E("qP", lambda q: q.dma_start(out=wvu[:], in_=din["l1_WvuT"].ap()),
                  writes=[R(wvu)])

                latT = ctx.sbuf("l1_latT", [64, T], BF16)
                psA = [ctx.psum("l1_psA", [128, 512], F32) for _ in range(2)]
                nch = [(s, min(s + 512, T)) for s in range(0, T, 512)]
                for nt, (lo, hi) in enumerate(nch):
                    pb = nt % 2
                    w_ = hi - lo
                    lhs = [wkvd[:, kc * 64:(kc + 1) * 64] for kc in range(8)]
                    rhs = [xnT[:, kc * T + lo: kc * T + hi] for kc in range(8)]
                    mm_tile(psA[pb][0:64, 0:w_], R(psA[pb]), lhs, rhs, [R(wkvd), R(xnT)])
                    E("V", lambda q, pb=pb, lo=lo, hi=hi, w_=w_: q.tensor_copy(
                        out=latT[:, lo:hi], in_=psA[pb][0:64, 0:w_]),
                      reads=[R(psA[pb])], writes=[R(latT)])

                qfm = ctx.sbuf("l1_qfm", [128, 2 * T], BF16)
                kfm = ctx.sbuf("l1_kfm", [128, 2 * T], BF16)
                vtm = ctx.sbuf("l1_vtm", [128, NT * 256], BF16)
                sqb = ctx.sbuf("l1_sqb", [128, 512], BF16)
                gk = ctx.sbuf("l1_gk", [1, 2], F32)
                gtmp = ctx.sbuf("l1_gtmp", [1, 1], F32)
                bnd = ctx.sbuf("l1_bnd", [128, 1], F32)
                expb = ctx.sbuf("l1_expb", [128, 512], BF16)
                expT = ctx.sbuf("l1_expT", [128, 512], BF16)
                sums = ctx.sbuf("l1_sums", [128, (T + 511) // 512], F32)
                rsum = ctx.sbuf("l1_rsum", [128, 1], F32)
                otile = ctx.sbuf("l1_otile", [128, 256], BF16)
                psN = ctx.psum("l1_psN", [1, 512], F32)
                psT = ctx.psum("l1_psT", [128, 1024], BF16)
                psO = ctx.psum("l1_psO", [128, 512], F32)

                for h in range(8):
                    for m in range(2):
                        for nt, (lo, hi) in enumerate(nch):
                            pb = nt % 2
                            w_ = hi - lo
                            lhs = [wqt[:, kc * 2048 + h * 256 + m * 128:
                                       kc * 2048 + h * 256 + (m + 1) * 128]
                                   for kc in range(8)]
                            rhs = [xnT[:, kc * T + lo: kc * T + hi] for kc in range(8)]
                            mm_tile(psA[pb][:, 0:w_], R(psA[pb]), lhs, rhs, [R(wqt), R(xnT)])
                            E("V", lambda q, pb=pb, m=m, lo=lo, hi=hi, w_=w_: q.tensor_copy(
                                out=qfm[:, m * T + lo: m * T + hi], in_=psA[pb][:, 0:w_]),
                              reads=[R(psA[pb])], writes=[R(qfm)])
                            mm_tile(psA[1 - pb][:, 0:w_], R(psA[1 - pb]),
                                    [wku[0:64, h * 256 + m * 128: h * 256 + (m + 1) * 128]],
                                    [latT[:, lo:hi]], [R(wku), R(latT)])
                            E("S", lambda q, pb=pb, m=m, lo=lo, hi=hi, w_=w_: q.activation(
                                out=kfm[:, m * T + lo: m * T + hi], in_=psA[1 - pb][:, 0:w_],
                                func=AF.Copy),
                              reads=[R(psA[1 - pb])], writes=[R(kfm)])
                    for jt in range(NT):
                        pb = jt % 2
                        mm_tile(psA[pb][:, 0:256], R(psA[pb]),
                                [latT[:, jt * 128:(jt + 1) * 128]],
                                [wvu[0:64, h * 256:(h + 1) * 256]], [R(latT), R(wvu)])
                        E("V", lambda q, pb=pb, jt=jt: q.tensor_copy(
                            out=vtm[:, jt * 256:(jt + 1) * 256], in_=psA[pb][:, 0:256]),
                          reads=[R(psA[pb])], writes=[R(vtm)])
                    E("V", lambda q: q.memset(gk[:], 0.0), writes=[R(gk)])
                    for src, srcnm, gi in ((qfm, R(qfm), 0), (kfm, R(kfm), 1)):
                        for nt, (lo, hi) in enumerate(nch):
                            w_ = hi - lo
                            E("S", lambda q, src=src, lo=lo, hi=hi: q.activation(
                                out=sqb[:, 0:hi - lo], in_=src[:, lo:hi], func=AF.Square),
                              reads=[srcnm], writes=[R(sqb)])
                            E("T", lambda q, w_=w_: q.matmul(psN[:, 0:w_], onesc[:],
                                                             sqb[:, 0:w_], start=True,
                                                             stop=False),
                              reads=[R(onesc), R(sqb)], writes=[R(psN)])
                            E("S", lambda q, src=src, lo=lo, hi=hi: q.activation(
                                out=sqb[:, 0:hi - lo], in_=src[:, T + lo: T + hi],
                                func=AF.Square),
                              reads=[srcnm], writes=[R(sqb)])
                            E("T", lambda q, w_=w_: q.matmul(psN[:, 0:w_], onesc[:],
                                                             sqb[:, 0:w_], start=False,
                                                             stop=True),
                              reads=[R(onesc), R(sqb)], writes=[R(psN)])
                            E("V", lambda q, w_=w_: q.tensor_reduce(
                                out=gtmp[0:1, 0:1], in_=psN[0:1, 0:w_], axis=AX.X,
                                op=ALU.max),
                              reads=[R(psN)], writes=[R(gtmp)])
                            E("V", lambda q, gi=gi: q.tensor_tensor(
                                out=gk[0:1, gi:gi + 1], in0=gk[0:1, gi:gi + 1],
                                in1=gtmp[0:1, 0:1], op=ALU.max),
                              reads=[R(gtmp), R(gk)], writes=[R(gk)])
                    E("V", lambda q: q.tensor_mul(out=gk[0:1, 0:1], in0=gk[0:1, 0:1],
                                                  in1=gk[0:1, 1:2]),
                      reads=[R(gk)], writes=[R(gk)])
                    E("S", lambda q: q.activation(out=gk[0:1, 0:1], in_=gk[0:1, 0:1],
                                                  func=AF.Sqrt, scale=1.0 / 256.0),
                      reads=[R(gk)], writes=[R(gk)])
                    E("V", lambda q: q.tensor_scalar(out=gk[0:1, 0:1], in0=gk[0:1, 0:1],
                                                     scalar1=-1.0, scalar2=None, op0=ALU.mult),
                      reads=[R(gk)], writes=[R(gk)])
                    pbcast(bnd[:, 0:1], R(bnd), gk[0:1, 0:1], [R(gk)], 1)

                    for jt in range(NT):
                        for kc, (klo, khi) in enumerate(nch):
                            kw = khi - klo
                            lhs = [qfm[:, m * T + jt * 128: m * T + (jt + 1) * 128]
                                   for m in range(2)]
                            rhs = [kfm[:, m * T + klo: m * T + khi] for m in range(2)]
                            mm_tile(psA[0][:, 0:kw], R(psA[0]), lhs, rhs, [R(qfm), R(kfm)])
                            E("S", lambda q, kc=kc, kw=kw: q.activation(
                                out=expb[:, 0:kw], in_=psA[0][:, 0:kw], func=AF.Exp,
                                bias=bnd[:, 0:1], scale=1.0 / 16.0,
                                accum_out=sums[:, kc:kc + 1]),
                              reads=[R(psA[0]), R(bnd)], writes=[R(expb), R(sums)])
                            nsub = kw // 128
                            for i in range(nsub):
                                E("T", lambda q, i=i: q.transpose(
                                    psT[:, i * 128:(i + 1) * 128],
                                    expb[:, i * 128:(i + 1) * 128], ident[:]),
                                  reads=[R(expb), R(ident)], writes=[R(psT)])
                            E("V", lambda q, kw=kw: q.tensor_copy(out=expT[:, 0:kw],
                                                                  in_=psT[:, 0:kw]),
                              reads=[R(psT)], writes=[R(expT)])
                            for i in range(nsub):
                                kt = klo // 128 + i
                                E("T", lambda q, i=i, kt=kt, kc=kc: q.matmul(
                                    psO[:, 0:256], expT[:, i * 128:(i + 1) * 128],
                                    vtm[:, kt * 256:(kt + 1) * 256],
                                    start=(kc == 0 and i == 0),
                                    stop=(kt == NT - 1), skip_group_check=True),
                                  reads=[R(expT), R(vtm)], writes=[R(psO)])
                        E("V", lambda q: q.tensor_reduce(out=rsum[:], in_=sums[:, 0:len(nch)],
                                                         axis=AX.X, op=ALU.add),
                          reads=[R(sums)], writes=[R(rsum)])
                        E("V", lambda q: q.reciprocal(out=rsum[:], in_=rsum[:]),
                          reads=[R(rsum)], writes=[R(rsum)])
                        E("S", lambda q: q.activation(out=otile[:], in_=psO[:, 0:256],
                                                      func=AF.Identity, scale=rsum[:, 0:1]),
                          reads=[R(psO), R(rsum)], writes=[R(otile)])
                        E("qA", lambda q, jt=jt, h=h: q.dma_start(
                            out=oT_scr.ap()[jt * 128:(jt + 1) * 128, h * 256:(h + 1) * 256],
                            in_=otile[:]),
                          reads=[R(otile)], writes=["l1_oT"])

            with ctx.scope():
                woT = ctx.sbuf("l1_woT", [128, 16 * 1024], BF16)
                E("qP", lambda q: q.dma_start(
                    out=woT[:].rearrange("p (c n) -> p c n", c=16),
                    in_=din["l1_WoT"].ap().rearrange("(c p) n -> p c n", p=128)),
                  writes=[R(woT)])
                oin = [ctx.sbuf("l1_oin", [128, 2048], BF16) for _ in range(2)]
                otT = ctx.sbuf("l1_otT", [128, 16 * 128], BF16)
                sb = ctx.sbuf("l1_sb", [128, 1024], F32)
                psT2 = [ctx.psum("l1_psT2", [128, 1024], BF16) for _ in range(2)]
                psS = [ctx.psum("l1_psS", [128, 512], F32) for _ in range(2)]
                for jt in range(NT):
                    b = jt % 2
                    E("qS", lambda q, jt=jt, b=b: q.dma_start(
                        out=oin[b][:], in_=oT_scr.ap()[jt * 128:(jt + 1) * 128, :]),
                      reads=["l1_oT"], writes=[R(oin[b])])
                    for g2 in range(2):
                        for i in range(8):
                            E("T", lambda q, b=b, g2=g2, i=i: q.transpose(
                                psT2[g2][:, i * 128:(i + 1) * 128],
                                oin[b][:, g2 * 1024 + i * 128: g2 * 1024 + (i + 1) * 128],
                                ident[:]),
                              reads=[R(oin[b]), R(ident)], writes=[R(psT2[g2])])
                        E("V", lambda q, g2=g2: q.tensor_copy(
                            out=otT[:, g2 * 1024:(g2 + 1) * 1024], in_=psT2[g2][:]),
                          reads=[R(psT2[g2])], writes=[R(otT)])
                    for nh in range(2):
                        lhs = [otT[:, kc * 128:(kc + 1) * 128] for kc in range(16)]
                        rhs = [woT[:, kc * 1024 + nh * 512: kc * 1024 + (nh + 1) * 512]
                               for kc in range(16)]
                        mm_tile(psS[nh][:], R(psS[nh]), lhs, rhs, [R(otT), R(woT)])
                        E("S", lambda q, nh=nh: q.activation(
                            out=sb[:, nh * 512:(nh + 1) * 512], in_=psS[nh][:], func=AF.Copy),
                          reads=[R(psS[nh])], writes=[R(sb)])
                    E("qA", lambda q, jt=jt: q.dma_start(
                        out=s_scr.ap()[jt * 128:(jt + 1) * 128, :], in_=sb[:]),
                      reads=[R(sb)], writes=["s_scr"])

            hT_scr = dram("l1_hT", [4096, T], BF16)
            with ctx.scope():
                q_sc = ctx.sbuf("l1q_sc", [128, 64], F32)
                q_am = ctx.sbuf("l1q_am", [128, 64], F32)
                q_t1 = ctx.sbuf("l1q_t1", [128, 1024], F32)
                q_b = ctx.sbuf("l1q_b", [128, 1024], F32)
                q_c = ctx.sbuf("l1q_c", [128, 1024], F32)
                q_f8a = ctx.sbuf("l1q_f8a", [128, 64], F8E4)
                q_f8b = ctx.sbuf("l1q_f8b", [128, 64], F8E4)
                sgn = ctx.sbuf("l1q_sgn", [128, 1024], F32)
                abf = ctx.sbuf("l1q_abf", [128, 1024], BF16)
                psQ = ctx.psum("l1q_psQ", [128, 1024], BF16)

                def nv_quant(buf, bufnm, lo, tw, gs_col, rgs_col, gsnms):
                    nb = tw // 16
                    bap = buf[:, lo:lo + tw]
                    E("S", lambda q: q.activation(out=sgn[:, 0:tw], in_=bap, func=AF.Sign),
                      reads=[bufnm], writes=[R(sgn)])
                    E("V", lambda q: q.tensor_reduce(
                        out=q_am[:, 0:nb], in_=bap.rearrange("p (b k) -> p b k", k=16),
                        axis=AX.X, op=ALU.max, apply_absolute_value=True),
                      reads=[bufnm], writes=[R(q_am)])
                    E("V", lambda q: q.tensor_scalar(out=q_sc[:, 0:nb], in0=q_am[:, 0:nb],
                                                     scalar1=gs_col, scalar2=1.0 / 6.0,
                                                     op0=ALU.mult, op1=ALU.mult),
                      reads=[R(q_am)] + list(gsnms), writes=[R(q_sc)])
                    E("V", lambda q: q.tensor_scalar(out=q_sc[:, 0:nb], in0=q_sc[:, 0:nb],
                                                     scalar1=0.0, scalar2=448.0,
                                                     op0=ALU.max, op1=ALU.min),
                      reads=[R(q_sc)], writes=[R(q_sc)])
                    e4m3fn(q_sc, R(q_sc), q_t1, q_f8a, q_f8b, nb)
                    E("V", lambda q: q.tensor_scalar(out=q_am[:, 0:nb], in0=q_sc[:, 0:nb],
                                                     scalar1=rgs_col, scalar2=2.0 ** -40,
                                                     op0=ALU.mult, op1=ALU.max),
                      reads=[R(q_sc)] + list(gsnms), writes=[R(q_am)])
                    E("V", lambda q: q.reciprocal(out=q_am[:, 0:nb], in_=q_am[:, 0:nb]),
                      reads=[R(q_am)], writes=[R(q_am)])
                    E("S", lambda q: q.activation(out=bap, in_=bap, func=AF.Abs),
                      reads=[bufnm], writes=[bufnm])
                    E("V", lambda q: q.tensor_tensor(
                        out=bap.rearrange("p (b k) -> p b k", k=16),
                        in0=bap.rearrange("p (b k) -> p b k", k=16),
                        in1=q_am[:, 0:nb].rearrange("p (b o) -> p b o", o=1).broadcast_to([128, nb, 16]),
                        op=ALU.mult),
                      reads=[bufnm, R(q_am)], writes=[bufnm])
                    E("V", lambda q: q.tensor_scalar(out=bap, in0=bap, scalar1=6.0,
                                                     scalar2=None, op0=ALU.min),
                      reads=[bufnm], writes=[bufnm])
                    digitize_fp4_ap(bap, bufnm, q_t1, q_b, q_c, tw)
                    E("V", lambda q: q.tensor_mul(out=bap, in0=bap, in1=sgn[:, 0:tw]),
                      reads=[bufnm, R(sgn)], writes=[bufnm])
                    E("V", lambda q: q.tensor_tensor(
                        out=bap.rearrange("p (b k) -> p b k", k=16),
                        in0=bap.rearrange("p (b k) -> p b k", k=16),
                        in1=q_sc[:, 0:nb].rearrange("p (b o) -> p b o", o=1).broadcast_to([128, nb, 16]),
                        op=ALU.mult),
                      reads=[bufnm, R(q_sc)], writes=[bufnm])

                def digitize_fp4_ap(bap, bufnm, acc1, acc2, acc3, tw):
                    E("V", lambda q: q.tensor_scalar(out=acc1[:, 0:tw], in0=bap, scalar1=0.25,
                                                     scalar2=None, op0=ALU.is_ge),
                      reads=[bufnm], writes=[R(acc1)])
                    for mid in (0.75, 1.25, 1.75):
                        E("V", lambda q, mid=mid: q.scalar_tensor_tensor(
                            out=acc1[:, 0:tw], in0=bap, scalar=mid, in1=acc1[:, 0:tw],
                            op0=ALU.is_ge, op1=ALU.add),
                          reads=[bufnm, R(acc1)], writes=[R(acc1)])
                    E("V", lambda q: q.tensor_scalar(out=acc2[:, 0:tw], in0=bap, scalar1=2.5,
                                                     scalar2=None, op0=ALU.is_ge),
                      reads=[bufnm], writes=[R(acc2)])
                    E("V", lambda q: q.scalar_tensor_tensor(
                        out=acc2[:, 0:tw], in0=bap, scalar=3.5, in1=acc2[:, 0:tw],
                        op0=ALU.is_ge, op1=ALU.add),
                      reads=[bufnm, R(acc2)], writes=[R(acc2)])
                    E("V", lambda q: q.tensor_scalar(out=acc3[:, 0:tw], in0=bap, scalar1=5.0,
                                                     scalar2=2.0, op0=ALU.is_ge, op1=ALU.mult),
                      reads=[bufnm], writes=[R(acc3)])
                    E("V", lambda q: q.scalar_tensor_tensor(
                        out=bap, in0=acc1[:, 0:tw], scalar=0.5, in1=acc2[:, 0:tw],
                        op0=ALU.mult, op1=ALU.add),
                      reads=[R(acc1), R(acc2)], writes=[bufnm])
                    E("V", lambda q: q.tensor_add(out=bap, in0=bap, in1=acc3[:, 0:tw]),
                      reads=[bufnm, R(acc3)], writes=[bufnm])

                wbuf = [ctx.sbuf("l1q_wbuf", [128, 1024], F32) for _ in range(2)]
                gmax = ctx.sbuf("l1q_gmax", [128, 8], F32)
                gs = ctx.sbuf("l1q_gs", [128, 8], F32)
                rgs = ctx.sbuf("l1q_rgs", [128, 8], F32)
                psc = ctx.sbuf("l1q_psc", [128, 8], F32)
                E("V", lambda q: q.memset(gmax[:], 0.0), writes=[R(gmax)])
                for wi, (nm, ntile, nch2) in enumerate((("l1_gw", 32, 1), ("l1_uw", 32, 1),
                                                        ("l1_dw", 8, 4))):
                    for it in range(ntile):
                        for c2 in range(nch2):
                            b = it % 2
                            E("qS", lambda q, nm=nm, it=it, b=b, c2=c2: q.dma_start(
                                out=wbuf[b][:],
                                in_=din[nm].ap()[it * 128:(it + 1) * 128,
                                                 c2 * 1024:(c2 + 1) * 1024]),
                              writes=[R(wbuf[b])])
                            E("V", lambda q, b=b: q.tensor_reduce(
                                out=q_sc[:, 0:1], in_=wbuf[b][:], axis=AX.X, op=ALU.max,
                                apply_absolute_value=True),
                              reads=[R(wbuf[b])], writes=[R(q_sc)])
                            E("V", lambda q, wi=wi: q.tensor_tensor(
                                out=gmax[:, wi:wi + 1], in0=gmax[:, wi:wi + 1],
                                in1=q_sc[:, 0:1], op=ALU.max),
                              reads=[R(gmax), R(q_sc)], writes=[R(gmax)])
                    pmax_col(gmax[:, wi:wi + 1], R(gmax))
                E("V", lambda q: q.reciprocal(out=gs[:, 0:3], in_=gmax[:, 0:3]),
                  reads=[R(gmax)], writes=[R(gs)])
                E("V", lambda q: q.tensor_scalar(out=gs[:, 0:3], in0=gs[:, 0:3],
                                                 scalar1=2688.0, scalar2=None, op0=ALU.mult),
                  reads=[R(gs)], writes=[R(gs)])
                sin_sb = ctx.sbuf("l1_sin", [1, 2], F32)
                E("qS", lambda q: q.dma_start(out=sin_sb[0:1, 0:1], in_=din["l1_s_in"].ap()),
                  writes=[R(sin_sb)])
                E("qS", lambda q: q.dma_start(out=sin_sb[0:1, 1:2],
                                              in_=din["l1_s_in_down"].ap()),
                  reads=[R(sin_sb)], writes=[R(sin_sb)])
                pbcast(gs[:, 3:5], R(gs), sin_sb[0:1, 0:2], [R(sin_sb)], 2)
                E("V", lambda q: q.reciprocal(out=rgs[:, 0:5], in_=gs[:, 0:5]),
                  reads=[R(gs)], writes=[R(rgs)])
                E("V", lambda q: q.tensor_mul(out=psc[:, 0:2], in0=rgs[:, 0:2],
                                              in1=rgs[:, 3:4].broadcast_to([128, 2])),
                  reads=[R(rgs)], writes=[R(psc)])
                E("V", lambda q: q.tensor_mul(out=psc[:, 2:3], in0=rgs[:, 2:3],
                                              in1=rgs[:, 4:5]),
                  reads=[R(rgs)], writes=[R(psc)])

                AgT = ctx.sbuf("l1_AgT", [128, 8 * 4096], BF16)
                AuT = ctx.sbuf("l1_AuT", [128, 8 * 4096], BF16)
                for wi, (nm, AT) in enumerate((("l1_gw", AgT), ("l1_uw", AuT))):
                    for it in range(32):
                        b = it % 2
                        E("qS", lambda q, nm=nm, it=it, b=b: q.dma_start(
                            out=wbuf[b][:], in_=din[nm].ap()[it * 128:(it + 1) * 128, :]),
                          writes=[R(wbuf[b])])
                        nv_quant(wbuf[b], R(wbuf[b]), 0, 1024, gs[:, wi:wi + 1],
                                 rgs[:, wi:wi + 1], [R(gs), R(rgs)])
                        E("V", lambda q, b=b: q.tensor_copy(out=abf[:], in_=wbuf[b][:]),
                          reads=[R(wbuf[b])], writes=[R(abf)])
                        for kc in range(8):
                            E("T", lambda q, kc=kc: q.transpose(
                                psQ[:, kc * 128:(kc + 1) * 128],
                                abf[:, kc * 128:(kc + 1) * 128], ident[:]),
                              reads=[R(abf), R(ident)], writes=[R(psQ)])
                        dst = AT[:].rearrange("p (c n) -> p c n", c=8)[:, :,
                                                                      it * 128:(it + 1) * 128]
                        E("V", lambda q, dst=dst: q.tensor_copy(
                            out=dst, in_=psQ[:].rearrange("p (c n) -> p c n", c=8)),
                          reads=[R(psQ)], writes=[R(AT)])

                xnf2 = ctx.sbuf("l1_xnf2", [128, 1024], F32)
                AxT = ctx.sbuf("l1_AxT", [128, 8 * 128], BF16)
                gbuf = ctx.sbuf("l1_gbuf", [128, 4096], F32)
                AhT = ctx.sbuf("l1_AhT", [128, 32 * 128], BF16)
                psM = [ctx.psum("l1_psM", [128, 512], F32) for _ in range(2)]
                for jt in range(NT):
                    E("qS", lambda q, jt=jt: q.dma_start(
                        out=xnf2[:], in_=xn_f.ap()[jt * 128:(jt + 1) * 128, :]),
                      reads=["xn_f"], writes=[R(xnf2)])
                    nv_quant(xnf2, R(xnf2), 0, 1024, gs[:, 3:4], rgs[:, 3:4],
                             [R(gs), R(rgs)])
                    E("V", lambda q: q.tensor_copy(out=abf[:], in_=xnf2[:]),
                      reads=[R(xnf2)], writes=[R(abf)])
                    for kc in range(8):
                        E("T", lambda q, kc=kc: q.transpose(
                            psQ[:, kc * 128:(kc + 1) * 128], abf[:, kc * 128:(kc + 1) * 128],
                            ident[:]),
                          reads=[R(abf), R(ident)], writes=[R(psQ)])
                    E("V", lambda q: q.tensor_copy(out=AxT[:], in_=psQ[:]),
                      reads=[R(psQ)], writes=[R(AxT)])
                    for nc_ in range(8):
                        pb = nc_ % 2
                        lhs = [AxT[:, kc * 128:(kc + 1) * 128] for kc in range(8)]
                        rhs = [AgT[:, kc * 4096 + nc_ * 512: kc * 4096 + (nc_ + 1) * 512]
                               for kc in range(8)]
                        mm_tile(psM[pb][:], R(psM[pb]), lhs, rhs, [R(AxT), R(AgT)])
                        E("S", lambda q, pb=pb, nc_=nc_: q.activation(
                            out=gbuf[:, nc_ * 512:(nc_ + 1) * 512], in_=psM[pb][:],
                            func=AF.Silu, scale=psc[:, 0:1]),
                          reads=[R(psM[pb]), R(psc)], writes=[R(gbuf)])
                    for nc_ in range(8):
                        pb = nc_ % 2
                        lhs = [AxT[:, kc * 128:(kc + 1) * 128] for kc in range(8)]
                        rhs = [AuT[:, kc * 4096 + nc_ * 512: kc * 4096 + (nc_ + 1) * 512]
                               for kc in range(8)]
                        mm_tile(psM[pb][:], R(psM[pb]), lhs, rhs, [R(AxT), R(AuT)])
                        E("V", lambda q, pb=pb, nc_=nc_: q.scalar_tensor_tensor(
                            out=gbuf[:, nc_ * 512:(nc_ + 1) * 512], in0=psM[pb][:],
                            scalar=psc[:, 1:2], in1=gbuf[:, nc_ * 512:(nc_ + 1) * 512],
                            op0=ALU.mult, op1=ALU.mult),
                          reads=[R(psM[pb]), R(psc), R(gbuf)], writes=[R(gbuf)])
                    for c4 in range(4):
                        nv_quant(gbuf, R(gbuf), c4 * 1024, 1024, gs[:, 4:5], rgs[:, 4:5],
                                 [R(gs), R(rgs)])
                        E("V", lambda q, c4=c4: q.tensor_copy(
                            out=abf[:], in_=gbuf[:, c4 * 1024:(c4 + 1) * 1024]),
                          reads=[R(gbuf)], writes=[R(abf)])
                        for kc in range(8):
                            E("T", lambda q, kc=kc: q.transpose(
                                psQ[:, kc * 128:(kc + 1) * 128],
                                abf[:, kc * 128:(kc + 1) * 128], ident[:]),
                              reads=[R(abf), R(ident)], writes=[R(psQ)])
                        E("V", lambda q, c4=c4: q.tensor_copy(
                            out=AhT[:, c4 * 1024:(c4 + 1) * 1024], in_=psQ[:]),
                          reads=[R(psQ)], writes=[R(AhT)])
                    E("qA", lambda q, jt=jt: q.dma_start(
                        out=hT_scr.ap().rearrange("(c p) t -> p c t", p=128)[:, :,
                                                                            jt * 128:(jt + 1) * 128],
                        in_=AhT[:].rearrange("p (c n) -> p c n", c=32)),
                      reads=[R(AhT)], writes=["l1_hT"])

            l1_dbg = dram("l1_dbg", [T, D])
            with ctx.scope():
                q_sc = ctx.sbuf("l1d_sc", [128, 64], F32)
                q_am = ctx.sbuf("l1d_am", [128, 64], F32)
                q_t1 = ctx.sbuf("l1d_t1", [128, 1024], F32)
                q_b = ctx.sbuf("l1d_b", [128, 1024], F32)
                q_c = ctx.sbuf("l1d_c", [128, 1024], F32)
                q_f8a = ctx.sbuf("l1d_f8a", [128, 64], F8E4)
                q_f8b = ctx.sbuf("l1d_f8b", [128, 64], F8E4)
                sgn = ctx.sbuf("l1d_sgn", [128, 1024], F32)
                abf = ctx.sbuf("l1d_abf", [128, 1024], BF16)
                psQ = ctx.psum("l1d_psQ", [128, 1024], BF16)

                def nv_quant(buf, bufnm, lo, tw, gs_col, rgs_col, gsnms):
                    nb = tw // 16
                    bap = buf[:, lo:lo + tw]
                    E("S", lambda q: q.activation(out=sgn[:, 0:tw], in_=bap, func=AF.Sign),
                      reads=[bufnm], writes=[R(sgn)])
                    E("V", lambda q: q.tensor_reduce(
                        out=q_am[:, 0:nb], in_=bap.rearrange("p (b k) -> p b k", k=16),
                        axis=AX.X, op=ALU.max, apply_absolute_value=True),
                      reads=[bufnm], writes=[R(q_am)])
                    E("V", lambda q: q.tensor_scalar(out=q_sc[:, 0:nb], in0=q_am[:, 0:nb],
                                                     scalar1=gs_col, scalar2=1.0 / 6.0,
                                                     op0=ALU.mult, op1=ALU.mult),
                      reads=[R(q_am)] + list(gsnms), writes=[R(q_sc)])
                    E("V", lambda q: q.tensor_scalar(out=q_sc[:, 0:nb], in0=q_sc[:, 0:nb],
                                                     scalar1=0.0, scalar2=448.0,
                                                     op0=ALU.max, op1=ALU.min),
                      reads=[R(q_sc)], writes=[R(q_sc)])
                    e4m3fn(q_sc, R(q_sc), q_t1, q_f8a, q_f8b, nb)
                    E("V", lambda q: q.tensor_scalar(out=q_am[:, 0:nb], in0=q_sc[:, 0:nb],
                                                     scalar1=rgs_col, scalar2=2.0 ** -40,
                                                     op0=ALU.mult, op1=ALU.max),
                      reads=[R(q_sc)] + list(gsnms), writes=[R(q_am)])
                    E("V", lambda q: q.reciprocal(out=q_am[:, 0:nb], in_=q_am[:, 0:nb]),
                      reads=[R(q_am)], writes=[R(q_am)])
                    E("S", lambda q: q.activation(out=bap, in_=bap, func=AF.Abs),
                      reads=[bufnm], writes=[bufnm])
                    E("V", lambda q: q.tensor_tensor(
                        out=bap.rearrange("p (b k) -> p b k", k=16),
                        in0=bap.rearrange("p (b k) -> p b k", k=16),
                        in1=q_am[:, 0:nb].rearrange("p (b o) -> p b o", o=1).broadcast_to([128, nb, 16]),
                        op=ALU.mult),
                      reads=[bufnm, R(q_am)], writes=[bufnm])
                    E("V", lambda q: q.tensor_scalar(out=bap, in0=bap, scalar1=6.0,
                                                     scalar2=None, op0=ALU.min),
                      reads=[bufnm], writes=[bufnm])
                    E("V", lambda q: q.tensor_scalar(out=q_b[:, 0:tw], in0=bap, scalar1=0.25,
                                                     scalar2=None, op0=ALU.is_ge),
                      reads=[bufnm], writes=[R(q_b)])
                    for mid in (0.75, 1.25, 1.75):
                        E("V", lambda q, mid=mid: q.scalar_tensor_tensor(
                            out=q_b[:, 0:tw], in0=bap, scalar=mid, in1=q_b[:, 0:tw],
                            op0=ALU.is_ge, op1=ALU.add),
                          reads=[bufnm, R(q_b)], writes=[R(q_b)])
                    E("V", lambda q: q.tensor_scalar(out=q_c[:, 0:tw], in0=bap, scalar1=2.5,
                                                     scalar2=None, op0=ALU.is_ge),
                      reads=[bufnm], writes=[R(q_c)])
                    E("V", lambda q: q.scalar_tensor_tensor(
                        out=q_c[:, 0:tw], in0=bap, scalar=3.5, in1=q_c[:, 0:tw],
                        op0=ALU.is_ge, op1=ALU.add),
                      reads=[bufnm, R(q_c)], writes=[R(q_c)])
                    E("V", lambda q: q.tensor_scalar(out=q_t1[:, 0:tw], in0=bap, scalar1=5.0,
                                                     scalar2=2.0, op0=ALU.is_ge, op1=ALU.mult),
                      reads=[bufnm], writes=[R(q_t1)])
                    E("V", lambda q: q.scalar_tensor_tensor(
                        out=bap, in0=q_b[:, 0:tw], scalar=0.5, in1=q_c[:, 0:tw],
                        op0=ALU.mult, op1=ALU.add),
                      reads=[R(q_b), R(q_c)], writes=[bufnm])
                    E("V", lambda q: q.tensor_add(out=bap, in0=bap, in1=q_t1[:, 0:tw]),
                      reads=[bufnm, R(q_t1)], writes=[bufnm])
                    E("V", lambda q: q.tensor_mul(out=bap, in0=bap, in1=sgn[:, 0:tw]),
                      reads=[bufnm, R(sgn)], writes=[bufnm])
                    E("V", lambda q: q.tensor_tensor(
                        out=bap.rearrange("p (b k) -> p b k", k=16),
                        in0=bap.rearrange("p (b k) -> p b k", k=16),
                        in1=q_sc[:, 0:nb].rearrange("p (b o) -> p b o", o=1).broadcast_to([128, nb, 16]),
                        op=ALU.mult),
                      reads=[bufnm, R(q_sc)], writes=[bufnm])

                gs2 = ctx.sbuf("l1d_gs", [128, 2], F32)
                rgs2 = ctx.sbuf("l1d_rgs", [128, 2], F32)
                psc2 = ctx.sbuf("l1d_psc", [128, 1], F32)
                wbuf = [ctx.sbuf("l1d_wbuf", [128, 1024], F32) for _ in range(2)]
                E("V", lambda q: q.memset(gs2[:], 0.0), writes=[R(gs2)])
                for it in range(8):
                    for c2 in range(4):
                        b = it % 2
                        E("qS", lambda q, it=it, b=b, c2=c2: q.dma_start(
                            out=wbuf[b][:],
                            in_=din["l1_dw"].ap()[it * 128:(it + 1) * 128,
                                                  c2 * 1024:(c2 + 1) * 1024]),
                          writes=[R(wbuf[b])])
                        E("V", lambda q, b=b: q.tensor_reduce(
                            out=q_sc[:, 0:1], in_=wbuf[b][:], axis=AX.X, op=ALU.max,
                            apply_absolute_value=True),
                          reads=[R(wbuf[b])], writes=[R(q_sc)])
                        E("V", lambda q: q.tensor_tensor(
                            out=gs2[:, 0:1], in0=gs2[:, 0:1], in1=q_sc[:, 0:1], op=ALU.max),
                          reads=[R(gs2), R(q_sc)], writes=[R(gs2)])
                pmax_col(gs2[:, 0:1], R(gs2))
                E("V", lambda q: q.reciprocal(out=gs2[:, 0:1], in_=gs2[:, 0:1]),
                  reads=[R(gs2)], writes=[R(gs2)])
                E("V", lambda q: q.tensor_scalar(out=gs2[:, 0:1], in0=gs2[:, 0:1],
                                                 scalar1=2688.0, scalar2=None, op0=ALU.mult),
                  reads=[R(gs2)], writes=[R(gs2)])
                sin2 = ctx.sbuf("l1d_sin", [1, 1], F32)
                E("qS", lambda q: q.dma_start(out=sin2[0:1, 0:1],
                                              in_=din["l1_s_in_down"].ap()),
                  writes=[R(sin2)])
                pbcast(gs2[:, 1:2], R(gs2), sin2[0:1, 0:1], [R(sin2)], 1)
                E("V", lambda q: q.reciprocal(out=rgs2[:], in_=gs2[:]),
                  reads=[R(gs2)], writes=[R(rgs2)])
                E("V", lambda q: q.tensor_mul(out=psc2[:, 0:1], in0=rgs2[:, 0:1],
                                              in1=rgs2[:, 1:2]),
                  reads=[R(rgs2)], writes=[R(psc2)])

                AdT = ctx.sbuf("l1_AdT", [128, 32 * 1024], BF16)
                for it in range(8):
                    for c2 in range(4):
                        b = (it * 4 + c2) % 2
                        E("qS", lambda q, it=it, b=b, c2=c2: q.dma_start(
                            out=wbuf[b][:],
                            in_=din["l1_dw"].ap()[it * 128:(it + 1) * 128,
                                                  c2 * 1024:(c2 + 1) * 1024]),
                          writes=[R(wbuf[b])])
                        nv_quant(wbuf[b], R(wbuf[b]), 0, 1024, gs2[:, 0:1], rgs2[:, 0:1],
                                 [R(gs2), R(rgs2)])
                        E("V", lambda q, b=b: q.tensor_copy(out=abf[:], in_=wbuf[b][:]),
                          reads=[R(wbuf[b])], writes=[R(abf)])
                        for kc in range(8):
                            E("T", lambda q, kc=kc: q.transpose(
                                psQ[:, kc * 128:(kc + 1) * 128],
                                abf[:, kc * 128:(kc + 1) * 128], ident[:]),
                              reads=[R(abf), R(ident)], writes=[R(psQ)])
                        dst = AdT[:].rearrange("p (c n) -> p c n", c=32)[
                            :, c2 * 8:(c2 + 1) * 8, it * 128:(it + 1) * 128]
                        E("V", lambda q, dst=dst: q.tensor_copy(
                            out=dst, in_=psQ[:].rearrange("p (c n) -> p c n", c=8)),
                          reads=[R(psQ)], writes=[R(AdT)])
                AhT2 = ctx.sbuf("l1_AhT2", [128, 32 * 128], BF16)
                fout = ctx.sbuf("l1_fout", [128, 1024], F32)
                stile2 = ctx.sbuf("l1_stile2", [128, 1024], F32)
                xtile2 = ctx.sbuf("l1_xtile2", [128, 1024], F32)
                psM = [ctx.psum("l1d_psM", [128, 512], F32) for _ in range(2)]
                for jt in range(NT):
                    E("qS", lambda q, jt=jt: q.dma_start(
                        out=AhT2[:].rearrange("p (c n) -> p c n", c=32),
                        in_=hT_scr.ap().rearrange("(c p) t -> p c t", p=128)[
                            :, :, jt * 128:(jt + 1) * 128]),
                      reads=["l1_hT"], writes=[R(AhT2)])
                    for nh in range(2):
                        lhs = [AhT2[:, kc * 128:(kc + 1) * 128] for kc in range(32)]
                        rhs = [AdT[:, kc * 1024 + nh * 512: kc * 1024 + (nh + 1) * 512]
                               for kc in range(32)]
                        mm_tile(psM[nh][:], R(psM[nh]), lhs, rhs, [R(AhT2), R(AdT)])
                        E("S", lambda q, nh=nh: q.activation(
                            out=fout[:, nh * 512:(nh + 1) * 512], in_=psM[nh][:],
                            func=AF.Identity, scale=psc2[:, 0:1]),
                          reads=[R(psM[nh]), R(psc2)], writes=[R(fout)])
                    E("qA", lambda q, jt=jt: q.dma_start(
                        out=l1_dbg.ap()[jt * 128:(jt + 1) * 128, :], in_=fout[:]),
                      reads=[R(fout)], writes=["l1_dbg"])
                    E("S", lambda q: q.activation(out=fout[:], in_=fout[:], func=AF.Silu),
                      reads=[R(fout)], writes=[R(fout)])
                    residual_epilogue(jt, fout, R(fout), dst_dram, dst_nm, stile2, xtile2,
                                      xsrc, xsrc_nm)


        def layer3(xsrc, xsrc_nm, dst_dram, dst_nm):
            NC = n_cores
            k_loc = dram("l3_kloc", [64, T], BF16)
            v_loc = dram("l3_vloc", [T, 64], BF16)
            k_ag = dram("l3_kag", [64 * NC, T], BF16)
            v_ag = dram("l3_vag", [NC * T, 64], BF16)
            s64 = ctx.sbuf("l3_s64", [128, NT * 64], F32)
            xnmax = ctx.sbuf("l3_xnmax", [128, 1], F32)
            E("V", lambda q: q.memset(xnmax[:], 0.0), writes=[R(xnmax)])

            with ctx.scope():
                xnT = ctx.sbuf("xnT3", [128, 8 * T], BF16)
                layernorm_phase(xnT, xsrc, xsrc_nm, write_xn_f=True,
                                absmax_col=xnmax[:, 0:1], absmax_nm=R(xnmax))
                onesc = ctx.sbuf("l3_onesc", [128, 1], BF16)
                E("V", lambda q: q.memset(onesc[:], 1.0), writes=[R(onesc)])
                wq3 = ctx.sbuf("l3_wq", [128, 8 * 64], BF16)
                wk3 = ctx.sbuf("l3_wk", [128, 8 * 64], BF16)
                wv3 = ctx.sbuf("l3_wv", [128, 8 * 64], BF16)
                for nm, sb in (("l3_WqT", wq3), ("l3_WkT", wk3), ("l3_WvT", wv3)):
                    E("qP", lambda q, nm=nm, sb=sb: q.dma_start(
                        out=sb[:].rearrange("p (c n) -> p c n", c=8),
                        in_=din[nm].ap().rearrange("(c p) n -> p c n", p=128)),
                      writes=[R(sb)])
                bq3 = ctx.sbuf("l3_bq", [64, 1], F32)
                bk3 = ctx.sbuf("l3_bk", [64, 1], F32)
                bvr = ctx.sbuf("l3_bvr", [1, 64], F32)
                bvb = ctx.sbuf("l3_bvb", [128, 64], F32)
                E("qS", lambda q: q.dma_start(out=bq3[:], in_=din["l3_bq"].ap()),
                  writes=[R(bq3)])
                E("qS", lambda q: q.dma_start(out=bk3[:], in_=din["l3_bk"].ap()),
                  writes=[R(bk3)])
                E("qS", lambda q: q.dma_start(out=bvr[:], in_=din["l3_bv"].ap()),
                  writes=[R(bvr)])
                pbcast(bvb[:], R(bvb), bvr[0:1, :], [R(bvr)], 64)

                qT3 = ctx.sbuf("l3_qT", [128, T], BF16)
                kT3 = ctx.sbuf("l3_kT", [64, T], BF16)
                vtm3 = ctx.sbuf("l3_vtm", [128, NT * 64], BF16)
                vf32 = ctx.sbuf("l3_vf32", [128, 64], F32)
                psA = [ctx.psum("l3_psA", [128, 512], F32) for _ in range(2)]
                nch = [(s, min(s + 512, T)) for s in range(0, T, 512)]
                for wi, (wsb, dstT, bcol) in enumerate(((wq3, qT3, bq3), (wk3, kT3, bk3))):
                    for nt, (lo, hi) in enumerate(nch):
                        pb = nt % 2
                        w_ = hi - lo
                        lhs = [wsb[:, kc * 64:(kc + 1) * 64] for kc in range(8)]
                        rhs = [xnT[:, kc * T + lo: kc * T + hi] for kc in range(8)]
                        mm_tile(psA[pb][0:64, 0:w_], R(psA[pb]), lhs, rhs, [R(wsb), R(xnT)])
                        E("S", lambda q, pb=pb, lo=lo, hi=hi, w_=w_, dstT=dstT, bcol=bcol:
                          q.activation(out=dstT[0:64, lo:hi], in_=psA[pb][0:64, 0:w_],
                                       func=AF.Identity, bias=bcol[:, 0:1], scale=1.0),
                          reads=[R(psA[pb]), R(bcol)], writes=[R(dstT)])
                        if dstT is qT3:
                            mm_tile(psA[pb][64:128, 0:w_], R(psA[pb]), lhs, rhs,
                                    [R(wsb), R(xnT)])
                            E("S", lambda q, pb=pb, lo=lo, hi=hi, w_=w_, bcol=bcol:
                              q.activation(out=qT3[64:128, lo:hi], in_=psA[pb][64:128, 0:w_],
                                           func=AF.Identity, bias=bcol[:, 0:1], scale=1.0),
                              reads=[R(psA[pb]), R(bcol)], writes=[R(qT3)])
                for jt in range(NT):
                    pb = jt % 2
                    lhs = [xnT[:, kc * T + jt * 128: kc * T + (jt + 1) * 128]
                           for kc in range(8)]
                    rhs = [wv3[:, kc * 64:(kc + 1) * 64] for kc in range(8)]
                    mm_tile(psA[pb][:, 0:64], R(psA[pb]), lhs, rhs, [R(xnT), R(wv3)])
                    E("V", lambda q, pb=pb: q.tensor_add(out=vf32[:], in0=psA[pb][:, 0:64],
                                                         in1=bvb[:]),
                      reads=[R(psA[pb]), R(bvb)], writes=[R(vf32)])
                    E("V", lambda q, jt=jt: q.tensor_copy(out=vtm3[:, jt * 64:(jt + 1) * 64],
                                                          in_=vf32[:]),
                      reads=[R(vf32)], writes=[R(vtm3)])
                gk3 = ctx.sbuf("l3_gk", [1, 2], F32)
                gtmp3 = ctx.sbuf("l3_gtmp", [1, 1], F32)
                sqb3 = ctx.sbuf("l3_sqb", [64, 512], BF16)
                psN3 = ctx.psum("l3_psN", [1, 512], F32)
                E("V", lambda q: q.memset(gk3[:], 0.0), writes=[R(gk3)])
                for gi, srcT in ((0, qT3), (1, kT3)):
                    for nt, (lo, hi) in enumerate(nch):
                        w_ = hi - lo
                        E("S", lambda q, srcT=srcT, lo=lo, hi=hi: q.activation(
                            out=sqb3[:, 0:hi - lo], in_=srcT[0:64, lo:hi], func=AF.Square),
                          reads=[R(srcT)], writes=[R(sqb3)])
                        E("T", lambda q, w_=w_: q.matmul(psN3[:, 0:w_], onesc[0:64, :],
                                                         sqb3[:, 0:w_], start=True, stop=True, skip_group_check=True),
                          reads=[R(onesc), R(sqb3)], writes=[R(psN3)])
                        E("V", lambda q, w_=w_: q.tensor_reduce(
                            out=gtmp3[0:1, 0:1], in_=psN3[0:1, 0:w_], axis=AX.X, op=ALU.max),
                          reads=[R(psN3)], writes=[R(gtmp3)])
                        E("V", lambda q, gi=gi: q.tensor_tensor(
                            out=gk3[0:1, gi:gi + 1], in0=gk3[0:1, gi:gi + 1],
                            in1=gtmp3[0:1, 0:1], op=ALU.max),
                          reads=[R(gtmp3), R(gk3)], writes=[R(gk3)])
                E("qP", lambda q: q.dma_start(out=k_loc.ap(), in_=kT3[:]),
                  reads=[R(kT3)], writes=["l3_kloc"])
                E("qP", lambda q: q.dma_start(
                    out=v_loc.ap().rearrange("(j p) d -> p j d", p=128),
                    in_=vtm3[:].rearrange("p (j d) -> p j d", d=64)),
                  reads=[R(vtm3)], writes=["l3_vloc"])
                E("P", lambda q: q.collective_compute(
                    "AllGather", ALU.bypass, replica_groups=[list(range(NC))],
                    ins=[k_loc.ap()], outs=[k_ag.ap()]),
                  reads=["l3_kloc"], writes=["l3_kag"])
                E("P", lambda q: q.collective_compute(
                    "AllGather", ALU.bypass, replica_groups=[list(range(NC))],
                    ins=[v_loc.ap()], outs=[v_ag.ap()]),
                  reads=["l3_vloc"], writes=["l3_vag"])
                gkb = dram("l3_gkb_i", [1, 1])
                gko = dram("l3_gkb_o", [1, 1])
                E("qP", lambda q: q.dma_start(out=gkb.ap(), in_=gk3[0:1, 1:2]),
                  reads=[R(gk3)], writes=["l3_gkb_i"])
                E("P", lambda q: q.collective_compute(
                    "AllReduce", ALU.max, replica_groups=[list(range(NC))],
                    ins=[gkb.ap()], outs=[gko.ap()]),
                  reads=["l3_gkb_i"], writes=["l3_gkb_o"])
                E("qP", lambda q: q.dma_start(out=gk3[0:1, 1:2], in_=gko.ap()),
                  reads=["l3_gkb_o"], writes=[R(gk3)])
                bnd3 = ctx.sbuf("l3_bnd", [128, 1], F32)
                E("V", lambda q: q.tensor_mul(out=gk3[0:1, 0:1], in0=gk3[0:1, 0:1],
                                              in1=gk3[0:1, 1:2]),
                  reads=[R(gk3)], writes=[R(gk3)])
                E("S", lambda q: q.activation(out=gk3[0:1, 0:1], in_=gk3[0:1, 0:1],
                                              func=AF.Sqrt, scale=1.0 / 1024.0),
                  reads=[R(gk3)], writes=[R(gk3)])
                E("V", lambda q: q.tensor_scalar(out=gk3[0:1, 0:1], in0=gk3[0:1, 0:1],
                                                 scalar1=-1.0, scalar2=None, op0=ALU.mult),
                  reads=[R(gk3)], writes=[R(gk3)])
                pbcast(bnd3[:, 0:1], R(bnd3), gk3[0:1, 0:1], [R(gk3)], 1)
                NPAIR = (NC + 1) // 2
                kall = ctx.sbuf("l3_kall", [128, NPAIR * T], BF16)
                vall = ctx.sbuf("l3_vall", [128, NC * NT * 65], BF16)
                for j in range(NC):
                    po = 64 * (j % 2)
                    E("qS", lambda q, j=j, po=po: q.dma_start(
                        out=kall[po:po + 64, (j // 2) * T:(j // 2 + 1) * T],
                        in_=k_ag.ap()[j * 64:(j + 1) * 64, :]),
                      reads=["l3_kag"], writes=[R(kall)])
                E("qS", lambda q: q.dma_start(
                    out=vall[:].rearrange("p (j e) -> p j e", e=65)[:, :, 0:64],
                    in_=v_ag.ap().rearrange("(j p) d -> p j d", p=128)),
                  reads=["l3_vag"], writes=[R(vall)])
                E("V", lambda q: q.memset(
                    vall[:].rearrange("p (j e) -> p j e", e=65)[:, :, 64:65], 1.0),
                  reads=[R(vall)], writes=[R(vall)])
                expb3 = ctx.sbuf("l3_expb", [128, 512], BF16)
                expT3 = ctx.sbuf("l3_expT", [128, 512], BF16)
                expbufs = [expb3, expT3]
                rsum3 = ctx.sbuf("l3_rsum", [128, 1], F32)
                osb3 = ctx.sbuf("l3_osb", [128, 512], BF16)
                psT3 = ctx.psum("l3_psT", [128, 1024], BF16)
                psO3 = ctx.psum("l3_psO", [128, 512], F32)
                nkt_g = NC * (T // 128)
                for qc in range(T // 512):
                    qlo = qc * 512
                    for kt in range(nkt_g):
                        j, kl = divmod(kt, NT)
                        po = 64 * (j % 2)
                        b = kt % 2
                        E("T", lambda q, j=j, kl=kl, po=po, b=b, qlo=qlo: q.matmul(
                            psA[b][:, 0:512],
                            kall[po:po + 64, (j // 2) * T + kl * 128:(j // 2) * T + (kl + 1) * 128],
                            qT3[po:po + 64, qlo:qlo + 512],
                            start=True, stop=True, skip_group_check=True),
                          reads=[R(qT3), R(kall)], writes=[R(psA[b])])
                        E("S", lambda q, b=b: q.activation(
                            out=expbufs[b][:], in_=psA[b][:, 0:512], func=AF.Exp,
                            bias=bnd3[:, 0:1], scale=1.0 / 32.0),
                          reads=[R(psA[b]), R(bnd3)], writes=[R(expbufs[b])])
                        E("T", lambda q, kt=kt, b=b, ng=nkt_g: q.matmul(
                            psO3[0:65, 0:512], vall[:, kt * 65:(kt + 1) * 65],
                            expbufs[b][:], start=(kt == 0), stop=(kt == ng - 1),
                            skip_group_check=True),
                          reads=[R(expbufs[b]), R(vall)], writes=[R(psO3)])
                    E("S", lambda q: q.activation(out=osb3[0:65, :], in_=psO3[0:65, 0:512],
                                                  func=AF.Copy),
                      reads=[R(psO3)], writes=[R(osb3)])
                    for i in range(4):
                        E("T", lambda q, i=i: q.transpose(
                            psT3[:, i * 128:(i + 1) * 128],
                            osb3[:, i * 128:(i + 1) * 128], ident[:]),
                          reads=[R(osb3), R(ident)], writes=[R(psT3)])
                    for i in range(4):
                        jt = qc * 4 + i
                        E("V", lambda q, i=i: q.reciprocal(
                            out=rsum3[:], in_=psT3[:, i * 128 + 64:i * 128 + 65]),
                          reads=[R(psT3)], writes=[R(rsum3)])
                        E("V", lambda q, i=i, jt=jt: q.tensor_scalar(
                            out=s64[:, jt * 64:(jt + 1) * 64],
                            in0=psT3[:, i * 128:i * 128 + 64],
                            scalar1=rsum3[:, 0:1], scalar2=None, op0=ALU.mult),
                          reads=[R(psT3), R(rsum3)], writes=[R(s64)])

            h3_scr = dram("l3_h", [T, 2816])
            hmax = ctx.sbuf("l3_hmax", [128, 1], F32)
            E("V", lambda q: q.memset(hmax[:], 0.0), writes=[R(hmax)])

            def qdq_cols(buf, bufnm, tmp, rs_col, s_col, scnms, qmin, qmax, w):
                E("V", lambda q: q.tensor_scalar(out=buf[:, 0:w], in0=buf[:, 0:w],
                                                 scalar1=rs_col, scalar2=None, op0=ALU.mult),
                  reads=[bufnm] + list(scnms), writes=[bufnm])
                round_ste(buf, bufnm, buf[:, 0:w], [bufnm], tmp, w)
                E("V", lambda q: q.tensor_scalar(out=buf[:, 0:w], in0=buf[:, 0:w],
                                                 scalar1=float(qmin), scalar2=float(qmax),
                                                 op0=ALU.max, op1=ALU.min),
                  reads=[bufnm], writes=[bufnm])
                E("V", lambda q: q.tensor_scalar(out=buf[:, 0:w], in0=buf[:, 0:w],
                                                 scalar1=s_col, scalar2=None, op0=ALU.mult),
                  reads=[bufnm] + list(scnms), writes=[bufnm])

            with ctx.scope():
                pmax_col(xnmax[:, 0:1], R(xnmax))
                xb_i = dram("l3_xb_i", [1, 1])
                xb_o = dram("l3_xb_o", [1, 1])
                E("qP", lambda q: q.dma_start(out=xb_i.ap(), in_=xnmax[0:1, :]),
                  reads=[R(xnmax)], writes=["l3_xb_i"])
                E("P", lambda q: q.collective_compute(
                    "AllReduce", ALU.max, replica_groups=[list(range(NC))],
                    ins=[xb_i.ap()], outs=[xb_o.ap()]),
                  reads=["l3_xb_i"], writes=["l3_xb_o"])
                E("qP", lambda q: q.dma_start(out=xnmax[0:1, :], in_=xb_o.ap()),
                  reads=["l3_xb_o"], writes=[R(xnmax)])
                clips = ctx.sbuf("l3_clips", [1, 2], F32)
                E("qS", lambda q: q.dma_start(out=clips[0:1, 0:1], in_=din["l3_clip_x"].ap()),
                  writes=[R(clips)])
                E("qS", lambda q: q.dma_start(out=clips[0:1, 1:2],
                                              in_=din["l3_clip_down"].ap()),
                  reads=[R(clips)], writes=[R(clips)])
                E("S", lambda q: q.activation(out=clips[0:1, :], in_=clips[0:1, :],
                                              func=AF.Sigmoid),
                  reads=[R(clips)], writes=[R(clips)])
                sx = ctx.sbuf("l3_sx", [128, 2], F32)
                E("V", lambda q: q.tensor_scalar(out=clips[0:1, 0:1], in0=clips[0:1, 0:1],
                                                 scalar1=xnmax[0:1, 0:1], scalar2=1.0 / 127.0,
                                                 op0=ALU.mult, op1=ALU.mult),
                  reads=[R(clips), R(xnmax)], writes=[R(clips)])
                pbcast(sx[:, 0:1], R(sx), clips[0:1, 0:1], [R(clips)], 1)
                E("V", lambda q: q.reciprocal(out=sx[:, 1:2], in_=sx[:, 0:1]),
                  reads=[R(sx)], writes=[R(sx)])

                wgT = ctx.sbuf("l3_wgT", [128, 8 * 2816], BF16)
                wuT = ctx.sbuf("l3_wuT", [128, 8 * 2816], BF16)
                wbuf = [ctx.sbuf("l3_wbuf", [128, 1024], F32) for _ in range(2)]
                wrow = ctx.sbuf("l3_wrow", [128, 2], F32)
                tmpq = ctx.sbuf("l3_tmpq", [128, 2816], F32)
                abf3 = ctx.sbuf("l3_abf", [128, 1024], BF16)
                psQ3 = ctx.psum("l3_psQ", [128, 1024], BF16)
                for nm, WT in (("l3_gw", wgT), ("l3_uw", wuT)):
                    for it in range(22):
                        b = it % 2
                        E("qS", lambda q, nm=nm, it=it, b=b: q.dma_start(
                            out=wbuf[b][:], in_=din[nm].ap()[it * 128:(it + 1) * 128, :]),
                          writes=[R(wbuf[b])])
                        E("V", lambda q, b=b: q.tensor_reduce(
                            out=wrow[:, 1:2], in_=wbuf[b][:], axis=AX.X, op=ALU.max,
                            apply_absolute_value=True),
                          reads=[R(wbuf[b])], writes=[R(wrow)])
                        E("V", lambda q: q.tensor_scalar(out=wrow[:, 1:2], in0=wrow[:, 1:2],
                                                         scalar1=1.0 / 127.0, scalar2=None,
                                                         op0=ALU.mult),
                          reads=[R(wrow)], writes=[R(wrow)])
                        E("V", lambda q: q.reciprocal(out=wrow[:, 0:1], in_=wrow[:, 1:2]),
                          reads=[R(wrow)], writes=[R(wrow)])
                        qdq_cols(wbuf[b], R(wbuf[b]), tmpq, wrow[:, 0:1], wrow[:, 1:2],
                                 [R(wrow)], -128, 127, 1024)
                        E("V", lambda q, b=b: q.tensor_copy(out=abf3[:], in_=wbuf[b][:]),
                          reads=[R(wbuf[b])], writes=[R(abf3)])
                        for kc in range(8):
                            E("T", lambda q, kc=kc: q.transpose(
                                psQ3[:, kc * 128:(kc + 1) * 128],
                                abf3[:, kc * 128:(kc + 1) * 128], ident[:]),
                              reads=[R(abf3), R(ident)], writes=[R(psQ3)])
                        dst = WT[:].rearrange("p (c n) -> p c n", c=8)[:, :,
                                                                      it * 128:(it + 1) * 128]
                        E("V", lambda q, dst=dst: q.tensor_copy(
                            out=dst, in_=psQ3[:].rearrange("p (c n) -> p c n", c=8)),
                          reads=[R(psQ3)], writes=[R(WT)])

                xnf3 = ctx.sbuf("l3_xnf", [128, 1024], F32)
                xqT3 = ctx.sbuf("l3_xqT", [128, 8 * 128], BF16)
                gbuf3 = ctx.sbuf("l3_gbuf", [128, 2816], F32)
                mcol = ctx.sbuf("l3_mcol", [128, 1], F32)
                psM3 = [ctx.psum("l3_psM", [128, 512], F32) for _ in range(2)]
                nch28 = [(s, min(s + 512, 2816)) for s in range(0, 2816, 512)]
                for jt in range(NT):
                    E("qS", lambda q, jt=jt: q.dma_start(
                        out=xnf3[:], in_=xn_f.ap()[jt * 128:(jt + 1) * 128, :]),
                      reads=["xn_f"], writes=[R(xnf3)])
                    qdq_cols(xnf3, R(xnf3), tmpq, sx[:, 1:2], sx[:, 0:1], [R(sx)],
                             -128, 127, 1024)
                    E("V", lambda q: q.tensor_copy(out=abf3[:], in_=xnf3[:]),
                      reads=[R(xnf3)], writes=[R(abf3)])
                    for kc in range(8):
                        E("T", lambda q, kc=kc: q.transpose(
                            psQ3[:, kc * 128:(kc + 1) * 128],
                            abf3[:, kc * 128:(kc + 1) * 128], ident[:]),
                          reads=[R(abf3), R(ident)], writes=[R(psQ3)])
                    E("V", lambda q: q.tensor_copy(out=xqT3[:], in_=psQ3[:]),
                      reads=[R(psQ3)], writes=[R(xqT3)])
                    for nc_, (lo, hi) in enumerate(nch28):
                        pb = nc_ % 2
                        w_ = hi - lo
                        lhs = [xqT3[:, kc * 128:(kc + 1) * 128] for kc in range(8)]
                        rhs = [wgT[:, kc * 2816 + lo: kc * 2816 + hi] for kc in range(8)]
                        mm_tile(psM3[pb][:, 0:w_], R(psM3[pb]), lhs, rhs, [R(xqT3), R(wgT)])
                        E("S", lambda q, pb=pb, lo=lo, hi=hi, w_=w_: q.activation(
                            out=gbuf3[:, lo:hi], in_=psM3[pb][:, 0:w_], func=AF.Silu),
                          reads=[R(psM3[pb])], writes=[R(gbuf3)])
                    for nc_, (lo, hi) in enumerate(nch28):
                        pb = nc_ % 2
                        w_ = hi - lo
                        lhs = [xqT3[:, kc * 128:(kc + 1) * 128] for kc in range(8)]
                        rhs = [wuT[:, kc * 2816 + lo: kc * 2816 + hi] for kc in range(8)]
                        mm_tile(psM3[pb][:, 0:w_], R(psM3[pb]), lhs, rhs, [R(xqT3), R(wuT)])
                        E("V", lambda q, pb=pb, lo=lo, hi=hi, w_=w_: q.tensor_mul(
                            out=gbuf3[:, lo:hi], in0=gbuf3[:, lo:hi], in1=psM3[pb][:, 0:w_]),
                          reads=[R(psM3[pb]), R(gbuf3)], writes=[R(gbuf3)])
                    E("V", lambda q: q.tensor_reduce(out=mcol[:], in_=gbuf3[:], axis=AX.X,
                                                     op=ALU.max, apply_absolute_value=True),
                      reads=[R(gbuf3)], writes=[R(mcol)])
                    E("V", lambda q: q.tensor_tensor(out=hmax[:], in0=hmax[:], in1=mcol[:],
                                                     op=ALU.max),
                      reads=[R(mcol), R(hmax)], writes=[R(hmax)])
                    E("qA", lambda q, jt=jt: q.dma_start(
                        out=h3_scr.ap()[jt * 128:(jt + 1) * 128, :], in_=gbuf3[:]),
                      reads=[R(gbuf3)], writes=["l3_h"])

            with ctx.scope():
                pmax_col(hmax[:, 0:1], R(hmax))
                hb_i = dram("l3_hb_i", [1, 1])
                hb_o = dram("l3_hb_o", [1, 1])
                E("qP", lambda q: q.dma_start(out=hb_i.ap(), in_=hmax[0:1, :]),
                  reads=[R(hmax)], writes=["l3_hb_i"])
                E("P", lambda q: q.collective_compute(
                    "AllReduce", ALU.max, replica_groups=[list(range(NC))],
                    ins=[hb_i.ap()], outs=[hb_o.ap()]),
                  reads=["l3_hb_i"], writes=["l3_hb_o"])
                E("qP", lambda q: q.dma_start(out=hmax[0:1, :], in_=hb_o.ap()),
                  reads=["l3_hb_o"], writes=[R(hmax)])
                clips2 = ctx.sbuf("l3_clips2", [1, 1], F32)
                E("qS", lambda q: q.dma_start(out=clips2[0:1, 0:1],
                                              in_=din["l3_clip_down"].ap()),
                  writes=[R(clips2)])
                E("S", lambda q: q.activation(out=clips2[0:1, :], in_=clips2[0:1, :],
                                              func=AF.Sigmoid),
                  reads=[R(clips2)], writes=[R(clips2)])
                sh3 = ctx.sbuf("l3_sh", [128, 2], F32)
                E("V", lambda q: q.tensor_scalar(out=clips2[0:1, 0:1], in0=clips2[0:1, 0:1],
                                                 scalar1=hmax[0:1, 0:1], scalar2=1.0 / 127.0,
                                                 op0=ALU.mult, op1=ALU.mult),
                  reads=[R(clips2), R(hmax)], writes=[R(clips2)])
                pbcast(sh3[:, 0:1], R(sh3), clips2[0:1, 0:1], [R(clips2)], 1)
                E("V", lambda q: q.reciprocal(out=sh3[:, 1:2], in_=sh3[:, 0:1]),
                  reads=[R(sh3)], writes=[R(sh3)])
                dwT = ctx.sbuf("l3_dwT", [128, 22 * 1024], BF16)
                wbuf2 = [ctx.sbuf("l3_wbuf2", [128, 2816], F32) for _ in range(2)]
                wrow2 = ctx.sbuf("l3_wrow2", [128, 2], F32)
                tmpq2 = ctx.sbuf("l3_tmpq2", [128, 2816], F32)
                abf4 = ctx.sbuf("l3_abf4", [128, 2816], BF16)
                psQ4 = ctx.psum("l3_psQ4", [128, 1024], BF16)
                for it in range(8):
                    b = it % 2
                    E("qS", lambda q, it=it, b=b: q.dma_start(
                        out=wbuf2[b][:], in_=din["l3_dw"].ap()[it * 128:(it + 1) * 128, :]),
                      writes=[R(wbuf2[b])])
                    E("V", lambda q, b=b: q.tensor_reduce(
                        out=wrow2[:, 1:2], in_=wbuf2[b][:], axis=AX.X, op=ALU.max,
                        apply_absolute_value=True),
                      reads=[R(wbuf2[b])], writes=[R(wrow2)])
                    E("V", lambda q: q.tensor_scalar(out=wrow2[:, 1:2], in0=wrow2[:, 1:2],
                                                     scalar1=1.0 / 127.0, scalar2=None,
                                                     op0=ALU.mult),
                      reads=[R(wrow2)], writes=[R(wrow2)])
                    E("V", lambda q: q.reciprocal(out=wrow2[:, 0:1], in_=wrow2[:, 1:2]),
                      reads=[R(wrow2)], writes=[R(wrow2)])
                    qdq_cols(wbuf2[b], R(wbuf2[b]), tmpq2, wrow2[:, 0:1], wrow2[:, 1:2],
                             [R(wrow2)], -128, 127, 2816)
                    E("V", lambda q, b=b: q.tensor_copy(out=abf4[:], in_=wbuf2[b][:]),
                      reads=[R(wbuf2[b])], writes=[R(abf4)])
                    for kc in range(22):
                        E("T", lambda q, kc=kc, it=it: q.transpose(
                            psQ4[:, (kc % 8) * 128:(kc % 8 + 1) * 128],
                            abf4[:, kc * 128:(kc + 1) * 128], ident[:]),
                          reads=[R(abf4), R(ident)], writes=[R(psQ4)])
                        E("V", lambda q, kc=kc, it=it: q.tensor_copy(
                            out=dwT[:, kc * 1024 + it * 128: kc * 1024 + (it + 1) * 128],
                            in_=psQ4[:, (kc % 8) * 128:(kc % 8 + 1) * 128]),
                          reads=[R(psQ4)], writes=[R(dwT)])
                hbufB = ctx.sbuf("l3_hbufB", [128, 2816], F32)
                hT3 = ctx.sbuf("l3_hT3", [128, 22 * 128], BF16)
                fout3 = ctx.sbuf("l3_fout", [128, 1024], F32)
                sg3 = ctx.sbuf("l3_sg", [128, 1024], F32)
                xtile3 = ctx.sbuf("l3_xtile", [128, 1024], F32)
                psM4 = [ctx.psum("l3_psM4", [128, 512], F32) for _ in range(2)]
                for jt in range(NT):
                    E("qS", lambda q, jt=jt: q.dma_start(
                        out=hbufB[:], in_=h3_scr.ap()[jt * 128:(jt + 1) * 128, :]),
                      reads=["l3_h"], writes=[R(hbufB)])
                    qdq_cols(hbufB, R(hbufB), tmpq2, sh3[:, 1:2], sh3[:, 0:1], [R(sh3)],
                             -128, 127, 2816)
                    E("V", lambda q: q.tensor_copy(out=abf4[:], in_=hbufB[:]),
                      reads=[R(hbufB)], writes=[R(abf4)])
                    for kc in range(22):
                        E("T", lambda q, kc=kc: q.transpose(
                            psQ4[:, (kc % 8) * 128:(kc % 8 + 1) * 128],
                            abf4[:, kc * 128:(kc + 1) * 128], ident[:]),
                          reads=[R(abf4), R(ident)], writes=[R(psQ4)])
                        E("V", lambda q, kc=kc: q.tensor_copy(
                            out=hT3[:, kc * 128:(kc + 1) * 128],
                            in_=psQ4[:, (kc % 8) * 128:(kc % 8 + 1) * 128]),
                          reads=[R(psQ4)], writes=[R(hT3)])
                    for nh in range(2):
                        lhs = [hT3[:, kc * 128:(kc + 1) * 128] for kc in range(22)]
                        rhs = [dwT[:, kc * 1024 + nh * 512: kc * 1024 + (nh + 1) * 512]
                               for kc in range(22)]
                        mm_tile(psM4[nh][:], R(psM4[nh]), lhs, rhs, [R(hT3), R(dwT)])
                        E("S", lambda q, nh=nh: q.activation(
                            out=fout3[:, nh * 512:(nh + 1) * 512], in_=psM4[nh][:],
                            func=AF.Silu),
                          reads=[R(psM4[nh])], writes=[R(fout3)])
                    E("V", lambda q, jt=jt: q.tensor_tensor(
                        out=sg3[:].rearrange("p (r c) -> p r c", c=64),
                        in0=fout3[:].rearrange("p (r c) -> p r c", c=64),
                        in1=s64[:, jt * 64:(jt + 1) * 64].rearrange(
                            "p (o c) -> p o c", o=1).broadcast_to([128, 16, 64]),
                        op=ALU.mult),
                      reads=[R(fout3), R(s64)], writes=[R(sg3)])
                    E("qS", lambda q, jt=jt: q.dma_start(
                        out=xtile3[:], in_=xsrc.ap()[jt * 128:(jt + 1) * 128, :]),
                      reads=[xsrc_nm], writes=[R(xtile3)])
                    E("V", lambda q: q.tensor_add(out=sg3[:], in0=sg3[:], in1=xtile3[:]),
                      reads=[R(sg3), R(xtile3)], writes=[R(sg3)])
                    E("qA", lambda q, jt=jt: q.dma_start(
                        out=dst_dram.ap()[jt * 128:(jt + 1) * 128, :], in_=sg3[:]),
                      reads=[R(sg3)], writes=[dst_nm])


        def layer2(xsrc, xsrc_nm, dst_dram, dst_nm):
            NC = n_cores
            oT2_scr = dram("l2_oT", [T, 1024], BF16)
            xnmax2 = ctx.sbuf("l2_xnmax", [128, 1], F32)
            E("V", lambda q: q.memset(xnmax2[:], 0.0), writes=[R(xnmax2)])

            with ctx.scope():
                xnT = ctx.sbuf("xnT2", [128, 8 * T], BF16)
                layernorm_phase(xnT, xsrc, xsrc_nm, write_xn_f=True,
                                absmax_col=xnmax2[:, 0:1], absmax_nm=R(xnmax2))
                onesc = ctx.sbuf("l2_onesc", [128, 1], BF16)
                E("V", lambda q: q.memset(onesc[:], 1.0), writes=[R(onesc)])
                wq2 = ctx.sbuf("l2_wq", [128, 8 * 1024], BF16)
                wqr2 = ctx.sbuf("l2_wqr", [128, 8 * 512], BF16)
                wkr2 = ctx.sbuf("l2_wkr", [128, 8 * 32], BF16)
                wkvd2 = ctx.sbuf("l2_wkvd", [128, 8 * 256], BF16)
                wku2 = ctx.sbuf("l2_wku", [128, 2 * 1024], BF16)
                wvu2 = ctx.sbuf("l2_wvu", [128, 2 * 1024], BF16)
                for nm, sb, c_ in (("l2_Wq", wq2, 8), ("l2_Wqr", wqr2, 8), ("l2_Wkr", wkr2, 8),
                                   ("l2_Wkvd", wkvd2, 8), ("l2_Wku", wku2, 2),
                                   ("l2_Wvu", wvu2, 2)):
                    E("qP", lambda q, nm=nm, sb=sb, c_=c_: q.dma_start(
                        out=sb[:].rearrange("p (c n) -> p c n", c=c_),
                        in_=din[nm].ap().rearrange("(c p) n -> p c n", p=128)),
                      writes=[R(sb)])
                ropeC = ctx.sbuf("l2_ropeC", [128, NT * 16], F32)
                ropeS = ctx.sbuf("l2_ropeS", [128, NT * 16], F32)
                E("qS", lambda q: q.dma_start(
                    out=ropeC[:].rearrange("p (j i) -> p j i", i=16),
                    in_=din["ropeC"].ap().rearrange("(j p) i -> p j i", p=128)),
                  writes=[R(ropeC)])
                E("qS", lambda q: q.dma_start(
                    out=ropeS[:].rearrange("p (j i) -> p j i", i=16),
                    in_=din["ropeS"].ap().rearrange("(j p) i -> p j i", p=128)),
                  writes=[R(ropeS)])
                maskB = ctx.sbuf("l2_maskB", [128, 128], F32)
                E("qS", lambda q: q.dma_start(out=maskB[:], in_=din["causmaskB"].ap()),
                  writes=[R(maskB)])
                lobo = ctx.sbuf("l2_lobo", [1, 16], F32)
                E("qS", lambda q: q.dma_start(out=lobo[:], in_=din["l2_lobo"].ap()),
                  writes=[R(lobo)])
                E("S", lambda q: q.activation(out=lobo[:], in_=lobo[:], func=AF.Exp),
                  reads=[R(lobo)], writes=[R(lobo)])
                chc = ctx.sbuf("l2_chc", [128, 16], F32)
                pbcast(chc[:, 0:16], R(chc), lobo[0:1, 0:16], [R(lobo)], 16)

                latT2 = ctx.sbuf("l2_latT", [128, 2 * T], BF16)
                psA = [ctx.psum("l2_psA", [128, 512], F32) for _ in range(2)]
                nch = [(s, min(s + 512, T)) for s in range(0, T, 512)]
                for m in range(2):
                    for nt, (lo, hi) in enumerate(nch):
                        pb = nt % 2
                        w_ = hi - lo
                        lhs = [wkvd2[:, kc * 256 + m * 128: kc * 256 + (m + 1) * 128]
                               for kc in range(8)]
                        rhs = [xnT[:, kc * T + lo: kc * T + hi] for kc in range(8)]
                        mm_tile(psA[pb][:, 0:w_], R(psA[pb]), lhs, rhs, [R(wkvd2), R(xnT)])
                        E("V", lambda q, pb=pb, m=m, lo=lo, hi=hi, w_=w_: q.tensor_copy(
                            out=latT2[:, m * T + lo: m * T + hi], in_=psA[pb][:, 0:w_]),
                          reads=[R(psA[pb])], writes=[R(latT2)])

                qr_fmh = ctx.sbuf("l2_qrfmh", [32, T], BF16)
                kr_fm = ctx.sbuf("l2_krfm", [32, T], BF16)
                qrt = ctx.sbuf("l2_qrt", [128, 512], F32)
                qrt2 = ctx.sbuf("l2_qrt2", [128, 512], F32)
                qrt3 = ctx.sbuf("l2_qrt3", [128, 512], F32)
                krt = ctx.sbuf("l2_krt", [128, 32], F32)
                krt2 = ctx.sbuf("l2_krt2", [128, 32], F32)
                rbf = ctx.sbuf("l2_rbf", [128, 512], BF16)
                psR = ctx.psum("l2_psR", [128, 512], F32)
                psT2 = ctx.psum("l2_psT2", [128, 1024], BF16)

                def rope_tm(dst, dstnm, srcv, srcnm, jt, nh):
                    cosb = ropeC[:, jt * 16:(jt + 1) * 16].rearrange("p (o i) -> p o i", o=1)
                    sinb = ropeS[:, jt * 16:(jt + 1) * 16].rearrange("p (o i) -> p o i", o=1)
                    s3 = srcv.rearrange("p (h z) -> p h z", z=32)
                    d3 = dst[:, 0:nh * 32].rearrange("p (h z) -> p h z", z=32)
                    E("V", lambda q: q.tensor_tensor(out=d3[:, :, 0:16], in0=s3[:, :, 0:16],
                                                     in1=cosb.broadcast_to([128, nh, 16]),
                                                     op=ALU.mult),
                      reads=[srcnm, R(ropeC)], writes=[dstnm])
                    E("V", lambda q: q.tensor_tensor(out=d3[:, :, 16:32], in0=s3[:, :, 0:16],
                                                     in1=sinb.broadcast_to([128, nh, 16]),
                                                     op=ALU.mult),
                      reads=[srcnm, R(ropeS)], writes=[dstnm])
                    E("V", lambda q: q.tensor_tensor(out=qrt2[:, 0:nh * 32].rearrange(
                        "p (h z) -> p h z", z=32)[:, :, 0:16], in0=s3[:, :, 16:32],
                        in1=sinb.broadcast_to([128, nh, 16]), op=ALU.mult),
                      reads=[srcnm, R(ropeS)], writes=[R(qrt2)])
                    E("V", lambda q: q.tensor_tensor(out=qrt2[:, 0:nh * 32].rearrange(
                        "p (h z) -> p h z", z=32)[:, :, 16:32], in0=s3[:, :, 16:32],
                        in1=cosb.broadcast_to([128, nh, 16]), op=ALU.mult),
                      reads=[srcnm, R(ropeC)], writes=[R(qrt2)])
                    E("V", lambda q: q.tensor_sub(
                        out=d3[:, :, 0:16], in0=d3[:, :, 0:16],
                        in1=qrt2[:, 0:nh * 32].rearrange("p (h z) -> p h z", z=32)[:, :, 0:16]),
                      reads=[dstnm, R(qrt2)], writes=[dstnm])
                    E("V", lambda q: q.tensor_add(
                        out=d3[:, :, 16:32], in0=d3[:, :, 16:32],
                        in1=qrt2[:, 0:nh * 32].rearrange("p (h z) -> p h z", z=32)[:, :, 16:32]),
                      reads=[dstnm, R(qrt2)], writes=[dstnm])

                for jt in range(NT):
                    lhs = [xnT[:, kc * T + jt * 128: kc * T + (jt + 1) * 128] for kc in range(8)]
                    rhs = [wkr2[:, kc * 32:(kc + 1) * 32] for kc in range(8)]
                    mm_tile(psR[:, 0:32], R(psR), lhs, rhs, [R(xnT), R(wkr2)])
                    E("V", lambda q: q.tensor_copy(out=krt[:], in_=psR[:, 0:32]),
                      reads=[R(psR)], writes=[R(krt)])
                    rope_tm(krt2, R(krt2), krt[:], R(krt), jt, 1)
                    E("V", lambda q: q.tensor_copy(out=rbf[:, 0:32], in_=krt2[:]),
                      reads=[R(krt2)], writes=[R(rbf)])
                    E("T", lambda q: q.transpose(psT2[0:32, 0:128], rbf[:, 0:32], ident[:]),
                      reads=[R(rbf), R(ident)], writes=[R(psT2)])
                    E("V", lambda q, jt=jt: q.tensor_copy(
                        out=kr_fm[0:32, jt * 128:(jt + 1) * 128], in_=psT2[0:32, 0:128]),
                      reads=[R(psT2)], writes=[R(kr_fm)])

                qcb = ctx.sbuf("l2_qcb", [64, T], BF16)
                kcb = ctx.sbuf("l2_kcb", [64, T], BF16)
                vtm2 = ctx.sbuf("l2_vtm", [128, NT * 64], BF16)
                scbuf = ctx.sbuf("l2_scbuf", [128, T], F32)
                mcol2 = ctx.sbuf("l2_mcol2", [128, 1], F32)
                expb2 = ctx.sbuf("l2_expb", [128, 512], BF16)
                expT2 = ctx.sbuf("l2_expT", [128, 512], BF16)
                sums2 = ctx.sbuf("l2_sums", [128, (T + 511) // 512], F32)
                rsum2 = ctx.sbuf("l2_rsum", [128, 1], F32)
                otile2 = ctx.sbuf("l2_otile", [128, 64], BF16)
                psE = ctx.psum("l2_psE", [128, 512], F32)
                psTx = ctx.psum("l2_psTx", [128, 1024], BF16)
                psO2 = ctx.psum("l2_psO", [128, 512], F32)
                for h in range(16):
                    for jt in range(NT):
                        lhs = [xnT[:, kc * T + jt * 128: kc * T + (jt + 1) * 128]
                               for kc in range(8)]
                        rhs = [wqr2[:, kc * 512 + h * 32: kc * 512 + (h + 1) * 32]
                               for kc in range(8)]
                        mm_tile(psR[:, 0:32], R(psR), lhs, rhs, [R(xnT), R(wqr2)])
                        E("V", lambda q: q.tensor_copy(out=krt[:], in_=psR[:, 0:32]),
                          reads=[R(psR)], writes=[R(krt)])
                        rope_tm(krt2, R(krt2), krt[:], R(krt), jt, 1)
                        E("V", lambda q: q.tensor_copy(out=rbf[:, 0:32], in_=krt2[:]),
                          reads=[R(krt2)], writes=[R(rbf)])
                        E("T", lambda q: q.transpose(psT2[0:32, 0:128], rbf[:, 0:32],
                                                     ident[:]),
                          reads=[R(rbf), R(ident)], writes=[R(psT2)])
                        E("V", lambda q, jt=jt: q.tensor_copy(
                            out=qr_fmh[0:32, jt * 128:(jt + 1) * 128], in_=psT2[0:32, 0:128]),
                          reads=[R(psT2)], writes=[R(qr_fmh)])
                    for nt, (lo, hi) in enumerate(nch):
                        pb = nt % 2
                        w_ = hi - lo
                        lhs = [wq2[:, kc * 1024 + h * 64: kc * 1024 + (h + 1) * 64]
                               for kc in range(8)]
                        rhs = [xnT[:, kc * T + lo: kc * T + hi] for kc in range(8)]
                        mm_tile(psA[pb][0:64, 0:w_], R(psA[pb]), lhs, rhs, [R(wq2), R(xnT)])
                        E("V", lambda q, pb=pb, lo=lo, hi=hi, w_=w_: q.tensor_copy(
                            out=qcb[:, lo:hi], in_=psA[pb][0:64, 0:w_]),
                          reads=[R(psA[pb])], writes=[R(qcb)])
                        lhs = [wku2[:, kc * 1024 + h * 64: kc * 1024 + (h + 1) * 64]
                               for kc in range(2)]
                        rhs = [latT2[:, kc * T + lo: kc * T + hi] for kc in range(2)]
                        mm_tile(psA[1 - pb][0:64, 0:w_], R(psA[1 - pb]), lhs, rhs,
                                [R(wku2), R(latT2)])
                        E("S", lambda q, pb=pb, lo=lo, hi=hi, w_=w_: q.activation(
                            out=kcb[:, lo:hi], in_=psA[1 - pb][0:64, 0:w_], func=AF.Copy),
                          reads=[R(psA[1 - pb])], writes=[R(kcb)])
                    for jt in range(NT):
                        pb = jt % 2
                        lhs = [latT2[:, kc * T + jt * 128: kc * T + (jt + 1) * 128]
                               for kc in range(2)]
                        rhs = [wvu2[:, kc * 1024 + h * 64: kc * 1024 + (h + 1) * 64]
                               for kc in range(2)]
                        mm_tile(psA[pb][:, 0:64], R(psA[pb]), lhs, rhs, [R(latT2), R(wvu2)])
                        E("V", lambda q, pb=pb, jt=jt: q.tensor_copy(
                            out=vtm2[:, jt * 64:(jt + 1) * 64], in_=psA[pb][:, 0:64]),
                          reads=[R(psA[pb])], writes=[R(vtm2)])
                    invsq96 = 1.0 / (96.0 ** 0.5)
                    for jt in range(NT):
                        kend = (jt + 1) * 128
                        chunks2 = [(s, min(s + 512, kend)) for s in range(0, kend, 512)]
                        nkt_end = kend // 128
                        for ci, (klo, khi) in enumerate(chunks2):
                            kw = khi - klo
                            mm_tile(psE[:, 0:kw], R(psE),
                                    [qcb[:, jt * 128:(jt + 1) * 128]],
                                    [kcb[:, klo:khi]], [R(qcb), R(kcb)], last=False)
                            E("T", lambda q, jt=jt, klo=klo, kw=kw: q.matmul(
                                psE[:, 0:kw],
                                qr_fmh[0:32, jt * 128:(jt + 1) * 128],
                                kr_fm[0:32, klo:klo + kw], start=False, stop=True, skip_group_check=True),
                              reads=[R(qr_fmh), R(kr_fm)], writes=[R(psE)])
                            E("S", lambda q, klo=klo, khi=khi, kw=kw: q.activation(
                                out=scbuf[:, klo:khi], in_=psE[:, 0:kw], func=AF.Identity,
                                scale=invsq96),
                              reads=[R(psE)], writes=[R(scbuf)])
                        E("V", lambda q, kend=kend: q.tensor_sub(
                            out=scbuf[:, kend - 128:kend], in0=scbuf[:, kend - 128:kend],
                            in1=maskB[:]),
                          reads=[R(scbuf), R(maskB)], writes=[R(scbuf)])
                        E("V", lambda q, kend=kend: q.tensor_reduce(
                            out=mcol2[:], in_=scbuf[:, 0:kend], axis=AX.X, op=ALU.max),
                          reads=[R(scbuf)], writes=[R(mcol2)])
                        E("V", lambda q: q.tensor_scalar(out=mcol2[:], in0=mcol2[:],
                                                         scalar1=-1.0, scalar2=None,
                                                         op0=ALU.mult),
                          reads=[R(mcol2)], writes=[R(mcol2)])
                        for ci, (klo, khi) in enumerate(chunks2):
                            kw = khi - klo
                            E("S", lambda q, ci=ci, klo=klo, khi=khi, kw=kw: q.activation(
                                out=expb2[:, 0:kw], in_=scbuf[:, klo:khi], func=AF.Exp,
                                bias=mcol2[:, 0:1], scale=1.0,
                                accum_out=sums2[:, ci:ci + 1]),
                              reads=[R(scbuf), R(mcol2)], writes=[R(expb2), R(sums2)])
                            nsub = kw // 128
                            for i in range(nsub):
                                E("T", lambda q, i=i: q.transpose(
                                    psTx[:, i * 128:(i + 1) * 128],
                                    expb2[:, i * 128:(i + 1) * 128], ident[:]),
                                  reads=[R(expb2), R(ident)], writes=[R(psTx)])
                            E("V", lambda q, kw=kw: q.tensor_copy(out=expT2[:, 0:kw],
                                                                  in_=psTx[:, 0:kw]),
                              reads=[R(psTx)], writes=[R(expT2)])
                            for i in range(nsub):
                                kt = klo // 128 + i
                                E("T", lambda q, i=i, kt=kt, nkt_end=nkt_end: q.matmul(
                                    psO2[:, 0:64], expT2[:, i * 128:(i + 1) * 128],
                                    vtm2[:, kt * 64:(kt + 1) * 64],
                                    start=(kt == 0), stop=(kt == nkt_end - 1), skip_group_check=True),
                                  reads=[R(expT2), R(vtm2)], writes=[R(psO2)])
                        E("V", lambda q, n2=len(chunks2): q.tensor_reduce(
                            out=rsum2[:], in_=sums2[:, 0:n2], axis=AX.X, op=ALU.add),
                          reads=[R(sums2)], writes=[R(rsum2)])
                        E("V", lambda q, h=h: q.tensor_scalar(
                            out=rsum2[:], in0=rsum2[:], scalar1=chc[:, h:h + 1],
                            scalar2=None, op0=ALU.add),
                          reads=[R(rsum2), R(chc)], writes=[R(rsum2)])
                        E("V", lambda q: q.reciprocal(out=rsum2[:], in_=rsum2[:]),
                          reads=[R(rsum2)], writes=[R(rsum2)])
                        E("S", lambda q: q.activation(out=otile2[:], in_=psO2[:, 0:64],
                                                      func=AF.Identity, scale=rsum2[:, 0:1]),
                          reads=[R(psO2), R(rsum2)], writes=[R(otile2)])
                        E("qA", lambda q, jt=jt, h=h: q.dma_start(
                            out=oT2_scr.ap()[jt * 128:(jt + 1) * 128, h * 64:(h + 1) * 64],
                            in_=otile2[:]),
                          reads=[R(otile2)], writes=["l2_oT"])

            with ctx.scope():
                wo2 = ctx.sbuf("l2_wo", [128, 8 * 1024], BF16)
                E("qP", lambda q: q.dma_start(
                    out=wo2[:].rearrange("p (c n) -> p c n", c=8),
                    in_=din["l2_Wo"].ap().rearrange("(c p) n -> p c n", p=128)),
                  writes=[R(wo2)])
                oin = [ctx.sbuf("l2_oin", [128, 1024], BF16) for _ in range(2)]
                otT = ctx.sbuf("l2_otT", [128, 8 * 128], BF16)
                sb = ctx.sbuf("l2_sb", [128, 1024], F32)
                psT3b = ctx.psum("l2_psT3", [128, 1024], BF16)
                psS = [ctx.psum("l2_psS", [128, 512], F32) for _ in range(2)]
                for jt in range(NT):
                    b = jt % 2
                    E("qS", lambda q, jt=jt, b=b: q.dma_start(
                        out=oin[b][:], in_=oT2_scr.ap()[jt * 128:(jt + 1) * 128, :]),
                      reads=["l2_oT"], writes=[R(oin[b])])
                    for i in range(8):
                        E("T", lambda q, b=b, i=i: q.transpose(
                            psT3b[:, i * 128:(i + 1) * 128],
                            oin[b][:, i * 128:(i + 1) * 128], ident[:]),
                          reads=[R(oin[b]), R(ident)], writes=[R(psT3b)])
                    E("V", lambda q: q.tensor_copy(out=otT[:], in_=psT3b[:]),
                      reads=[R(psT3b)], writes=[R(otT)])
                    for nh in range(2):
                        lhs = [otT[:, kc * 128:(kc + 1) * 128] for kc in range(8)]
                        rhs = [wo2[:, kc * 1024 + nh * 512: kc * 1024 + (nh + 1) * 512]
                               for kc in range(8)]
                        mm_tile(psS[nh][:], R(psS[nh]), lhs, rhs, [R(otT), R(wo2)])
                        E("S", lambda q, nh=nh: q.activation(
                            out=sb[:, nh * 512:(nh + 1) * 512], in_=psS[nh][:], func=AF.Copy),
                          reads=[R(psS[nh])], writes=[R(sb)])
                    E("qA", lambda q, jt=jt: q.dma_start(
                        out=s_scr.ap()[jt * 128:(jt + 1) * 128, :], in_=sb[:]),
                      reads=[R(sb)], writes=["s_scr"])

            upmax = ctx.sbuf("l2_upmax", [128, 2], F32)
            E("V", lambda q: q.memset(upmax[:], 0.0), writes=[R(upmax)])
            scrA = dram("l2_scrA", [T, 3072])
            scrB = dram("l2_scrB", [T, 3072])
            scrC = dram("l2_scrC", [T, 3072])

            def arscalar(col, colnm, bname, nv=1):
                for j in range(nv):
                    pmax_col(col[:, j:j + 1], colnm)
                bi = dram(bname + "_i", [1, nv])
                bo = dram(bname + "_o", [1, nv])
                E("qP", lambda q: q.dma_start(out=bi.ap(), in_=col[0:1, 0:nv]),
                  reads=[colnm], writes=[bname + "_i"])
                E("P", lambda q: q.collective_compute(
                    "AllReduce", ALU.max, replica_groups=[list(range(NC))],
                    ins=[bi.ap()], outs=[bo.ap()]),
                  reads=[bname + "_i"], writes=[bname + "_o"])
                E("qS", lambda q: q.dma_start(out=pm_row[:, 0:nv], in_=bo.ap()),
                  reads=[bname + "_o"], writes=[R(pm_row)])
                pbcast(col[:, 0:nv], colnm, pm_row[0:1, 0:nv], [R(pm_row)], nv)

            def mk_scale(dst, dstnm, src_ap, srcnms):
                E("V", lambda q: q.tensor_scalar(out=dst[:, 0:1], in0=src_ap,
                                                 scalar1=1.0 / 32767.0, scalar2=1e-12,
                                                 op0=ALU.mult, op1=ALU.max),
                  reads=list(srcnms), writes=[dstnm])
                E("V", lambda q: q.reciprocal(out=dst[:, 1:2], in_=dst[:, 0:1]),
                  reads=[dstnm], writes=[dstnm])

            with ctx.scope():
                arscalar(xnmax2, R(xnmax2), "l2_xmax")
                sxc = ctx.sbuf("l2_sxc", [128, 2], F32)
                mk_scale(sxc, R(sxc), xnmax2[:, 0:1], [R(xnmax2)])
                tmpr = ctx.sbuf("l2_tmpr", [128, 3072], F32)
                xqT2 = ctx.sbuf("l2_xqT", [128, 8 * T], BF16)
                xnf4 = ctx.sbuf("l2_xnf4", [128, 1024], F32)
                abf5 = ctx.sbuf("l2_abf5", [128, 3072], BF16)
                psQ5 = ctx.psum("l2_psQ5", [128, 1024], BF16)

                def qdq2(buf, bufnm, rs_ap, s_ap, scnms, qmin, qmax, w):
                    E("V", lambda q: q.tensor_scalar(out=buf[:, 0:w], in0=buf[:, 0:w],
                                                     scalar1=rs_ap, scalar2=None,
                                                     op0=ALU.mult),
                      reads=[bufnm] + list(scnms), writes=[bufnm])
                    round_ste(buf, bufnm, buf[:, 0:w], [bufnm], tmpr, w)
                    E("V", lambda q: q.tensor_scalar(out=buf[:, 0:w], in0=buf[:, 0:w],
                                                     scalar1=float(qmin), scalar2=float(qmax),
                                                     op0=ALU.max, op1=ALU.min),
                      reads=[bufnm], writes=[bufnm])
                    E("V", lambda q: q.tensor_scalar(out=buf[:, 0:w], in0=buf[:, 0:w],
                                                     scalar1=s_ap, scalar2=None, op0=ALU.mult),
                      reads=[bufnm] + list(scnms), writes=[bufnm])

                for jt in range(NT):
                    E("qS", lambda q, jt=jt: q.dma_start(
                        out=xnf4[:], in_=xn_f.ap()[jt * 128:(jt + 1) * 128, :]),
                      reads=["xn_f"], writes=[R(xnf4)])
                    qdq2(xnf4, R(xnf4), sxc[:, 1:2], sxc[:, 0:1], [R(sxc)], -32768, 32767,
                         1024)
                    E("V", lambda q: q.tensor_copy(out=abf5[:, 0:1024], in_=xnf4[:]),
                      reads=[R(xnf4)], writes=[R(abf5)])
                    for kc in range(8):
                        E("T", lambda q, kc=kc: q.transpose(
                            psQ5[:, kc * 128:(kc + 1) * 128],
                            abf5[:, kc * 128:(kc + 1) * 128], ident[:]),
                          reads=[R(abf5), R(ident)], writes=[R(psQ5)])
                    dstx = xqT2[:].rearrange("p (c t) -> p c t", c=8)[:, :,
                                                                     jt * 128:(jt + 1) * 128]
                    E("V", lambda q, dstx=dstx: q.tensor_copy(
                        out=dstx, in_=psQ5[:].rearrange("p (c t) -> p c t", c=8)),
                      reads=[R(psQ5)], writes=[R(xqT2)])

                wgT2 = ctx.sbuf("l2_wgT", [128, 8 * 3072], BF16)
                wuT2 = ctx.sbuf("l2_wuT", [128, 8 * 3072], BF16)
                wbuf5 = [ctx.sbuf("l2_wbuf5", [128, 1024], F32) for _ in range(2)]
                am5 = ctx.sbuf("l2_am5", [128, 96], F32)
                rs5 = ctx.sbuf("l2_rs5", [128, 96], F32)

                def lpbq(buf, bufnm, w):
                    nb = w // 32
                    E("V", lambda q, nb=nb, w=w: q.tensor_reduce(
                        out=am5[:, 0:nb], in_=buf[:, 0:w].rearrange("p (b k) -> p b k", k=32),
                        axis=AX.X, op=ALU.max, apply_absolute_value=True),
                      reads=[bufnm], writes=[R(am5)])
                    E("V", lambda q, nb=nb: q.tensor_scalar(out=am5[:, 0:nb],
                                                            in0=am5[:, 0:nb],
                                                            scalar1=1.0 / 7.0, scalar2=1e-12,
                                                            op0=ALU.mult, op1=ALU.max),
                      reads=[R(am5)], writes=[R(am5)])
                    E("V", lambda q, nb=nb: q.reciprocal(out=rs5[:, 0:nb], in_=am5[:, 0:nb]),
                      reads=[R(am5)], writes=[R(rs5)])
                    E("V", lambda q, nb=nb, w=w: q.tensor_tensor(
                        out=buf[:, 0:w].rearrange("p (b k) -> p b k", k=32),
                        in0=buf[:, 0:w].rearrange("p (b k) -> p b k", k=32),
                        in1=rs5[:, 0:nb].rearrange("p (b o) -> p b o", o=1).broadcast_to(
                            [128, nb, 32]), op=ALU.mult),
                      reads=[bufnm, R(rs5)], writes=[bufnm])
                    round_ste(buf, bufnm, buf[:, 0:w], [bufnm], tmpr, w)
                    E("V", lambda q, w=w: q.tensor_scalar(out=buf[:, 0:w], in0=buf[:, 0:w],
                                                          scalar1=-8.0, scalar2=7.0,
                                                          op0=ALU.max, op1=ALU.min),
                      reads=[bufnm], writes=[bufnm])
                    E("V", lambda q, nb=nb, w=w: q.tensor_tensor(
                        out=buf[:, 0:w].rearrange("p (b k) -> p b k", k=32),
                        in0=buf[:, 0:w].rearrange("p (b k) -> p b k", k=32),
                        in1=am5[:, 0:nb].rearrange("p (b o) -> p b o", o=1).broadcast_to(
                            [128, nb, 32]), op=ALU.mult),
                      reads=[bufnm, R(am5)], writes=[bufnm])

                for nm, WT in (("l2_gw", wgT2), ("l2_uw", wuT2)):
                    for it in range(24):
                        b = it % 2
                        E("qS", lambda q, nm=nm, it=it, b=b: q.dma_start(
                            out=wbuf5[b][:], in_=din[nm].ap()[it * 128:(it + 1) * 128, :]),
                          writes=[R(wbuf5[b])])
                        lpbq(wbuf5[b], R(wbuf5[b]), 1024)
                        E("V", lambda q, b=b: q.tensor_copy(out=abf5[:, 0:1024],
                                                            in_=wbuf5[b][:]),
                          reads=[R(wbuf5[b])], writes=[R(abf5)])
                        for kc in range(8):
                            E("T", lambda q, kc=kc: q.transpose(
                                psQ5[:, kc * 128:(kc + 1) * 128],
                                abf5[:, kc * 128:(kc + 1) * 128], ident[:]),
                              reads=[R(abf5), R(ident)], writes=[R(psQ5)])
                        dstw = WT[:].rearrange("p (c n) -> p c n", c=8)[:, :,
                                                                       it * 128:(it + 1) * 128]
                        E("V", lambda q, dstw=dstw: q.tensor_copy(
                            out=dstw, in_=psQ5[:].rearrange("p (c n) -> p c n", c=8)),
                          reads=[R(psQ5)], writes=[R(WT)])

                ubig = ctx.sbuf("l2_ubig", [128, 3072], F32)
                mcol5 = ctx.sbuf("l2_mcol5", [128, 1], F32)
                psM5 = [ctx.psum("l2_psM5", [128, 512], F32) for _ in range(2)]
                nch30 = [(s, min(s + 512, 3072)) for s in range(0, 3072, 512)]
                for wi, (WT, scr, scr_nm) in enumerate(((wuT2, scrA, "l2_scrA"),
                                                        (wgT2, scrB, "l2_scrB"))):
                    for jt in range(NT):
                        for nc_, (lo, hi) in enumerate(nch30):
                            pb = nc_ % 2
                            w_ = hi - lo
                            lhs = [xqT2[:, kc * T + jt * 128: kc * T + (jt + 1) * 128]
                                   for kc in range(8)]
                            rhs = [WT[:, kc * 3072 + lo: kc * 3072 + hi] for kc in range(8)]
                            mm_tile(psM5[pb][:, 0:w_], R(psM5[pb]), lhs, rhs,
                                    [R(xqT2), R(WT)])
                            E("S", lambda q, pb=pb, lo=lo, hi=hi, w_=w_: q.activation(
                                out=ubig[:, lo:hi], in_=psM5[pb][:, 0:w_], func=AF.Copy),
                              reads=[R(psM5[pb])], writes=[R(ubig)])
                        E("V", lambda q: q.tensor_reduce(out=mcol5[:], in_=ubig[:], axis=AX.X,
                                                         op=ALU.max,
                                                         apply_absolute_value=True),
                          reads=[R(ubig)], writes=[R(mcol5)])
                        E("V", lambda q, wi=wi: q.tensor_tensor(
                            out=upmax[:, wi:wi + 1], in0=upmax[:, wi:wi + 1], in1=mcol5[:],
                            op=ALU.max),
                          reads=[R(mcol5), R(upmax)], writes=[R(upmax)])
                        E("qA", lambda q, jt=jt, scr=scr, scr_nm=scr_nm: q.dma_start(
                            out=scr.ap()[jt * 128:(jt + 1) * 128, :], in_=ubig[:]),
                          reads=[R(ubig)], writes=[scr_nm])

            g2max = ctx.sbuf("l2_g2max", [128, 1], F32)
            E("V", lambda q: q.memset(g2max[:], 0.0), writes=[R(g2max)])
            with ctx.scope():
                arscalar(upmax, R(upmax), "l2_ugmax", nv=2)
                suc = ctx.sbuf("l2_suc", [128, 2], F32)
                sgc = ctx.sbuf("l2_sgc", [128, 2], F32)
                mk_scale(suc, R(suc), upmax[:, 0:1], [R(upmax)])
                mk_scale(sgc, R(sgc), upmax[:, 1:2], [R(upmax)])
                gbig = ctx.sbuf("l2_gbig", [128, 3072], F32)
                sbig = ctx.sbuf("l2_sbig", [128, 3072], F32)
                tmpr2 = ctx.sbuf("l2_tmpr2", [128, 3072], F32)
                mcol6 = ctx.sbuf("l2_mcol6", [128, 1], F32)

                def qdq3(buf, bufnm, rs_ap, s_ap, scnms, qmin, qmax, w):
                    E("V", lambda q: q.tensor_scalar(out=buf[:, 0:w], in0=buf[:, 0:w],
                                                     scalar1=rs_ap, scalar2=None,
                                                     op0=ALU.mult),
                      reads=[bufnm] + list(scnms), writes=[bufnm])
                    round_ste(buf, bufnm, buf[:, 0:w], [bufnm], tmpr2, w)
                    E("V", lambda q: q.tensor_scalar(out=buf[:, 0:w], in0=buf[:, 0:w],
                                                     scalar1=float(qmin), scalar2=float(qmax),
                                                     op0=ALU.max, op1=ALU.min),
                      reads=[bufnm], writes=[bufnm])
                    E("V", lambda q: q.tensor_scalar(out=buf[:, 0:w], in0=buf[:, 0:w],
                                                     scalar1=s_ap, scalar2=None, op0=ALU.mult),
                      reads=[bufnm] + list(scnms), writes=[bufnm])

                for jt in range(NT):
                    E("qS", lambda q, jt=jt: q.dma_start(
                        out=gbig[:], in_=scrB.ap()[jt * 128:(jt + 1) * 128, :]),
                      reads=["l2_scrB"], writes=[R(gbig)])
                    qdq3(gbig, R(gbig), sgc[:, 1:2], sgc[:, 0:1], [R(sgc)], -32768, 32767,
                         3072)
                    E("S", lambda q: q.activation(out=sbig[:], in_=gbig[:], func=AF.Sigmoid),
                      reads=[R(gbig)], writes=[R(sbig)])
                    E("V", lambda q: q.tensor_scalar(out=sbig[:], in0=sbig[:],
                                                     scalar1=65536.0, scalar2=None,
                                                     op0=ALU.mult),
                      reads=[R(sbig)], writes=[R(sbig)])
                    round_ste(sbig, R(sbig), sbig[:], [R(sbig)], tmpr2, 3072)
                    E("V", lambda q: q.tensor_scalar(out=sbig[:], in0=sbig[:], scalar1=0.0,
                                                     scalar2=65535.0, op0=ALU.max,
                                                     op1=ALU.min),
                      reads=[R(sbig)], writes=[R(sbig)])
                    E("V", lambda q: q.tensor_scalar(out=sbig[:], in0=sbig[:],
                                                     scalar1=1.0 / 65536.0, scalar2=None,
                                                     op0=ALU.mult),
                      reads=[R(sbig)], writes=[R(sbig)])
                    E("V", lambda q: q.tensor_mul(out=gbig[:], in0=gbig[:], in1=sbig[:]),
                      reads=[R(gbig), R(sbig)], writes=[R(gbig)])
                    E("V", lambda q: q.tensor_reduce(out=mcol6[:], in_=gbig[:], axis=AX.X,
                                                     op=ALU.max, apply_absolute_value=True),
                      reads=[R(gbig)], writes=[R(mcol6)])
                    E("V", lambda q: q.tensor_tensor(out=g2max[:], in0=g2max[:], in1=mcol6[:],
                                                     op=ALU.max),
                      reads=[R(mcol6), R(g2max)], writes=[R(g2max)])
                    E("qA", lambda q, jt=jt: q.dma_start(
                        out=scrC.ap()[jt * 128:(jt + 1) * 128, :], in_=gbig[:]),
                      reads=[R(gbig)], writes=["l2_scrC"])

            omax = ctx.sbuf("l2_omax", [128, 1], F32)
            E("V", lambda q: q.memset(omax[:], 0.0), writes=[R(omax)])
            with ctx.scope():
                arscalar(g2max, R(g2max), "l2_g2m")
                s2c = ctx.sbuf("l2_s2c", [128, 2], F32)
                suc2 = ctx.sbuf("l2_suc2", [128, 2], F32)
                mk_scale(s2c, R(s2c), g2max[:, 0:1], [R(g2max)])
                mk_scale(suc2, R(suc2), upmax[:, 0:1], [R(upmax)])
                gbig = ctx.sbuf("l2_gbigC", [128, 3072], F32)
                ubig2 = ctx.sbuf("l2_ubigC", [128, 3072], F32)
                tmpr3 = ctx.sbuf("l2_tmpr3", [128, 3072], F32)
                mcol7 = ctx.sbuf("l2_mcol7", [128, 1], F32)

                def qdq4(buf, bufnm, rs_ap, s_ap, scnms, w):
                    E("V", lambda q: q.tensor_scalar(out=buf[:, 0:w], in0=buf[:, 0:w],
                                                     scalar1=rs_ap, scalar2=None,
                                                     op0=ALU.mult),
                      reads=[bufnm] + list(scnms), writes=[bufnm])
                    round_ste(buf, bufnm, buf[:, 0:w], [bufnm], tmpr3, w)
                    E("V", lambda q: q.tensor_scalar(out=buf[:, 0:w], in0=buf[:, 0:w],
                                                     scalar1=-32768.0, scalar2=32767.0,
                                                     op0=ALU.max, op1=ALU.min),
                      reads=[bufnm], writes=[bufnm])
                    E("V", lambda q: q.tensor_scalar(out=buf[:, 0:w], in0=buf[:, 0:w],
                                                     scalar1=s_ap, scalar2=None, op0=ALU.mult),
                      reads=[bufnm] + list(scnms), writes=[bufnm])

                for jt in range(NT):
                    E("qS", lambda q, jt=jt: q.dma_start(
                        out=gbig[:], in_=scrC.ap()[jt * 128:(jt + 1) * 128, :]),
                      reads=["l2_scrC"], writes=[R(gbig)])
                    E("qS", lambda q, jt=jt: q.dma_start(
                        out=ubig2[:], in_=scrA.ap()[jt * 128:(jt + 1) * 128, :]),
                      reads=["l2_scrA"], writes=[R(ubig2)])
                    qdq4(gbig, R(gbig), s2c[:, 1:2], s2c[:, 0:1], [R(s2c)], 3072)
                    qdq4(ubig2, R(ubig2), suc2[:, 1:2], suc2[:, 0:1], [R(suc2)], 3072)
                    E("V", lambda q: q.tensor_mul(out=gbig[:], in0=gbig[:], in1=ubig2[:]),
                      reads=[R(gbig), R(ubig2)], writes=[R(gbig)])
                    E("V", lambda q: q.tensor_reduce(out=mcol7[:], in_=gbig[:], axis=AX.X,
                                                     op=ALU.max, apply_absolute_value=True),
                      reads=[R(gbig)], writes=[R(mcol7)])
                    E("V", lambda q: q.tensor_tensor(out=omax[:], in0=omax[:], in1=mcol7[:],
                                                     op=ALU.max),
                      reads=[R(mcol7), R(omax)], writes=[R(omax)])
                    E("qA", lambda q, jt=jt: q.dma_start(
                        out=scrA.ap()[jt * 128:(jt + 1) * 128, :], in_=gbig[:]),
                      reads=[R(gbig)], writes=["l2_scrA2"])

            with ctx.scope():
                arscalar(omax, R(omax), "l2_om")
                soc = ctx.sbuf("l2_soc", [128, 2], F32)
                mk_scale(soc, R(soc), omax[:, 0:1], [R(omax)])
                dwT2 = ctx.sbuf("l2_dwT", [128, 24 * 1024], BF16)
                wbuf6 = [ctx.sbuf("l2_wbuf6", [128, 3072], F32) for _ in range(2)]
                am6 = ctx.sbuf("l2_am6", [128, 96], F32)
                rs6 = ctx.sbuf("l2_rs6", [128, 96], F32)
                tmpr4 = ctx.sbuf("l2_tmpr4", [128, 3072], F32)
                abf6 = ctx.sbuf("l2_abf6", [128, 3072], BF16)
                psQ6 = ctx.psum("l2_psQ6", [128, 1024], BF16)

                def lpbq2(buf, bufnm, w):
                    nb = w // 32
                    E("V", lambda q, nb=nb, w=w: q.tensor_reduce(
                        out=am6[:, 0:nb], in_=buf[:, 0:w].rearrange("p (b k) -> p b k", k=32),
                        axis=AX.X, op=ALU.max, apply_absolute_value=True),
                      reads=[bufnm], writes=[R(am6)])
                    E("V", lambda q, nb=nb: q.tensor_scalar(out=am6[:, 0:nb],
                                                            in0=am6[:, 0:nb],
                                                            scalar1=1.0 / 7.0, scalar2=1e-12,
                                                            op0=ALU.mult, op1=ALU.max),
                      reads=[R(am6)], writes=[R(am6)])
                    E("V", lambda q, nb=nb: q.reciprocal(out=rs6[:, 0:nb], in_=am6[:, 0:nb]),
                      reads=[R(am6)], writes=[R(rs6)])
                    E("V", lambda q, nb=nb, w=w: q.tensor_tensor(
                        out=buf[:, 0:w].rearrange("p (b k) -> p b k", k=32),
                        in0=buf[:, 0:w].rearrange("p (b k) -> p b k", k=32),
                        in1=rs6[:, 0:nb].rearrange("p (b o) -> p b o", o=1).broadcast_to(
                            [128, nb, 32]), op=ALU.mult),
                      reads=[bufnm, R(rs6)], writes=[bufnm])
                    round_ste(buf, bufnm, buf[:, 0:w], [bufnm], tmpr4, w)
                    E("V", lambda q, w=w: q.tensor_scalar(out=buf[:, 0:w], in0=buf[:, 0:w],
                                                          scalar1=-8.0, scalar2=7.0,
                                                          op0=ALU.max, op1=ALU.min),
                      reads=[bufnm], writes=[bufnm])
                    E("V", lambda q, nb=nb, w=w: q.tensor_tensor(
                        out=buf[:, 0:w].rearrange("p (b k) -> p b k", k=32),
                        in0=buf[:, 0:w].rearrange("p (b k) -> p b k", k=32),
                        in1=am6[:, 0:nb].rearrange("p (b o) -> p b o", o=1).broadcast_to(
                            [128, nb, 32]), op=ALU.mult),
                      reads=[bufnm, R(am6)], writes=[bufnm])

                for it in range(8):
                    b = it % 2
                    E("qS", lambda q, it=it, b=b: q.dma_start(
                        out=wbuf6[b][:], in_=din["l2_dw"].ap()[it * 128:(it + 1) * 128, :]),
                      writes=[R(wbuf6[b])])
                    lpbq2(wbuf6[b], R(wbuf6[b]), 3072)
                    E("V", lambda q, b=b: q.tensor_copy(out=abf6[:], in_=wbuf6[b][:]),
                      reads=[R(wbuf6[b])], writes=[R(abf6)])
                    for kc in range(24):
                        E("T", lambda q, kc=kc: q.transpose(
                            psQ6[:, (kc % 8) * 128:(kc % 8 + 1) * 128],
                            abf6[:, kc * 128:(kc + 1) * 128], ident[:]),
                          reads=[R(abf6), R(ident)], writes=[R(psQ6)])
                        E("V", lambda q, kc=kc, it=it: q.tensor_copy(
                            out=dwT2[:, kc * 1024 + it * 128: kc * 1024 + (it + 1) * 128],
                            in_=psQ6[:, (kc % 8) * 128:(kc % 8 + 1) * 128]),
                          reads=[R(psQ6)], writes=[R(dwT2)])
                obig = ctx.sbuf("l2_obig", [128, 3072], F32)
                oT6 = ctx.sbuf("l2_oT6", [128, 24 * 128], BF16)
                fout6 = ctx.sbuf("l2_fout6", [128, 1024], F32)
                stile6 = ctx.sbuf("l2_stile6", [128, 1024], F32)
                xtile6 = ctx.sbuf("l2_xtile6", [128, 1024], F32)
                psM6 = [ctx.psum("l2_psM6", [128, 512], F32) for _ in range(2)]

                def qdq5(buf, bufnm, rs_ap, s_ap, scnms, w):
                    E("V", lambda q: q.tensor_scalar(out=buf[:, 0:w], in0=buf[:, 0:w],
                                                     scalar1=rs_ap, scalar2=None,
                                                     op0=ALU.mult),
                      reads=[bufnm] + list(scnms), writes=[bufnm])
                    round_ste(buf, bufnm, buf[:, 0:w], [bufnm], tmpr4, w)
                    E("V", lambda q: q.tensor_scalar(out=buf[:, 0:w], in0=buf[:, 0:w],
                                                     scalar1=-32768.0, scalar2=32767.0,
                                                     op0=ALU.max, op1=ALU.min),
                      reads=[bufnm], writes=[bufnm])
                    E("V", lambda q: q.tensor_scalar(out=buf[:, 0:w], in0=buf[:, 0:w],
                                                     scalar1=s_ap, scalar2=None, op0=ALU.mult),
                      reads=[bufnm] + list(scnms), writes=[bufnm])

                for jt in range(NT):
                    E("qS", lambda q, jt=jt: q.dma_start(
                        out=obig[:], in_=scrA.ap()[jt * 128:(jt + 1) * 128, :]),
                      reads=["l2_scrA2"], writes=[R(obig)])
                    qdq5(obig, R(obig), soc[:, 1:2], soc[:, 0:1], [R(soc)], 3072)
                    E("V", lambda q: q.tensor_copy(out=abf6[:], in_=obig[:]),
                      reads=[R(obig)], writes=[R(abf6)])
                    for kc in range(24):
                        E("T", lambda q, kc=kc: q.transpose(
                            psQ6[:, (kc % 8) * 128:(kc % 8 + 1) * 128],
                            abf6[:, kc * 128:(kc + 1) * 128], ident[:]),
                          reads=[R(abf6), R(ident)], writes=[R(psQ6)])
                        E("V", lambda q, kc=kc: q.tensor_copy(
                            out=oT6[:, kc * 128:(kc + 1) * 128],
                            in_=psQ6[:, (kc % 8) * 128:(kc % 8 + 1) * 128]),
                          reads=[R(psQ6)], writes=[R(oT6)])
                    for nh in range(2):
                        lhs = [oT6[:, kc * 128:(kc + 1) * 128] for kc in range(24)]
                        rhs = [dwT2[:, kc * 1024 + nh * 512: kc * 1024 + (nh + 1) * 512]
                               for kc in range(24)]
                        mm_tile(psM6[nh][:], R(psM6[nh]), lhs, rhs, [R(oT6), R(dwT2)])
                        E("S", lambda q, nh=nh: q.activation(
                            out=fout6[:, nh * 512:(nh + 1) * 512], in_=psM6[nh][:],
                            func=AF.Silu),
                          reads=[R(psM6[nh])], writes=[R(fout6)])
                    residual_epilogue(jt, fout6, R(fout6), dst_dram, dst_nm, stile6, xtile6,
                                      xsrc, xsrc_nm)

        last = max(layers) if layers else -1
        if 0 in layers:
            with ctx.scope():
                layer0(x_in, "x_in", out if last == 0 else xres,
                       "out_hbm" if last == 0 else "xres")
        if 1 in layers:
            src0 = xres if 0 in layers else x_in
            src0nm = "xres" if 0 in layers else "x_in"
            layer1(src0, src0nm, out if last == 1 else xres,
                   "out_hbm" if last == 1 else "xres")
        if 2 in layers:
            src0 = xres if len([l for l in layers if l < 2]) else x_in
            src0nm = "xres" if len([l for l in layers if l < 2]) else "x_in"
            layer2(src0, src0nm, out if last == 2 else xres,
                   "out_hbm" if last == 2 else "xres")
        if 3 in layers:
            src0 = xres if len([l for l in layers if l < 3]) else x_in
            src0nm = "xres" if len([l for l in layers if l < 3]) else "x_in"
            layer3(src0, src0nm, out if last == 3 else xres,
                   "out_hbm" if last == 3 else "xres")
        if not layers:
            fin = ctx.sbuf("fin", [128, D], F32)
            for jt in range(NT):
                E("qS", lambda q, jt=jt: q.dma_start(
                    out=fin[:], in_=x_in.ap()[jt * 128:(jt + 1) * 128, :]),
                  writes=[R(fin)])
                E("qS", lambda q, jt=jt: q.dma_start(
                    out=out.ap()[jt * 128:(jt + 1) * 128, :], in_=fin[:]),
                  reads=[R(fin)], writes=["out_hbm"])
        ctx.barrier()
        ctx.stack.close()
    return nc


def _consts_arr():
    a = np.zeros((128, len(CONSTS)), np.float32)
    for v, i in CIDX.items():
        a[:, i] = v
    return a


def host_inputs(inputs):
    g = {}
    g["ident"] = np.eye(128, dtype=np.float32)
    g["consts"] = _consts_arr()
    g["ones_row"] = np.ones((1, 128), np.float32)
    tr = lambda a: np.ascontiguousarray(np.asarray(a).T)
    g["l0_WkT"] = tr(inputs["L0_attn_Wk"])
    g["l0_WvT"] = tr(inputs["L0_attn_Wv"])
    g["l0_WrT"] = tr(inputs["L0_attn_Wr"])
    g["l0_WoT"] = tr(inputs["L0_attn_Wo"])
    g["l0_td"] = np.ascontiguousarray(np.asarray(inputs["L0_attn_time_decay"]).reshape(8, 128).T)
    g["l0_tf"] = np.ascontiguousarray(np.asarray(inputs["L0_attn_time_first"]).reshape(8, 128).T)
    g["l0_gwT"] = tr(inputs["L0_ffn_gate_w"])
    g["l0_We1T"] = np.ascontiguousarray(np.transpose(np.asarray(inputs["L0_ffn_We1"]), (0, 2, 1)))
    g["l0_We3T"] = np.ascontiguousarray(np.transpose(np.asarray(inputs["L0_ffn_We3"]), (0, 2, 1)))
    g["l0_We2T"] = np.ascontiguousarray(np.transpose(np.asarray(inputs["L0_ffn_We2"]), (0, 2, 1)))
    g["l0_be1"] = np.ascontiguousarray(np.asarray(inputs["L0_ffn_be1"]))
    g["l0_be3"] = np.ascontiguousarray(np.asarray(inputs["L0_ffn_be3"]))
    g["l0_be2"] = np.ascontiguousarray(np.asarray(inputs["L0_ffn_be2"]))
    g["l0_Ws1T"] = tr(inputs["L0_ffn_Ws1"])
    g["l0_Ws3T"] = tr(inputs["L0_ffn_Ws3"])
    g["l0_Ws2T"] = tr(inputs["L0_ffn_Ws2"])
    g["l0_bs1"] = np.ascontiguousarray(np.asarray(inputs["L0_ffn_bs1"]))[None, :]
    g["l0_bs3"] = np.ascontiguousarray(np.asarray(inputs["L0_ffn_bs3"]))[None, :]
    g["l0_bs2"] = np.ascontiguousarray(np.asarray(inputs["L0_ffn_bs2"]))[None, :]
    g["l1_WqT"] = tr(inputs["L1_attn_Wq"])
    g["l1_WkvdT"] = tr(inputs["L1_attn_Wkv_down"])
    g["l1_WkuT"] = tr(inputs["L1_attn_Wk_up"])
    g["l1_WvuT"] = tr(inputs["L1_attn_Wv_up"])
    g["l1_WoT"] = tr(inputs["L1_attn_Wo"])
    g["l1_gw"] = np.ascontiguousarray(np.asarray(inputs["L1_ffn_gate_w"]))
    g["l1_uw"] = np.ascontiguousarray(np.asarray(inputs["L1_ffn_up_w"]))
    g["l1_dw"] = np.ascontiguousarray(np.asarray(inputs["L1_ffn_down_w"]))
    g["l1_s_in"] = np.ascontiguousarray(np.asarray(inputs["L1_ffn_s_in"]).reshape(1, 1))
    g["l1_s_in_down"] = np.ascontiguousarray(np.asarray(inputs["L1_ffn_s_in_down"]).reshape(1, 1))
    g["l2_Wq"] = np.ascontiguousarray(np.asarray(inputs["L2_attn_Wq"]))
    g["l2_Wqr"] = np.ascontiguousarray(np.asarray(inputs["L2_attn_Wqr"]))
    g["l2_Wkr"] = np.ascontiguousarray(np.asarray(inputs["L2_attn_Wkr"]))
    g["l2_Wkvd"] = np.ascontiguousarray(np.asarray(inputs["L2_attn_Wkvd"]))
    g["l2_Wku"] = np.ascontiguousarray(np.asarray(inputs["L2_attn_Wku"]))
    g["l2_Wvu"] = np.ascontiguousarray(np.asarray(inputs["L2_attn_Wvu"]))
    g["l2_Wo"] = np.ascontiguousarray(np.asarray(inputs["L2_attn_Wo"]))
    g["l2_lobo"] = np.ascontiguousarray(np.asarray(inputs["L2_attn_lobo_log"]).reshape(1, 16))
    g["l2_gw"] = np.ascontiguousarray(np.asarray(inputs["L2_ffn_w_gate"]))
    g["l2_uw"] = np.ascontiguousarray(np.asarray(inputs["L2_ffn_w_up"]))
    g["l2_dw"] = np.ascontiguousarray(np.asarray(inputs["L2_ffn_w_down"]))
    T_ = np.asarray(inputs["L0_attn_x"]).shape[1] if "L0_attn_x" in inputs else 4096
    half = 16
    inv = 1.0 / (10000.0 ** (np.arange(half, dtype=np.float64) / half))
    ang = np.arange(T_, dtype=np.float64)[:, None] * inv[None, :]
    g["ropeC"] = np.cos(ang).astype(np.float32)
    g["ropeS"] = np.sin(ang).astype(np.float32)
    g["causmaskB"] = ((1.0 - np.tril(np.ones((128, 128)))) * 1e30).astype(np.float32)
    g["l3_WqT"] = tr(inputs["L3_attn_Wq"])
    g["l3_WkT"] = tr(inputs["L3_attn_Wk"])
    g["l3_WvT"] = tr(inputs["L3_attn_Wv"])
    g["l3_bq"] = np.ascontiguousarray(np.asarray(inputs["L3_attn_bq"]).reshape(64, 1))
    g["l3_bk"] = np.ascontiguousarray(np.asarray(inputs["L3_attn_bk"]).reshape(64, 1))
    g["l3_bv"] = np.ascontiguousarray(np.asarray(inputs["L3_attn_bv"]).reshape(1, 64))
    g["l3_clip_x"] = np.ascontiguousarray(np.asarray(inputs["L3_ffn_clip_x"]).reshape(1, 1))
    g["l3_clip_down"] = np.ascontiguousarray(np.asarray(inputs["L3_ffn_clip_down"]).reshape(1, 1))
    g["l3_gw"] = np.ascontiguousarray(np.asarray(inputs["L3_ffn_w_gate"]))
    g["l3_uw"] = np.ascontiguousarray(np.asarray(inputs["L3_ffn_w_up"]))
    g["l3_dw"] = np.ascontiguousarray(np.asarray(inputs["L3_ffn_w_down"]))
    return g


def kernel(**inputs):
    x = np.asarray(inputs["L0_attn_x"])
    B, T, _ = x.shape
    g = host_inputs(inputs)
    nc = build_nc(T=T, layers=(0, 1, 2, 3))
    in_maps = []
    for i in range(B):
        m = dict(g)
        m["x"] = np.ascontiguousarray(x[i])
        in_maps.append(m)
    res = run_bass_kernel_spmd(nc, in_maps, core_ids=list(range(B)))
    return np.stack([np.asarray(r["out"]) for r in res.results], axis=0)
